# version 16
# baseline (speedup 1.0000x reference)
import numpy as np
import ml_dtypes
from contextlib import ExitStack
import concourse.bass as bass
import concourse.mybir as mybir
from concourse.bass_utils import run_bass_kernel_spmd

F32 = mybir.dt.float32
BF16 = mybir.dt.bfloat16
I8 = mybir.dt.int8
I32 = mybir.dt.int32
AF = mybir.ActivationFunctionType
ALU = mybir.AluOpType
BF = ml_dtypes.bfloat16

D = 2048
NCH = 16
ATT_C = 24.0


class Sem:
    __slots__ = ("h", "step", "cnt")

    def __init__(self, h, step):
        self.h, self.step, self.cnt = h, step, 0

    def inc(self, instr, n=1):
        instr.then_inc(self.h, self.step * n)
        self.cnt += self.step * n
        return self.cnt


class K:
    def __init__(self):
        self.nc = bass.Bass("TRN2", target_bir_lowering=False)
        self.st = ExitStack()
        self.block = None

    def sem(self, name, step=1):
        return Sem(self.st.enter_context(self.nc.semaphore(name)), step)

    def sb(self, name, p, f, dtype):
        return self.st.enter_context(self.nc.sbuf_tensor(name, [p, f], dtype))

    def ps(self, name, p, f, dtype=F32):
        return self.st.enter_context(self.nc.psum_tensor(name, [p, f], dtype))

    def begin(self):
        self.block = self.st.enter_context(self.nc.Block())

    def t(self, fn): self.block.tensor(fn)
    def v(self, fn): self.block.vector(fn)
    def s(self, fn): self.block.scalar(fn)
    def g(self, fn): self.block.gpsimd(fn)
    def y(self, fn): self.block.sync(fn)


def acc_mm(t, out, pairs, sem=None):
    n = len(pairs)
    ins = None
    for i, (l, r) in enumerate(pairs):
        ins = t.matmul(out, l, r, start=(i == 0), stop=(i == n - 1))
    if sem is not None:
        sem.inc(ins)
    return ins


def emit_ln(k, T, eps_ap, Xc, XNc, stw, ps_a, ps_b, SQ, ROW, ONES, BONES, ONESROW,
            s_pe, s_act, s_dve, nch=NCH):
    act0, dve0 = s_act.cnt, s_dve.cnt

    def t_mean(t):
        t.wait_ge(s_act.h, act0)
        for c in range(nch):
            ins = t.matmul(ps_a[0:1, 0:stw], ONES[:, 0:1], Xc[c],
                           start=(c == 0), stop=(c == nch - 1))
        s_pe.inc(ins)
    k.t(t_mean)
    pe_mean = s_pe.cnt

    def s_mean(s):
        s.wait_ge(s_pe.h, pe_mean)
        s_act.inc(s.mul(ROW[0:1, 0:stw], ps_a[0:1, 0:stw], -1.0 / T))
    k.s(s_mean)
    act_mean = s_act.cnt

    def t_bcm(t):
        t.wait_ge(s_act.h, act_mean)
        t.wait_ge(s_dve.h, dve0)
        s_pe.inc(t.matmul(ps_b[:, 0:stw], ONESROW[0:1, :], ROW[0:1, 0:stw], start=True, stop=True))
    k.t(t_bcm)
    pe_bcm = s_pe.cnt

    def v_center(v):
        v.wait_ge(s_pe.h, pe_bcm)
        for c in range(nch):
            ins = v.tensor_tensor(XNc[c], Xc[c], ps_b[:, 0:stw], ALU.add)
        s_dve.inc(ins)
    k.v(v_center)
    dve_center = s_dve.cnt

    act_sq0 = s_act.cnt
    pe_sq0 = s_pe.cnt

    def s_square(s):
        s.wait_ge(s_dve.h, dve_center)
        for c in range(nch):
            if c >= 2:
                s.wait_ge(s_pe.h, pe_sq0 + c - 1)
            s_act.inc(s.activation(SQ[:, (c % 2) * stw:(c % 2 + 1) * stw],
                                   XNc[c], AF.Square))
    k.s(s_square)

    def t_sumsq(t):
        for c in range(nch):
            t.wait_ge(s_act.h, act_sq0 + c + 1)
            s_pe.inc(t.matmul(ps_a[0:1, stw:2 * stw], BONES[:, 0:1],
                              SQ[:, (c % 2) * stw:(c % 2 + 1) * stw],
                              start=(c == 0), stop=(c == nch - 1)))
    k.t(t_sumsq)
    pe_sumsq = s_pe.cnt

    def s_rstd(s):
        s.wait_ge(s_pe.h, pe_sumsq)
        s_act.inc(s.activation(ROW[0:1, stw:2 * stw], ps_a[0:1, stw:2 * stw], AF.Sqrt,
                               bias=eps_ap, scale=1.0 / T))
    k.s(s_rstd)
    act_rstd = s_act.cnt

    def v_recip(v):
        v.wait_ge(s_act.h, act_rstd)
        s_dve.inc(v.reciprocal(ROW[0:1, 0:stw], ROW[0:1, stw:2 * stw]))
    k.v(v_recip)
    dve_recip = s_dve.cnt

    def t_bcr(t):
        t.wait_ge(s_dve.h, dve_recip)
        s_pe.inc(t.matmul(ps_b[:, 0:stw], ONESROW[0:1, :], ROW[0:1, 0:stw], start=True, stop=True))
    k.t(t_bcr)
    pe_bcr = s_pe.cnt

    def v_norm(v):
        v.wait_ge(s_pe.h, pe_bcr)
        for c in range(nch):
            ins = v.tensor_tensor(XNc[c], XNc[c], ps_b[:, 0:stw], ALU.mult)
        s_dve.inc(ins)
    k.v(v_norm)
    return s_dve.cnt


def build_l0ffn(ntok=1024, stw=512):
    k = K()
    nc = k.nc
    nst = ntok // stw
    xres = nc.dram_tensor("xres", [D, ntok], F32, kind="ExternalInput").ap()
    wg_d = nc.dram_tensor("wg", [D, 8192], I8, kind="ExternalInput").ap()
    wu_d = nc.dram_tensor("wu", [D, 8192], I8, kind="ExternalInput").ap()
    wd_d = nc.dram_tensor("wd", [8192, D], I8, kind="ExternalInput").ap()
    sg_d = nc.dram_tensor("sg", [128, 64], F32, kind="ExternalInput").ap()
    su_d = nc.dram_tensor("su", [128, 64], F32, kind="ExternalInput").ap()
    sd_d = nc.dram_tensor("sd", [128, 16], F32, kind="ExternalInput").ap()
    xout = nc.dram_tensor("xout", [D, ntok], F32, kind="ExternalOutput").ap()

    X = k.sb("X", 128, NCH * stw, F32)
    XN = k.sb("XN", 128, NCH * stw, BF16)
    FUSE = k.sb("FUSE", 128, 64 * stw, BF16)
    WG = k.sb("WG", 128, 2 * 8192, BF16)
    WU = k.sb("WU", 128, 2 * 8192, BF16)
    WD = k.sb("WD", 128, 2 * 1024, BF16)
    G = k.sb("G", 128, 2 * stw, BF16)
    SQ = k.sb("SQ", 128, 2 * stw, BF16)
    SC = k.sb("SC", 128, 256, F32)
    ONES = k.sb("ONES", 128, 1, F32)
    CONSTS = k.sb("CONSTS", 128, 2, F32)
    BONES = k.sb("BONES", 128, 1, BF16)
    ONESROW = k.sb("ONESROW", 1, 128, F32)
    ROW = k.sb("ROW", 1, 2 * stw, F32)

    ps = [k.ps(f"ps{i}", 128, 512) for i in range(8)]

    s_x = k.sem("s_x", 16)
    s_st = k.sem("s_st", 16)
    s_wsc = k.sem("s_wsc", 16)
    s_wa = [k.sem("s_wa0", 16), k.sem("s_wa1", 16)]
    s_wb = [k.sem("s_wb0", 16), k.sem("s_wb1", 16)]
    s_g = k.sem("s_g", 1)
    s_pe = k.sem("s_pe", 1)
    s_act = k.sem("s_act", 1)
    s_dve = k.sem("s_dve", 1)

    k.begin()

    def g_init(g):
        s_wsc.inc(g.dma_start(out=SC[:, 0:64], in_=sg_d))
        s_wsc.inc(g.dma_start(out=SC[:, 64:128], in_=su_d))
        s_wsc.inc(g.dma_start(out=SC[:, 128:144], in_=sd_d))
        g.memset(ONES[:], 1.0)
        g.memset(CONSTS[:, 0:1], 1e-5)
        g.memset(BONES[:], 1.0)
        s_g.inc(g.memset(ONESROW[:], 1.0))
    k.g(g_init)
    w_scales, g_ones = s_wsc.cnt, s_g.cnt

    def v_init(v):
        v.wait_ge(s_wsc.h, w_scales)
        s_dve.inc(v.tensor_scalar(SC[:, 192:208], SC[:, 128:144], 0.5, None, ALU.mult))
    k.v(v_init)
    v_sdh = s_dve.cnt

    def g_wblk(g, fb, buf):
        for mat, dst in ((wg_d, WG), (wu_d, WU)):
            src = mat[:, fb * 512:(fb + 1) * 512]
            for kg in range(4):
                s_wa[buf].inc(g.dma_start(
                    out=dst[:, buf * 8192 + kg * 2048:buf * 8192 + (kg + 1) * 2048]
                        .rearrange("p (kc m) -> p kc m", kc=4),
                    in_=src[kg * 512:(kg + 1) * 512, :].rearrange("(kc p) m -> p kc m", p=128)))

    store_done = 0
    bufA_owner = [None, None]
    bufB_owner = None
    for st in range(nst):
        def y_load(y):
            y.wait_ge(s_st.h, store_done)
            for c in range(NCH):
                s_x.inc(y.dma_start(out=X[:, c * stw:(c + 1) * stw],
                                    in_=xres[c * 128:(c + 1) * 128, st * stw:(st + 1) * stw]))
        k.y(y_load)
        x_loaded = s_x.cnt

        def t_gate_ln(t):
            t.wait_ge(s_x.h, x_loaded)
            t.wait_ge(s_g.h, g_ones)
            if st > 0:
                t.wait_ge(s_dve.h, s_dve.cnt)
        k.t(t_gate_ln)

        class RowPair:
            def __getitem__(self, idx):
                part, free = idx
                lo, hi = free.start, free.stop
                if hi <= stw:
                    return ps[0][part, lo:hi]
                return ps[2][part, lo - stw:hi - stw]
        Xc = [X[:, c * stw:(c + 1) * stw] for c in range(NCH)]
        XNc = [XN[:, c * stw:(c + 1) * stw] for c in range(NCH)]
        ln_done = emit_ln(k, D, CONSTS[0:1, 0:1], Xc, XNc, stw, RowPair(), ps[1], SQ, ROW,
                          ONES, BONES, ONESROW, s_pe, s_act, s_dve)

        wblk_loaded = {}

        def g_w0(g):
            if bufA_owner[0] is not None:
                g.wait_ge(s_pe.h, bufA_owner[0])
            g_wblk(g, 0, 0)
        k.g(g_w0)
        wblk_loaded[0] = (s_wa[0], s_wa[0].cnt)

        peA0 = s_pe.cnt
        actA0 = s_act.cnt
        dveA0 = s_dve.cnt
        for fb in range(16):
            buf = fb % 2
            if fb + 1 < 16:
                def g_pref(g):
                    if fb >= 1:
                        g.wait_ge(s_pe.h, peA0 + 8 * fb)
                    elif bufA_owner[1] is not None:
                        g.wait_ge(s_pe.h, bufA_owner[1])
                    g_wblk(g, fb + 1, 1 - buf)
                k.g(g_pref)
                wblk_loaded[fb + 1] = (s_wa[1 - buf], s_wa[1 - buf].cnt)

            def t_A(t):
                sm, cv = wblk_loaded[fb]
                t.wait_ge(sm.h, cv)
                if fb == 0:
                    t.wait_ge(s_dve.h, ln_done)
                    t.wait_ge(s_dve.h, v_sdh)
                for oc in range(4):
                    fc = fb * 4 + oc
                    pg = ps[3 + 2 * (oc % 2)]
                    pu = ps[4 + 2 * (oc % 2)]
                    if fc >= 2:
                        t.wait_ge(s_act.h, actA0 + fc - 1)
                        t.wait_ge(s_dve.h, dveA0 + fc - 1)
                    base = buf * 8192
                    acc_mm(t, pg[:, 0:stw],
                           [(WG[:, base + kc * 512 + oc * 128:base + kc * 512 + (oc + 1) * 128],
                             XN[:, kc * stw:(kc + 1) * stw]) for kc in range(NCH)], s_pe)
                    acc_mm(t, pu[:, 0:stw],
                           [(WU[:, base + kc * 512 + oc * 128:base + kc * 512 + (oc + 1) * 128],
                             XN[:, kc * stw:(kc + 1) * stw]) for kc in range(NCH)], s_pe)
            k.t(t_A)

            def s_A(s):
                for oc in range(4):
                    fc = fb * 4 + oc
                    pg = ps[3 + 2 * (oc % 2)]
                    s.wait_ge(s_pe.h, peA0 + 8 * fb + 2 * oc + 1)
                    if fc >= 2:
                        s.wait_ge(s_dve.h, dveA0 + fc - 1)
                    s_act.inc(s.activation(G[:, (fc % 2) * stw:(fc % 2 + 1) * stw], pg[:, 0:stw],
                                           AF.Silu, scale=SC[:, fc:fc + 1]))
            k.s(s_A)

            def v_A(v):
                for oc in range(4):
                    fc = fb * 4 + oc
                    pu = ps[4 + 2 * (oc % 2)]
                    v.wait_ge(s_pe.h, peA0 + 8 * fb + 2 * oc + 2)
                    v.wait_ge(s_act.h, actA0 + fc + 1)
                    s_dve.inc(v.scalar_tensor_tensor(
                        FUSE[:, fc * stw:(fc + 1) * stw], pu[:, 0:stw], SC[:, 64 + fc:64 + fc + 1],
                        G[:, (fc % 2) * stw:(fc % 2 + 1) * stw], ALU.mult, ALU.mult))
            k.v(v_A)
        bufA_owner = [peA0 + 8 * 15, peA0 + 8 * 16]
        fuse_done = s_dve.cnt

        peB0 = s_pe.cnt
        wd_loaded = {}
        evac_cnt = s_dve.cnt
        for ocb in range(2):
            for kc in range(64):
                it = ocb * 64 + kc

                def g_wd(g):
                    if it >= 2:
                        g.wait_ge(s_pe.h, peB0 + it - 1)
                    elif bufB_owner is not None:
                        g.wait_ge(s_pe.h, bufB_owner)
                    s_wb[it % 2].inc(g.dma_start(
                        out=WD[:, (it % 2) * 1024:(it % 2 + 1) * 1024],
                        in_=wd_d[kc * 128:(kc + 1) * 128, ocb * 1024:(ocb + 1) * 1024]))
                k.g(g_wd)
                wd_loaded[it] = (s_wb[it % 2], s_wb[it % 2].cnt)

                def t_B(t):
                    sm, cv = wd_loaded[it]
                    t.wait_ge(sm.h, cv)
                    if it == 0:
                        t.wait_ge(s_dve.h, fuse_done)
                    if kc == 0 and ocb == 1:
                        t.wait_ge(s_dve.h, evac_cnt)
                    for ocl in range(8):
                        ins = t.matmul(ps[ocl][:, 0:stw],
                                       WD[:, (it % 2) * 1024 + ocl * 128:(it % 2) * 1024 + (ocl + 1) * 128],
                                       FUSE[:, kc * stw:(kc + 1) * stw],
                                       start=(kc == 0), stop=(kc == 63))
                    s_pe.inc(ins)
                k.t(t_B)
            peB_done = s_pe.cnt

            def v_B(v):
                v.wait_ge(s_pe.h, peB_done)
                for ocl in range(8):
                    c = ocb * 8 + ocl
                    s_dve.inc(v.scalar_tensor_tensor(
                        X[:, c * stw:(c + 1) * stw], ps[ocl][:, 0:stw], SC[:, 192 + c:192 + c + 1],
                        X[:, c * stw:(c + 1) * stw], ALU.mult, ALU.add))
            k.v(v_B)
            evac_cnt = s_dve.cnt

        bufB_owner = s_pe.cnt

        def y_store(y):
            y.wait_ge(s_dve.h, evac_cnt)
            for c in range(NCH):
                s_st.inc(y.dma_start(out=xout[c * 128:(c + 1) * 128, st * stw:(st + 1) * stw],
                                    in_=X[:, c * stw:(c + 1) * stw]))
            y.wait_ge(s_st.h, s_st.cnt)
        k.y(y_store)
        store_done = s_st.cnt

    nc._dbg = dict(X=X, XN=XN, FUSE=FUSE)
    k.st.close()
    return nc


def shard_fm(x):
    out = []
    for c in range(8):
        b, h = c // 2, c % 2
        out.append(np.ascontiguousarray(x[b, h * 1024:(h + 1) * 1024, :].T))
    return out


def unshard_fm(shards):
    x = np.empty((4, 2048, 2048), np.float32)
    for c in range(8):
        b, h = c // 2, c % 2
        x[b, h * 1024:(h + 1) * 1024, :] = np.asarray(shards[c]).T
    return x


def l0ffn_inmaps(xs, inputs):
    sg = np.ascontiguousarray(inputs["L0_ffn_s_gate"].reshape(64, 128).T)
    su = np.ascontiguousarray(inputs["L0_ffn_s_up"].reshape(64, 128).T)
    sd = np.ascontiguousarray(inputs["L0_ffn_s_down"].reshape(16, 128).T)
    return [{"xres": xs[c], "wg": inputs["L0_ffn_w_gate"], "wu": inputs["L0_ffn_w_up"],
             "wd": inputs["L0_ffn_w_down"], "sg": sg, "su": su, "sd": sd} for c in range(8)]


def _run_halved(builder, inmaps_fn, xs, inputs):
    maps = inmaps_fn(xs, inputs)
    for half in (0, 1):
        nc = builder(half)
        sel = [2 * b + half for b in range(4)]
        res = run_bass_kernel_spmd(nc, [maps[c] for c in sel], core_ids=list(range(4)))
        for b in range(4):
            xs[2 * b + half] = np.asarray(res.results[b]["xout"])
    return xs


def kernel(**inputs):
    nc = build_fused()
    maps = fused_inmaps(inputs)
    res = run_bass_kernel_spmd(nc, maps, core_ids=list(range(8)))
    xs = [np.asarray(res.results[c]["xout"]) for c in range(8)]
    return unshard_fm(xs)


if __name__ == "__main__":
    pass


def build_l0attn(half, T=2048, ntok=1024, stw=512, lnw=256):
    k = K()
    nc = k.nc
    own0 = half * ntok
    xfull = nc.dram_tensor("xfull", [D, T], F32, kind="ExternalInput").ap()
    wq_d = nc.dram_tensor("wq", [16, D, 128], F32, kind="ExternalInput").ap()
    wk_d = nc.dram_tensor("wk", [16, D, 128], F32, kind="ExternalInput").ap()
    wv_d = nc.dram_tensor("wv", [16, D, 128], F32, kind="ExternalInput").ap()
    wo_d = nc.dram_tensor("wo", [D, D], F32, kind="ExternalInput").ap()
    bqkvo_d = nc.dram_tensor("bqkvo", [128, 64], F32, kind="ExternalInput").ap()
    bvr_d = nc.dram_tensor("bvr", [1, D], F32, kind="ExternalInput").ap()
    xres = nc.dram_tensor("xres", [D, ntok], F32, kind="ExternalInput").ap()
    xout = nc.dram_tensor("xout", [D, ntok], F32, kind="ExternalOutput").ap()

    X = k.sb("X", 128, NCH * lnw, F32)
    XN = k.sb("XN", 128, NCH * T, BF16)
    QS = k.sb("QS", 128, ntok, BF16)
    KS = k.sb("KS", 128, T, BF16)
    VS = k.sb("VS", 128, 16 * 128, BF16)
    PT = k.sb("PT", 128, 16 * 512, BF16)
    Y = k.sb("Y", 128, 16 * ntok, BF16)
    WQ = k.sb("WQ", 128, 2 * 2048, BF16)
    WK = k.sb("WK", 128, 2 * 2048, BF16)
    WV = k.sb("WV", 128, 2 * 2048, BF16)
    WO = k.sb("WO", 128, 2 * 2048, BF16)
    XR = k.sb("XR", 128, 2 * 1024, F32)
    BIAS = k.sb("BIAS", 128, 64, F32)
    BVR = k.sb("BVR", 1, D, BF16)
    BONESROW = k.sb("BONESROW", 1, 128, BF16)
    RB = k.sb("RB", 128, 512, F32)
    MASK = k.sb("MASK", 128, 4 * 512, BF16)
    IDB = k.sb("IDB", 128, 128, BF16)
    SQ = k.sb("SQ", 128, 2 * lnw, BF16)
    CONSTS = k.sb("CONSTS", 128, 2, F32)
    ONES = k.sb("ONES", 128, 1, F32)
    BONES = k.sb("BONES", 128, 1, BF16)
    ONESROW = k.sb("ONESROW", 1, 128, F32)
    ROW = k.sb("ROW", 1, 4 * stw, F32)

    ps = [k.ps(f"ps{i}", 128, 512) for i in range(8)]

    s_x = k.sem("s_x", 16)
    s_xr = [k.sem("s_xr0", 16), k.sem("s_xr1", 16)]
    s_st = k.sem("s_st", 16)
    s_wsc = k.sem("s_wsc", 16)
    s_wh = [k.sem("s_wh0", 16), k.sem("s_wh1", 16)]
    s_wo = [k.sem("s_wo0", 16), k.sem("s_wo1", 16)]
    s_g = k.sem("s_g", 1)
    s_pe = k.sem("s_pe", 1)
    s_act = k.sem("s_act", 1)
    s_dve = k.sem("s_dve", 1)

    k.begin()

    def g_init(g):
        s_wsc.inc(g.dma_start(out=BIAS[:, 0:64], in_=bqkvo_d))
        s_wsc.inc(g.dma_start(out=BVR[:], in_=bvr_d))
        g.memset(ONES[:], 1.0)
        g.memset(BONES[:], 1.0)
        g.memset(ONESROW[:], 1.0)
        g.memset(BONESROW[:], 1.0)
        g.memset(CONSTS[:, 0:1], 1e-5)
        g.memset(CONSTS[:, 1:2], -ATT_C)
        g.memset(IDB[:], 1.0)
        s_g.inc(g.memset(MASK[:], 0.0))
        g.wait_ge(s_g.h, s_g.cnt)
        s_g.inc(g.affine_select(IDB[:], IDB[:], pattern=[[1, 128]], compare_op=ALU.is_equal,
                                fill=0.0, base=0, channel_multiplier=-1))
        g.wait_ge(s_g.h, s_g.cnt)
        for m in range(4):
            s_g.inc(g.affine_select(MASK[:, m * 512:(m + 1) * 512], MASK[:, m * 512:(m + 1) * 512],
                                    pattern=[[1, 512]], compare_op=ALU.is_ge, fill=-1e9,
                                    base=-m * 128, channel_multiplier=-1))
    k.g(g_init)
    g_done = s_g.cnt
    biases_in = s_wsc.cnt

    for sti in range(T // lnw):
        def y_load(y):
            if sti > 0:
                y.wait_ge(s_pe.h, s_pe.cnt)
                y.wait_ge(s_dve.h, s_dve.cnt)
            for c in range(NCH):
                s_x.inc(y.dma_start(out=X[:, c * lnw:(c + 1) * lnw],
                                    in_=xfull[c * 128:(c + 1) * 128, sti * lnw:(sti + 1) * lnw]))
        k.y(y_load)
        x_loaded = s_x.cnt

        def t_gate(t):
            t.wait_ge(s_x.h, x_loaded)
            t.wait_ge(s_g.h, g_done)
        k.t(t_gate)

        class RowPair:
            def __getitem__(self, idx):
                part, free = idx
                lo, hi = free.start, free.stop
                if hi <= lnw:
                    return ps[0][part, lo:hi]
                return ps[2][part, lo - lnw:hi - lnw]
        Xc = [X[:, c * lnw:(c + 1) * lnw] for c in range(NCH)]
        XNc = [XN[:, c * T + sti * lnw:c * T + (sti + 1) * lnw] for c in range(NCH)]
        emit_ln(k, D, CONSTS[0:1, 0:1], Xc, XNc, lnw, RowPair(), ps[1], SQ, ROW,
                ONES, BONES, ONESROW, s_pe, s_act, s_dve)
    ln_all = s_dve.cnt

    bufW_owner = [None, None]

    def g_whead(g, h, buf):
        for mat, dst in ((wq_d, WQ), (wk_d, WK), (wv_d, WV)):
            src = mat[h]
            for kg in range(2):
                s_wh[buf].inc(g.dma_start(
                    out=dst[:, buf * 2048 + kg * 1024:buf * 2048 + (kg + 1) * 1024]
                        .rearrange("p (c m) -> p c m", c=8),
                    in_=src[kg * 1024:(kg + 1) * 1024, :].rearrange("(c p) m -> p c m", p=128)))

    k.g(lambda g: g_whead(g, 0, 0))
    wh_loaded = {0: (s_wh[0], s_wh[0].cnt)}
    scale = float(128 ** -0.5)

    for h in range(16):
        buf = h % 2
        wbase = buf * 2048
        if h + 1 < 16:
            def g_pref(g):
                if bufW_owner[1 - buf] is not None:
                    g.wait_ge(s_pe.h, bufW_owner[1 - buf])
                g_whead(g, h + 1, 1 - buf)
            k.g(g_pref)
            wh_loaded[h + 1] = (s_wh[1 - buf], s_wh[1 - buf].cnt)

        prev_act, prev_dve, pe0 = s_act.cnt, s_dve.cnt, s_pe.cnt
        peQ = [pe0 + 1, pe0 + 2]
        peK = [pe0 + 3, pe0 + 4, pe0 + 5, pe0 + 6]
        peV = [pe0 + 7 + tt for tt in range(16)]
        actQK = [prev_act + 1 + j for j in range(6)]
        dveV = [prev_dve + 1 + tt for tt in range(16)]
        qps = [ps[0], ps[3]]
        kps = [ps[1], ps[4], ps[5], ps[6]]

        def t_qkv(t):
            sm, cv = wh_loaded[h]
            t.wait_ge(sm.h, cv)
            if h == 0:
                t.wait_ge(s_dve.h, ln_all)
                t.wait_ge(s_wsc.h, biases_in)
            else:
                t.wait_ge(s_act.h, prev_act)
                t.wait_ge(s_dve.h, prev_dve)
            for j in range(2):
                acc_mm(t, qps[j][:, 0:stw],
                       [(WQ[:, wbase + c * 128:wbase + (c + 1) * 128],
                         XN[:, c * T + own0 + j * stw:c * T + own0 + (j + 1) * stw])
                        for c in range(NCH)], s_pe)
            for j in range(4):
                acc_mm(t, kps[j][:, 0:stw],
                       [(WK[:, wbase + c * 128:wbase + (c + 1) * 128],
                         XN[:, c * T + j * stw:c * T + (j + 1) * stw]) for c in range(NCH)], s_pe)
            for tt in range(16):
                pv = ps[2] if tt % 2 == 0 else ps[7]
                if tt >= 2:
                    t.wait_ge(s_dve.h, dveV[tt - 2])
                for c in range(NCH):
                    t.matmul(pv[:, 0:128], XN[:, c * T + tt * 128:c * T + (tt + 1) * 128],
                             WV[:, wbase + c * 128:wbase + (c + 1) * 128],
                             start=(c == 0), stop=False)
                s_pe.inc(t.matmul(pv[:, 0:128], BONESROW[:], BVR[0:1, h * 128:(h + 1) * 128],
                                  start=False, stop=True))
        k.t(t_qkv)
        bufW_owner[buf] = s_pe.cnt
        assert s_pe.cnt == peV[15]

        def s_qk(sc):
            for j in range(2):
                sc.wait_ge(s_pe.h, peQ[j])
                s_act.inc(sc.activation(QS[:, j * stw:(j + 1) * stw], qps[j][:, 0:stw],
                                        AF.Identity, bias=BIAS[:, h:h + 1]))
            for j in range(4):
                sc.wait_ge(s_pe.h, peK[j])
                s_act.inc(sc.activation(KS[:, j * stw:(j + 1) * stw], kps[j][:, 0:stw],
                                        AF.Identity, bias=BIAS[:, 16 + h:16 + h + 1]))
        k.s(s_qk)
        assert s_act.cnt == actQK[5]

        def v_v(v):
            for tt in range(16):
                pv = ps[2] if tt % 2 == 0 else ps[7]
                v.wait_ge(s_pe.h, peV[tt])
                s_dve.inc(v.tensor_copy(VS[:, tt * 128:(tt + 1) * 128], pv[:, 0:128]))
        k.v(v_v)
        assert s_dve.cnt == dveV[15]
        evacA_act, evacA_dve = s_act.cnt, s_dve.cnt

        for qt in range(2):
            q0g = own0 + qt * stw
            kcmax = (q0g + stw) // 128
            peS0 = s_pe.cnt
            actS0 = s_act.cnt

            def t_scores(t):
                t.wait_ge(s_act.h, evacA_act)
                t.wait_ge(s_dve.h, evacA_dve)
                for kc in range(kcmax):
                    bank = ps[3] if kc % 2 == 0 else ps[4]
                    if kc >= 2:
                        t.wait_ge(s_act.h, actS0 + kc - 1)
                    m = kc - q0g // 128
                    diag = 0 <= m <= 3
                    ins = t.matmul(bank[:, 0:stw], KS[:, kc * 128:(kc + 1) * 128],
                                   QS[:, qt * stw:(qt + 1) * stw], start=True, stop=not diag)
                    if diag:
                        ins = t.matmul(bank[:, 0:stw], IDB[:], MASK[:, m * 512:(m + 1) * 512],
                                       start=False, stop=True)
                    s_pe.inc(ins)
            k.t(t_scores)

            def s_exp(sc):
                for kc in range(kcmax):
                    bank = ps[3] if kc % 2 == 0 else ps[4]
                    sc.wait_ge(s_pe.h, peS0 + kc + 1)
                    s_act.inc(sc.activation(PT[:, kc * stw:(kc + 1) * stw], bank[:, 0:stw],
                                            AF.Exp, bias=CONSTS[:, 1:2], scale=scale))
            k.s(s_exp)
            exp_done = s_act.cnt

            def t_denom(t):
                t.wait_ge(s_act.h, exp_done)
                acc_mm(t, ps[5][0:1, 0:stw],
                       [(BONES[:, 0:1], PT[:, kc * stw:(kc + 1) * stw]) for kc in range(kcmax)],
                       s_pe)
            k.t(t_denom)
            pe_denom = s_pe.cnt

            def v_recip(v):
                v.wait_ge(s_pe.h, pe_denom)
                s_dve.inc(v.tensor_scalar(ROW[0:1, 2 * stw:3 * stw], ps[5][0:1, 0:stw], 1e-30,
                                          None, ALU.add))
                v.wait_ge(s_dve.h, s_dve.cnt)
                s_dve.inc(v.reciprocal(ROW[0:1, 3 * stw:4 * stw], ROW[0:1, 2 * stw:3 * stw]))
            k.v(v_recip)
            dve_recip = s_dve.cnt

            def t_bcast(t):
                t.wait_ge(s_dve.h, dve_recip)
                s_pe.inc(t.matmul(ps[6][:, 0:stw], ONESROW[0:1, :], ROW[0:1, 3 * stw:4 * stw],
                                  start=True, stop=True))
            k.t(t_bcast)
            pe_bc = s_pe.cnt

            def s_rb(sc):
                sc.wait_ge(s_pe.h, pe_bc)
                s_act.inc(sc.copy(RB[:, 0:stw], ps[6][:, 0:stw]))
            k.s(s_rb)
            act_rb = s_act.cnt

            def t_av(t):
                acc_mm(t, ps[7][:, 0:stw],
                       [(VS[:, kc * 128:(kc + 1) * 128], PT[:, kc * stw:(kc + 1) * stw])
                        for kc in range(kcmax)], s_pe)
            k.t(t_av)
            pe_av = s_pe.cnt

            def v_norm2(v):
                v.wait_ge(s_pe.h, pe_av)
                v.wait_ge(s_act.h, act_rb)
                s_dve.inc(v.tensor_tensor(Y[:, h * ntok + qt * stw:h * ntok + (qt + 1) * stw],
                                          ps[7][:, 0:stw], RB[:, 0:stw], ALU.mult))
            k.v(v_norm2)
            evacA_act, evacA_dve = s_act.cnt, s_dve.cnt

    heads_done_pe, heads_done_act, heads_done_dve = s_pe.cnt, s_act.cnt, s_dve.cnt

    def g_wo(g, oc, buf):
        src = wo_d[:, oc * 128:(oc + 1) * 128]
        for kg in range(2):
            s_wo[buf].inc(g.dma_start(
                out=WO[:, buf * 2048 + kg * 1024:buf * 2048 + (kg + 1) * 1024]
                    .rearrange("p (c m) -> p c m", c=8),
                in_=src[kg * 1024:(kg + 1) * 1024, :].rearrange("(c p) m -> p c m", p=128)))

    k.g(lambda g: g_wo(g, 0, 0))
    wo_loaded = {0: (s_wo[0], s_wo[0].cnt)}
    xr_loaded = {}

    def y_xr0(y):
        s_xr[0].inc(y.dma_start(out=XR[:, 0:1024], in_=xres[0:128, :]))
    k.y(y_xr0)
    xr_loaded[0] = (s_xr[0], s_xr[0].cnt)

    peW0 = s_pe.cnt
    dveW0 = s_dve.cnt
    st_done = {}
    for oc in range(16):
        buf = oc % 2
        if oc + 1 < 16:
            def g_wpref(g):
                if oc >= 1:
                    g.wait_ge(s_pe.h, peW0 + 2 * oc)
                g_wo(g, oc + 1, 1 - buf)
            k.g(g_wpref)
            wo_loaded[oc + 1] = (s_wo[1 - buf], s_wo[1 - buf].cnt)

            def y_xr(y):
                if oc >= 1:
                    y.wait_ge(s_st.h, st_done[oc - 1])
                s_xr[1 - buf].inc(y.dma_start(out=XR[:, (1 - buf) * 1024:(2 - buf) * 1024],
                                              in_=xres[(oc + 1) * 128:(oc + 2) * 128, :]))
            k.y(y_xr)
            xr_loaded[oc + 1] = (s_xr[1 - buf], s_xr[1 - buf].cnt)

        def t_wo(t):
            sm, cv = wo_loaded[oc]
            t.wait_ge(sm.h, cv)
            if oc == 0:
                t.wait_ge(s_dve.h, heads_done_dve)
                t.wait_ge(s_act.h, heads_done_act)
            if oc >= 2:
                t.wait_ge(s_dve.h, dveW0 + 2 * (oc - 1))
            for j in range(2):
                acc_mm(t, ps[2 * buf + j][:, 0:stw],
                       [(WO[:, buf * 2048 + c * 128:buf * 2048 + (c + 1) * 128],
                         Y[:, c * ntok + j * stw:c * ntok + (j + 1) * stw]) for c in range(NCH)],
                       s_pe)
        k.t(t_wo)
        pe_oc = s_pe.cnt

        def v_wo(v):
            sm, cv = xr_loaded[oc]
            v.wait_ge(sm.h, cv)
            for j in range(2):
                v.wait_ge(s_pe.h, pe_oc - 1 + j)
                s_dve.inc(v.scalar_tensor_tensor(
                    XR[:, buf * 1024 + j * stw:buf * 1024 + (j + 1) * stw],
                    ps[2 * buf + j][:, 0:stw], BIAS[:, 48 + oc:48 + oc + 1],
                    XR[:, buf * 1024 + j * stw:buf * 1024 + (j + 1) * stw], ALU.add, ALU.add))
        k.v(v_wo)
        dve_oc = s_dve.cnt

        def y_sto(y):
            y.wait_ge(s_dve.h, dve_oc)
            s_st.inc(y.dma_start(out=xout[oc * 128:(oc + 1) * 128, :], in_=XR[:, buf * 1024:(buf + 1) * 1024]))
        k.y(y_sto)
        st_done[oc] = s_st.cnt

    def y_fin(y):
        y.wait_ge(s_st.h, s_st.cnt)
    k.y(y_fin)

    nc._dbg = dict(XN=XN, Y=Y, QS=QS, KS=KS, VS=VS, PT=PT)
    k.st.close()
    return nc


def l0attn_inmaps(xs, inputs):
    wq, wk, wv = inputs["L0_attn_Wq"], inputs["L0_attn_Wk"], inputs["L0_attn_Wv"]
    bqkvo = np.concatenate([inputs["L0_attn_bq"].T, inputs["L0_attn_bk"].T,
                            inputs["L0_attn_bv"].T,
                            inputs["L0_attn_bo"].reshape(16, 128).T], axis=1)
    bqkvo = np.ascontiguousarray(bqkvo)
    bvr = np.ascontiguousarray(inputs["L0_attn_bv"].reshape(1, 2048))
    maps = []
    for c in range(8):
        b, hlf = c // 2, c % 2
        xfull = np.ascontiguousarray(
            np.concatenate([np.asarray(xs[2 * b]), np.asarray(xs[2 * b + 1])], axis=1))
        maps.append({"xfull": xfull, "wq": wq, "wk": wk, "wv": wv, "wo": inputs["L0_attn_Wo"],
                     "bqkvo": bqkvo, "bvr": bvr, "xres": xs[c]})
    return maps


def build_l1ffn(half, T=2048, ntok=1024, lnw=256):
    k = K()
    nc = k.nc
    own0 = half * ntok
    xfull = nc.dram_tensor("xfull", [D, T], F32, kind="ExternalInput").ap()
    aggw_d = nc.dram_tensor("aggw", [D, D], F32, kind="ExternalInput").ap()
    d1w_d = nc.dram_tensor("d1w", [D, 8192], F32, kind="ExternalInput").ap()
    d2w_d = nc.dram_tensor("d2w", [8192, D], F32, kind="ExternalInput").ap()
    bcw_d = nc.dram_tensor("bcw", [D, D], F32, kind="ExternalInput").ap()
    gw_d = nc.dram_tensor("gw", [D, D], F32, kind="ExternalInput").ap()
    cols_d = nc.dram_tensor("cols", [128, 8 * 16 + 64], F32, kind="ExternalInput").ap()
    xout = nc.dram_tensor("xout", [D, ntok], F32, kind="ExternalOutput").ap()

    X = k.sb("X", 128, NCH * lnw, F32)
    XB = k.sb("XB", 128, 2 * 1024, F32)
    XN = k.sb("XN", 128, NCH * T, BF16)
    GT = k.sb("GT", 128, NCH * ntok, BF16)
    OT = k.sb("OT", 128, NCH * ntok, BF16)
    W1 = k.sb("W1", 128, 2 * 8192, BF16)
    W2 = k.sb("W2", 128, 2 * 2048, BF16)
    TH = k.sb("TH", 128, 16, F32)
    THB = k.sb("THB", 128, 16, BF16)
    TC = k.sb("TC", 128, 16, F32)
    TN = k.sb("TN", 128, 16, BF16)
    HB = k.sb("HB", 128, 64, BF16)
    SQT = k.sb("SQT", 128, 16, BF16)
    BCT = k.sb("BCT", 128, 16, F32)
    G2B2 = k.sb("G2B2", 128, 32, F32)
    COLS = k.sb("COLS", 128, 8 * 16 + 64, F32)
    TMP = k.sb("TMP", 128, 1024, F32)
    SQ = k.sb("SQ", 128, 2 * lnw, BF16)
    CONSTS = k.sb("CONSTS", 128, 2, F32)
    ONES = k.sb("ONES", 128, 1, F32)
    BONES = k.sb("BONES", 128, 1, BF16)
    ONESROW = k.sb("ONESROW", 1, 128, F32)
    ROW = k.sb("ROW", 1, 4 * lnw, F32)

    ps = [k.ps(f"ps{i}", 128, 512) for i in range(8)]

    s_x = k.sem("s_x", 16)
    s_xb = [k.sem("s_xb0", 16), k.sem("s_xb1", 16)]
    s_st = k.sem("s_st", 16)
    s_wsc = k.sem("s_wsc", 16)
    s_w1 = [k.sem("s_w10", 16), k.sem("s_w11", 16)]
    s_w2 = [k.sem("s_w20", 16), k.sem("s_w21", 16)]
    s_g = k.sem("s_g", 1)
    s_pe = k.sem("s_pe", 1)
    s_act = k.sem("s_act", 1)
    s_dve = k.sem("s_dve", 1)

    CA, CD2, CGM, CBT, CPG, CPB, CGB, CBC = (COLS[:, i * 16:(i + 1) * 16] for i in range(8))
    CD1 = COLS[:, 128:192]

    k.begin()

    def g_init(g):
        s_wsc.inc(g.dma_start(out=COLS[:], in_=cols_d))
        g.memset(ONES[:], 1.0)
        g.memset(BONES[:], 1.0)
        g.memset(ONESROW[:], 1.0)
        g.memset(CONSTS[:, 0:1], 1e-5)
        s_g.inc(g.memset(CONSTS[:, 1:2], 1e-6))
    k.g(g_init)
    g_done, cols_in = s_g.cnt, s_wsc.cnt

    def v_g2b2(v):
        v.wait_ge(s_wsc.h, cols_in)
        s_dve.inc(v.tensor_scalar(G2B2[:, 0:16], CGM, 0.5, None, ALU.mult))
        s_dve.inc(v.tensor_scalar(G2B2[:, 16:32], CBT, 0.5, None, ALU.mult))
    k.v(v_g2b2)
    g2_done = s_dve.cnt

    for sti in range(T // lnw):
        def y_load(y):
            if sti > 0:
                y.wait_ge(s_pe.h, s_pe.cnt)
                y.wait_ge(s_dve.h, s_dve.cnt)
            for c in range(NCH):
                s_x.inc(y.dma_start(out=X[:, c * lnw:(c + 1) * lnw],
                                    in_=xfull[c * 128:(c + 1) * 128, sti * lnw:(sti + 1) * lnw]))
        k.y(y_load)
        x_loaded = s_x.cnt

        def t_gate(t):
            t.wait_ge(s_x.h, x_loaded)
            t.wait_ge(s_g.h, g_done)
        k.t(t_gate)

        class RowPair:
            def __getitem__(self, idx):
                part, free = idx
                lo, hi = free.start, free.stop
                if hi <= lnw:
                    return ps[0][part, lo:hi]
                return ps[2][part, lo - lnw:hi - lnw]
        Xc = [X[:, c * lnw:(c + 1) * lnw] for c in range(NCH)]
        XNc = [XN[:, c * T + sti * lnw:c * T + (sti + 1) * lnw] for c in range(NCH)]
        emit_ln(k, D, CONSTS[0:1, 0:1], Xc, XNc, lnw, RowPair(), ps[1], SQ, ROW,
                ONES, BONES, ONESROW, s_pe, s_act, s_dve)
    ln_all = s_dve.cnt

    def emit_matvec(dram, nkc, noc, rhs_cols, out_ps, wbuf, wsems, gate_dve=None, gate_act=None):
        ncols = nkc * 128
        loaded = {}
        pe_gate = s_pe.cnt
        pe_base = s_pe.cnt

        def load(g, oc, buf):
            wsems[buf].inc(g.dma_start(
                out=wbuf[:, buf * ncols:(buf + 1) * ncols].rearrange("p (c m) -> p c m", c=nkc),
                in_=dram[:, oc * 128:(oc + 1) * 128].rearrange("(c p) m -> p c m", p=128)))

        def g_l0(g):
            g.wait_ge(s_pe.h, pe_gate)
            load(g, 0, 0)
        k.g(g_l0)
        loaded[0] = (wsems[0], wsems[0].cnt)
        for oc in range(noc):
            buf = oc % 2
            if oc + 1 < noc:
                def g_l(g):
                    if oc >= 1:
                        g.wait_ge(s_pe.h, pe_base + oc)
                    else:
                        g.wait_ge(s_pe.h, pe_gate)
                    load(g, oc + 1, 1 - buf)
                k.g(g_l)
                loaded[oc + 1] = (wsems[1 - buf], wsems[1 - buf].cnt)

            def t_mv(t):
                sm, cv = loaded[oc]
                t.wait_ge(sm.h, cv)
                if oc == 0:
                    if gate_dve is not None:
                        t.wait_ge(s_dve.h, gate_dve)
                    if gate_act is not None:
                        t.wait_ge(s_act.h, gate_act)
                acc_mm(t, out_ps[:, oc:oc + 1],
                       [(wbuf[:, buf * ncols + kc * 128:buf * ncols + (kc + 1) * 128],
                         rhs_cols(kc)) for kc in range(nkc)], s_pe)
            k.t(t_mv)
        return s_pe.cnt

    agg_done = emit_matvec(aggw_d, NCH, 16, lambda kc: XN[:, kc * T + T - 1:kc * T + T],
                           ps[3][:, 0:16], W2, s_w2, gate_dve=ln_all)

    def v_agg(v):
        v.wait_ge(s_pe.h, agg_done)
        v.wait_ge(s_wsc.h, cols_in)
        s_dve.inc(v.tensor_tensor(TH[:], ps[3][:, 0:16], CA, ALU.add))
    k.v(v_agg)

    prev_gelu = s_act.cnt
    for step in range(4):
        def t_tsum(t):
            t.wait_ge(s_dve.h, s_dve.cnt)
            s_pe.inc(t.matmul(ps[4][0:1, 0:16], ONES[:, 0:1], TH[:], start=True, stop=True))
        k.t(t_tsum)
        pe_tsum = s_pe.cnt

        def v_tmean(v):
            v.wait_ge(s_pe.h, pe_tsum)
            s_dve.inc(v.reduce_sum(ROW[0:1, 0:1], ps[4][0:1, 0:16], axis=mybir.AxisListType.X))
            v.wait_ge(s_dve.h, s_dve.cnt)
            s_dve.inc(v.tensor_scalar(ROW[0:1, 1:2], ROW[0:1, 0:1], -1.0 / D, None, ALU.mult))
        k.v(v_tmean)
        dve_tmean = s_dve.cnt

        def t_bcm(t):
            t.wait_ge(s_dve.h, dve_tmean)
            s_pe.inc(t.matmul(ps[5][:, 0:1], ONESROW[0:1, :], ROW[0:1, 1:2], start=True, stop=True))
        k.t(t_bcm)
        pe_bcm = s_pe.cnt

        def v_cent(v):
            v.wait_ge(s_pe.h, pe_bcm)
            s_dve.inc(v.tensor_scalar(TC[:], TH[:], ps[5][:, 0:1], None, ALU.add))
        k.v(v_cent)
        dve_cent = s_dve.cnt

        def s_sq(s):
            s.wait_ge(s_dve.h, dve_cent)
            s_act.inc(s.activation(SQT[:], TC[:], AF.Square))
        k.s(s_sq)
        act_sq = s_act.cnt

        def t_ssum(t):
            t.wait_ge(s_act.h, act_sq)
            s_pe.inc(t.matmul(ps[4][0:1, 16:32], BONES[:, 0:1], SQT[:], start=True, stop=True))
        k.t(t_ssum)
        pe_ssum = s_pe.cnt

        def v_var(v):
            v.wait_ge(s_pe.h, pe_ssum)
            s_dve.inc(v.reduce_sum(ROW[0:1, 2:3], ps[4][0:1, 16:32], axis=mybir.AxisListType.X))
        k.v(v_var)
        dve_var = s_dve.cnt

        def s_sqrt(s):
            s.wait_ge(s_dve.h, dve_var)
            s_act.inc(s.activation(ROW[0:1, 3:4], ROW[0:1, 2:3], AF.Sqrt,
                                   bias=CONSTS[0:1, 1:2], scale=1.0 / D))
        k.s(s_sqrt)
        act_sqrt = s_act.cnt

        def v_rcp(v):
            v.wait_ge(s_act.h, act_sqrt)
            s_dve.inc(v.reciprocal(ROW[0:1, 4:5], ROW[0:1, 3:4]))
        k.v(v_rcp)
        dve_rcp = s_dve.cnt

        def t_bcr(t):
            t.wait_ge(s_dve.h, dve_rcp)
            s_pe.inc(t.matmul(ps[5][:, 1:2], ONESROW[0:1, :], ROW[0:1, 4:5], start=True, stop=True))
        k.t(t_bcr)
        pe_bcr = s_pe.cnt

        def v_tn(v):
            v.wait_ge(s_pe.h, pe_bcr)
            s_dve.inc(v.tensor_scalar(TC[:], TC[:], ps[5][:, 1:2], None, ALU.mult))
            v.wait_ge(s_dve.h, s_dve.cnt)
            s_dve.inc(v.scalar_tensor_tensor(TN[:], TC[:], 1.0, CPG, ALU.mult, ALU.mult))
            v.wait_ge(s_dve.h, s_dve.cnt)
            s_dve.inc(v.tensor_tensor(TN[:], TN[:], CPB, ALU.add))
        k.v(v_tn)
        tn_done = s_dve.cnt

        d1_done = emit_matvec(d1w_d, NCH, 64, lambda kc: TN[:, kc:kc + 1], ps[6][:, 0:64],
                              W2, s_w2, gate_dve=tn_done, gate_act=prev_gelu)

        def s_gelu(s):
            s.wait_ge(s_pe.h, d1_done)
            for oc in range(64):
                s_act.inc(s.activation(HB[:, oc:oc + 1], ps[6][:, oc:oc + 1], AF.Gelu,
                                       bias=CD1[:, oc:oc + 1]))
        k.s(s_gelu)
        prev_gelu = s_act.cnt

        d2_done = emit_matvec(d2w_d, 64, 16, lambda kc: HB[:, kc:kc + 1], ps[7][:, 0:16],
                              W1, s_w1, gate_act=prev_gelu)

        def v_d2(v):
            v.wait_ge(s_pe.h, d2_done)
            s_dve.inc(v.tensor_tensor(TC[:], ps[7][:, 0:16], CD2, ALU.add))
            v.wait_ge(s_dve.h, s_dve.cnt)
            s_dve.inc(v.tensor_tensor(TH[:], TH[:], TC[:], ALU.add))
        k.v(v_d2)

    def v_thb(v):
        v.wait_ge(s_dve.h, s_dve.cnt)
        s_dve.inc(v.tensor_copy(THB[:], TH[:]))
    k.v(v_thb)
    thb_done = s_dve.cnt

    bc_done = emit_matvec(bcw_d, NCH, 16, lambda kc: THB[:, kc:kc + 1], ps[3][:, 0:16],
                          W2, s_w2, gate_dve=thb_done)

    def v_bc(v):
        v.wait_ge(s_pe.h, bc_done)
        s_dve.inc(v.tensor_tensor(BCT[:], ps[3][:, 0:16], CBC, ALU.add))
    k.v(v_bc)
    bc_ready = s_dve.cnt

    gw_loaded = {}

    def g_gw(g, oc, buf):
        g.wait_ge(s_pe.h, s_pe.cnt if oc <= 1 else gw_pe[oc - 2])
        src = gw_d[:, oc * 128:(oc + 1) * 128]
        for kg in range(2):
            s_w2[buf].inc(g.dma_start(
                out=W2[:, buf * 2048 + kg * 1024:buf * 2048 + (kg + 1) * 1024]
                    .rearrange("p (c m) -> p c m", c=8),
                in_=src[kg * 1024:(kg + 1) * 1024, :].rearrange("(c p) m -> p c m", p=128)))

    gw_pe = {}
    k.g(lambda g: g_gw(g, 0, 0))
    gw_loaded[0] = (s_w2[0], s_w2[0].cnt)
    actG0 = s_act.cnt
    dveG0 = s_dve.cnt
    for oc in range(16):
        buf = oc % 2
        if oc + 1 < 16:
            k.g(lambda g: g_gw(g, oc + 1, 1 - buf))
            gw_loaded[oc + 1] = (s_w2[1 - buf], s_w2[1 - buf].cnt)

        def t_gw(t):
            sm, cv = gw_loaded[oc]
            t.wait_ge(sm.h, cv)
            if oc <= 1:
                t.wait_ge(s_dve.h, bc_ready)
            if oc >= 2:
                t.wait_ge(s_act.h, actG0 + 2 * (oc - 1))
            for j in range(2):
                acc_mm(t, ps[2 * buf + j][:, 0:512],
                       [(W2[:, buf * 2048 + c * 128:buf * 2048 + (c + 1) * 128],
                         XN[:, c * T + own0 + j * 512:c * T + own0 + (j + 1) * 512])
                        for c in range(NCH)], s_pe)
        k.t(t_gw)
        gw_pe[oc] = s_pe.cnt
        pe_oc = s_pe.cnt

        def s_sig(s):
            for j in range(2):
                s.wait_ge(s_pe.h, pe_oc - 1 + j)
                s_act.inc(s.activation(GT[:, oc * ntok + j * 512:oc * ntok + (j + 1) * 512],
                                       ps[2 * buf + j][:, 0:512], AF.Sigmoid,
                                       bias=CGB[:, oc:oc + 1]))
        k.s(s_sig)
        act_oc = s_act.cnt

        def v_ot(v):
            v.wait_ge(s_act.h, act_oc)
            if oc == 0:
                v.wait_ge(s_dve.h, bc_ready)
            s_dve.inc(v.scalar_tensor_tensor(
                OT[:, oc * ntok:(oc + 1) * ntok], GT[:, oc * ntok:(oc + 1) * ntok],
                BCT[:, oc:oc + 1], XN[:, oc * T + own0:oc * T + own0 + ntok],
                ALU.mult, ALU.add))
        k.v(v_ot)
    ot_done = s_dve.cnt

    for sti in range(ntok // lnw):
        def t_gate2(t):
            t.wait_ge(s_dve.h, ot_done)
        k.t(t_gate2)

        class RowPair2:
            def __getitem__(self, idx):
                part, free = idx
                lo, hi = free.start, free.stop
                if hi <= lnw:
                    return ps[0][part, lo:hi]
                return ps[2][part, lo - lnw:hi - lnw]
        Xc = [OT[:, c * ntok + sti * lnw:c * ntok + (sti + 1) * lnw] for c in range(NCH)]
        XNc = [GT[:, c * ntok + sti * lnw:c * ntok + (sti + 1) * lnw] for c in range(NCH)]
        emit_ln(k, D, CONSTS[0:1, 1:2], Xc, XNc, lnw, RowPair2(), ps[1], SQ, ROW,
                BONES, BONES, ONESROW, s_pe, s_act, s_dve)
    lnf_done = s_dve.cnt

    st_done = {}
    for c in range(NCH):
        buf = c % 2

        def y_xb(y):
            if c >= 2:
                y.wait_ge(s_st.h, st_done[c - 1])
            s_xb[buf].inc(y.dma_start(out=XB[:, buf * 1024:(buf + 1) * 1024],
                                      in_=xfull[c * 128:(c + 1) * 128, own0:own0 + ntok]))
        k.y(y_xb)
        xb_in = (s_xb[buf], s_xb[buf].cnt)

        def v_app(v):
            if c == 0:
                v.wait_ge(s_dve.h, lnf_done)
                v.wait_ge(s_dve.h, g2_done)
            sm, cv = xb_in
            v.wait_ge(sm.h, cv)
            s_dve.inc(v.tensor_scalar(TMP[:], GT[:, c * ntok:(c + 1) * ntok],
                                      G2B2[:, c:c + 1], G2B2[:, 16 + c:16 + c + 1],
                                      ALU.mult, ALU.add))
            v.wait_ge(s_dve.h, s_dve.cnt)
            s_dve.inc(v.tensor_tensor(XB[:, buf * 1024:(buf + 1) * 1024],
                                      XB[:, buf * 1024:(buf + 1) * 1024], TMP[:], ALU.add))
        k.v(v_app)
        dve_c = s_dve.cnt

        def y_sto(y):
            y.wait_ge(s_dve.h, dve_c)
            s_st.inc(y.dma_start(out=xout[c * 128:(c + 1) * 128, :],
                                 in_=XB[:, buf * 1024:(buf + 1) * 1024]))
        k.y(y_sto)
        st_done[c] = s_st.cnt

    def y_fin(y):
        y.wait_ge(s_st.h, s_st.cnt)
    k.y(y_fin)

    nc._dbg = dict(XN=XN, GT=GT, OT=OT, TH=TH, BCT=BCT)
    k.st.close()
    return nc


def l1ffn_inmaps(xs, inputs):
    def colify(v, n=16):
        return np.ascontiguousarray(v.reshape(n, 128).T)
    cols = np.concatenate([colify(inputs["L1_ffn_agg_b"]), colify(inputs["L1_ffn_d2_b"]),
                           colify(inputs["L1_ffn_out_gamma"]), colify(inputs["L1_ffn_out_beta"]),
                           colify(inputs["L1_ffn_pn_gamma"]), colify(inputs["L1_ffn_pn_beta"]),
                           colify(inputs["L1_ffn_gate_b"]), colify(inputs["L1_ffn_bc_b"]),
                           colify(inputs["L1_ffn_d1_b"], 64)], axis=1)
    cols = np.ascontiguousarray(cols)
    maps = []
    for c in range(8):
        b = c // 2
        xfull = np.ascontiguousarray(
            np.concatenate([np.asarray(xs[2 * b]), np.asarray(xs[2 * b + 1])], axis=1))
        maps.append({"xfull": xfull, "aggw": inputs["L1_ffn_agg_w"], "d1w": inputs["L1_ffn_d1_w"],
                     "d2w": inputs["L1_ffn_d2_w"], "bcw": inputs["L1_ffn_bc_w"],
                     "gw": inputs["L1_ffn_gate_w"], "cols": cols})
    return maps


def build_l1attn(half, T=2048, ntok=1024):
    k = K()
    nc = k.nc
    CH = 256
    nchunks = (half + 1) * 4
    own_lo = half * 4
    xfull = nc.dram_tensor("xfull", [D, T], F32, kind="ExternalInput").ap()
    win_d = nc.dram_tensor("win", [D, 8512], F32, kind="ExternalInput").ap()
    wout_d = nc.dram_tensor("wout", [4096, D], F32, kind="ExternalInput").ap()
    cwk_d = nc.dram_tensor("cwk", [128, 4 * 34], F32, kind="ExternalInput").ap()
    cb_d = nc.dram_tensor("cb", [128, 34], F32, kind="ExternalInput").ap()
    auxc_d = nc.dram_tensor("auxc", [64, 2], F32, kind="ExternalInput").ap()
    drow_d = nc.dram_tensor("drow", [1, 64], F32, kind="ExternalInput").ap()
    nwr_d = nc.dram_tensor("nwr", [1, 4096], F32, kind="ExternalInput").ap()
    xout = nc.dram_tensor("xout", [D, ntok], F32, kind="ExternalOutput").ap()

    lnw = 256
    X = k.sb("X", 128, NCH * lnw, F32)
    XB = X
    XN = k.sb("XN", 128, NCH * lnw, BF16)
    PRE = k.sb("PRE", 128, 34 * (CH + 3), BF16)
    HALO = k.sb("HALO", 128, 34 * 3, BF16)
    XBC = k.sb("XBC", 128, 34 * CH, BF16)
    CONVT = k.sb("CONVT", 128, 2 * CH, F32)
    TMPY = CONVT
    ZS = k.sb("ZS", 128, 2 * 4096, BF16)
    CBTS = ZS[:, 0:384]
    LSEG = ZS[:, 384:768]
    M1S = ZS[:, 768:1152]
    XT = k.sb("XT", 128, 2 * 4096, BF16)
    YT = k.sb("YT", 128, 2 * 4096, BF16)
    YNF = k.sb("YNF", 128, 32 * CH, BF16)
    R = k.sb("R", 128, 4096, F32)
    RB = k.sb("RB", 128, 4096, BF16)
    WSTR = k.sb("WSTR", 128, 2 * 4096, BF16)
    DTJ = k.sb("DTJ", 64, 6 * 256, F32)
    NACS = k.sb("NACS", 64, 256, F32)
    DECF = k.sb("DECF", 64, 256, F32)
    DTD = k.sb("DTD", 64, 256, F32)
    ACSF = k.sb("ACSF", 1, 8 * 256, F32)
    NACSF = k.sb("NACSF", 1, 8 * 256, F32)
    DTT = k.sb("DTT", 128, 2 * 64, F32)
    DTDT = k.sb("DTDT", 128, 2 * 64, F32)
    EXAT = k.sb("EXAT", 128, 2 * 64, F32)
    ECER = k.sb("ECER", 1, 64, F32)
    ECEX = k.sb("ECEX", 128, 64, F32)
    DBC = k.sb("DBC", 128, 64, F32)
    NWB = k.sb("NWB", 128, 4096, BF16)
    VARC = k.sb("VARC", 128, 4, F32)
    CWK = k.sb("CWK", 128, 4 * 34, F32)
    CB = k.sb("CB", 128, 34, F32)
    AUXC = k.sb("AUXC", 64, 2, F32)
    DROW = k.sb("DROW", 1, 64, F32)
    ACOL = k.sb("ACOL", 64, 2, F32)
    MASKSEG = k.sb("MASKSEG", 128, 128, F32)
    IDF = k.sb("IDF", 128, 128, F32)
    IDB = k.sb("IDB", 128, 128, BF16)
    CONSTS = k.sb("CONSTS", 128, 2, F32)
    ONES = k.sb("ONES", 128, 1, F32)
    BONES = k.sb("BONES", 128, 1, BF16)
    ONESROW = k.sb("ONESROW", 1, 128, F32)
    ROW = k.sb("ROW", 1, 2 * lnw, F32)
    SQ = k.sb("SQ", 128, 2 * lnw, BF16)
    TMP2 = SQ
    BTOK = k.sb("BTOK", 128, 256, BF16)

    ps = [k.ps(f"ps{i}", 128, 512) for i in range(6)]
    psta = k.ps("psta", 128, 128, BF16)
    pstb = k.ps("pstb", 128, 128, BF16)

    s_x = k.sem("s_x", 16)
    s_nwq = k.sem("s_nwq", 16)
    s_xb = k.sem("s_xb", 16)
    s_st = k.sem("s_st", 16)
    s_wsc = k.sem("s_wsc", 16)
    s_w = [k.sem("s_w0", 16), k.sem("s_w1", 16)]
    s_fl = k.sem("s_fl", 16)
    s_g = k.sem("s_g", 1)
    s_pe = k.sem("s_pe", 1)
    s_act = k.sem("s_act", 1)
    s_dve = k.sem("s_dve", 1)

    def bar3(eng):
        eng.wait_ge(s_pe.h, s_pe.cnt)
        eng.wait_ge(s_act.h, s_act.cnt)
        eng.wait_ge(s_dve.h, s_dve.cnt)

    k.begin()

    def g_init(g):
        s_wsc.inc(g.dma_start(out=CWK[:], in_=cwk_d))
        s_wsc.inc(g.dma_start(out=CB[:], in_=cb_d))
        s_wsc.inc(g.dma_start(out=AUXC[:], in_=auxc_d))
        s_wsc.inc(g.dma_start(out=DROW[:], in_=drow_d))
        g.memset(ONES[:], 1.0)
        g.memset(BONES[:], 1.0)
        g.memset(ONESROW[:], 1.0)
        g.memset(CONSTS[:, 0:1], 1e-5)
        g.memset(CONSTS[:, 1:2], 1.0)
        g.memset(IDF[:], 1.0)
        g.memset(IDB[:], 1.0)
        g.memset(MASKSEG[:], 0.0)
        g.memset(R[:], 0.0)
        g.memset(RB[:], 0.0)
        s_g.inc(g.memset(HALO[:], 0.0))
        g.wait_ge(s_g.h, s_g.cnt)
        g.affine_select(IDF[:], IDF[:], pattern=[[1, 128]], compare_op=ALU.is_equal,
                        fill=0.0, base=0, channel_multiplier=-1)
        g.affine_select(IDB[:], IDB[:], pattern=[[1, 128]], compare_op=ALU.is_equal,
                        fill=0.0, base=0, channel_multiplier=-1)
        s_g.inc(g.affine_select(MASKSEG[:], MASKSEG[:], pattern=[[1, 128]], compare_op=ALU.is_ge,
                                fill=-1e9, base=0, channel_multiplier=-1))
    k.g(g_init)
    g_done = s_g.cnt
    aux_in = s_wsc.cnt



    def s_acol(s):
        s.wait_ge(s_wsc.h, aux_in)
        s_act.inc(s.activation(ACOL[:, 1:2], AUXC[:, 1:2], AF.Exp))
    k.s(s_acol)
    a1 = s_act.cnt

    def v_acol(v):
        v.wait_ge(s_act.h, a1)
        s_dve.inc(v.tensor_scalar(ACOL[:, 0:1], ACOL[:, 1:2], -1.0, None, ALU.mult))
    k.v(v_acol)

    def t_dbc(t):
        t.wait_ge(s_wsc.h, aux_in)
        t.wait_ge(s_g.h, g_done)
        s_pe.inc(t.matmul(ps[0][:, 0:64], ONESROW[0:1, :], DROW[0:1, :], start=True, stop=True))
    k.t(t_dbc)
    pe_dbc = s_pe.cnt

    def s_dbc(s):
        s.wait_ge(s_pe.h, pe_dbc)
        s_act.inc(s.copy(DBC[:], ps[0][:, 0:64]))
    k.s(s_dbc)

    for j in range(8):
        def y_nw(y, j=j):
            if j > 0:
                y.wait_ge(s_pe.h, s_pe.cnt)
            s_nwq.inc(y.dma_start(out=ROW[0:1, 0:512], in_=nwr_d[0:1, j * 512:(j + 1) * 512]))
        k.y(y_nw)
        nw_in = s_nwq.cnt

        def t_nw(t, j=j):
            t.wait_ge(s_nwq.h, nw_in)
            t.wait_ge(s_g.h, g_done)
            t.wait_ge(s_act.h, s_act.cnt)
            s_pe.inc(t.matmul(ps[1][:, 0:512], ONESROW[0:1, :], ROW[0:1, 0:512], start=True, stop=True))
        k.t(t_nw)
        pe_nw = s_pe.cnt

        def s_nw(s, j=j):
            s.wait_ge(s_pe.h, pe_nw)
            s_act.inc(s.copy(NWB[:, j * 512:(j + 1) * 512], ps[1][:, 0:512]))
        k.s(s_nw)
    def t_nwfin(t):
        t.wait_ge(s_act.h, s_act.cnt)
    k.t(t_nwfin)

    oV, oA, oE, oL, oDT, oACS = [j * 256 for j in range(6)]
    oMX, oDA = oA, oE

    st_done = {}
    for c2 in range(nchunks):
        own = c2 >= own_lo
        oc2 = c2 - own_lo
        def y_load(y):
            if c2 > 0:
                y.wait_ge(s_pe.h, s_pe.cnt)
                y.wait_ge(s_dve.h, s_dve.cnt)
                y.wait_ge(s_st.h, s_st.cnt)
            for c in range(NCH):
                s_x.inc(y.dma_start(out=X[:, c * lnw:(c + 1) * lnw],
                                    in_=xfull[c * 128:(c + 1) * 128, c2 * CH:(c2 + 1) * CH]))
        k.y(y_load)
        x_loaded = s_x.cnt

        def t_gate(t):
            t.wait_ge(s_x.h, x_loaded)
            t.wait_ge(s_g.h, g_done)
            bar3(t)
        k.t(t_gate)

        class RowPair:
            def __getitem__(self, idx):
                part, free = idx
                lo, hi = free.start, free.stop
                if hi <= lnw:
                    return ps[0][part, lo:hi]
                return ps[2][part, lo - lnw:hi - lnw]
        Xc = [X[:, c * lnw:(c + 1) * lnw] for c in range(NCH)]
        XNc = [XN[:, c * lnw:(c + 1) * lnw] for c in range(NCH)]
        emit_ln(k, D, CONSTS[0:1, 0:1], Xc, XNc, lnw, RowPair(), ps[1], SQ, ROW,
                ONES, BONES, ONESROW, s_pe, s_act, s_dve)

        wload = {}

        def g_wx(g, ch, buf):
            wd = 128 if ch < 34 else 64
            src = win_d[:, 4096 + ch * 128:4096 + ch * 128 + wd]
            s_w[buf].inc(g.dma_start(
                out=WSTR[:, buf * 4096:buf * 4096 + 16 * wd].rearrange("p (c m) -> p c m", c=16),
                in_=src.rearrange("(c p) m -> p c m", p=128)))

        def g_wx0(g):
            g.wait_ge(s_pe.h, s_pe.cnt)
            g_wx(g, 0, 0)
        k.g(g_wx0)
        wload[0] = (s_w[0], s_w[0].cnt)
        peP2 = s_pe.cnt
        dveP2 = s_dve.cnt
        ln_done = s_dve.cnt
        for ch in range(35):
            buf = ch % 2
            wd = 128 if ch < 34 else 64
            if ch + 1 < 35:
                def g_wn(g):
                    if ch >= 1:
                        g.wait_ge(s_pe.h, peP2 + ch)
                    g_wx(g, ch + 1, 1 - buf)
                k.g(g_wn)
                wload[ch + 1] = (s_w[1 - buf], s_w[1 - buf].cnt)

            def t_px(t):
                sm, cv = wload[ch]
                t.wait_ge(sm.h, cv)
                if ch == 0:
                    t.wait_ge(s_dve.h, ln_done)
                if ch >= 2:
                    t.wait_ge(s_dve.h, dveP2 + ch - 1)
                acc_mm(t, ps[3 + buf][0:wd, 0:CH],
                       [(WSTR[:, buf * 4096 + c * wd:buf * 4096 + (c + 1) * wd],
                         XN[:, c * lnw:(c + 1) * lnw]) for c in range(NCH)], s_pe)
            k.t(t_px)
            pe_ch = s_pe.cnt

            def v_px(v):
                v.wait_ge(s_pe.h, pe_ch)
                if ch < 34:
                    s_dve.inc(v.tensor_copy(PRE[:, ch * 259 + 3:(ch + 1) * 259], ps[3 + buf][:, 0:CH]))
                else:
                    s_dve.inc(v.tensor_copy(DTJ[:, oV:oV + 256], ps[3 + buf][0:64, 0:CH]))
            k.v(v_px)
        def v_halo(v):
            v.wait_ge(s_dve.h, s_dve.cnt)
            s_dve.inc(v.tensor_copy(
                PRE[:].rearrange("p (c w) -> p c w", c=34)[:, :, 0:3],
                HALO[:].rearrange("p (c w) -> p c w", c=34)))
        k.v(v_halo)

        dve_c3 = s_dve.cnt
        act_c3 = s_act.cnt

        def v_conv(v):
            bar3(v)
            for ch in range(34):
                base = ch * 259
                cb2 = (ch % 2) * 256
                if ch >= 2:
                    v.wait_ge(s_act.h, act_c3 + ch - 1)
                s_dve.inc(v.tensor_scalar(CONVT[:, cb2:cb2 + 256], PRE[:, base:base + 256],
                                          CWK[:, ch:ch + 1], None, ALU.mult))
                for kk in range(1, 4):
                    v.wait_ge(s_dve.h, s_dve.cnt)
                    s_dve.inc(v.scalar_tensor_tensor(CONVT[:, cb2:cb2 + 256],
                                                     PRE[:, base + kk:base + kk + 256],
                                                     CWK[:, kk * 34 + ch:kk * 34 + ch + 1],
                                                     CONVT[:, cb2:cb2 + 256], ALU.mult, ALU.add))
        k.v(v_conv)

        def s_conv(s):
            for ch in range(34):
                cb2 = (ch % 2) * 256
                s.wait_ge(s_dve.h, dve_c3 + 4 * (ch + 1))
                s_act.inc(s.activation(XBC[:, ch * CH:(ch + 1) * CH], CONVT[:, cb2:cb2 + 256],
                                       AF.Silu, bias=CB[:, ch:ch + 1]))
        k.s(s_conv)
        conv_done = s_act.cnt

        def v_halo2(v):
            v.wait_ge(s_dve.h, s_dve.cnt)
            v.wait_ge(s_act.h, conv_done)
            s_dve.inc(v.tensor_copy(
                HALO[:].rearrange("p (c w) -> p c w", c=34),
                PRE[:].rearrange("p (c w) -> p c w", c=34)[:, :, 256:259]))
        k.v(v_halo2)

        def s_dt1(s):
            bar3(s)
            s_act.inc(s.activation(DTJ[0:64, oV:oV + 256], DTJ[0:64, oV:oV + 256],
                                   AF.Identity, bias=AUXC[:, 0:1]))
            s.wait_ge(s_act.h, s_act.cnt)
            s_act.inc(s.activation(DTJ[0:64, oA:oA + 256], DTJ[0:64, oV:oV + 256], AF.Abs))
            s.wait_ge(s_act.h, s_act.cnt)
            s_act.inc(s.activation(DTJ[0:64, oE:oE + 256], DTJ[0:64, oA:oA + 256],
                                   AF.Exp, scale=-1.0))
            s.wait_ge(s_act.h, s_act.cnt)
            s_act.inc(s.activation(DTJ[0:64, oL:oL + 256], DTJ[0:64, oE:oE + 256],
                                   AF.Ln, bias=CONSTS[0:64, 1:2]))
        k.s(s_dt1)
        dt1 = s_act.cnt

        def v_dt2(v):
            v.wait_ge(s_act.h, dt1)
            s_dve.inc(v.tensor_scalar(DTJ[0:64, oMX:oMX + 256], DTJ[0:64, oV:oV + 256],
                                      0.0, None, ALU.max))
            v.wait_ge(s_dve.h, s_dve.cnt)
            s_dve.inc(v.tensor_tensor(DTJ[0:64, oDT:oDT + 256], DTJ[0:64, oMX:oMX + 256],
                                      DTJ[0:64, oL:oL + 256], ALU.add))
            v.wait_ge(s_dve.h, s_dve.cnt)
            s_dve.inc(v.tensor_scalar(DTJ[0:64, oDT:oDT + 256], DTJ[0:64, oDT:oDT + 256],
                                      100.0, None, ALU.min))
            v.wait_ge(s_dve.h, s_dve.cnt)
            s_dve.inc(v.tensor_scalar(DTJ[0:64, oDA:oDA + 256], DTJ[0:64, oDT:oDT + 256],
                                      ACOL[:, 0:1], None, ALU.mult))
            v.wait_ge(s_dve.h, s_dve.cnt)
            ones_ap = ONES[0:64, 0:1].unsqueeze(2).broadcast_to([64, 1, 256]).squeeze(1)
            s_dve.inc(v.tensor_tensor_scan(DTJ[0:64, oACS:oACS + 256], ones_ap,
                                           DTJ[0:64, oDA:oDA + 256], 0.0, ALU.mult, ALU.add))
            v.wait_ge(s_dve.h, s_dve.cnt)
            s_dve.inc(v.tensor_scalar(NACS[:], DTJ[0:64, oACS:oACS + 256], -1.0, None, ALU.mult))
            v.wait_ge(s_dve.h, s_dve.cnt)
            s_dve.inc(v.tensor_scalar(DECF[:], DTJ[0:64, oACS:oACS + 256],
                                      DTJ[0:64, oACS + 255:oACS + 256], -1.0,
                                      ALU.subtract, ALU.mult))
        k.v(v_dt2)
        dt2 = s_dve.cnt

        def s_dt3(s):
            s.wait_ge(s_dve.h, dt2)
            s_act.inc(s.activation(DECF[:], DECF[:], AF.Exp))
        k.s(s_dt3)
        dt3 = s_act.cnt

        def v_dt4(v):
            v.wait_ge(s_act.h, dt3)
            s_dve.inc(v.tensor_tensor(DTD[:], DTJ[0:64, oDT:oDT + 256], DECF[:], ALU.mult))
        k.v(v_dt4)

        dt4_done = s_dve.cnt

        def t_tr(t):
            bar3(t)
            for lt in range(2):
                t.transpose(ps[5][:, lt * 64:(lt + 1) * 64],
                            DTJ[0:64, oDT + lt * 128:oDT + (lt + 1) * 128], IDF[0:64, 0:64])
                t.transpose(ps[5][:, 128 + lt * 64:128 + (lt + 1) * 64],
                            DTD[:, lt * 128:(lt + 1) * 128], IDF[0:64, 0:64])
                s_pe.inc(t.transpose(ps[3][:, lt * 64:(lt + 1) * 64],
                                     DTJ[0:64, oACS + lt * 128:oACS + (lt + 1) * 128], IDF[0:64, 0:64]))
        k.t(t_tr)
        pe_tr1 = s_pe.cnt

        def v_tr1(v):
            v.wait_ge(s_pe.h, pe_tr1)
            s_dve.inc(v.tensor_copy(DTT[:], ps[5][:, 0:128]))
            s_dve.inc(v.tensor_copy(DTDT[:], ps[5][:, 128:256]))
        k.v(v_tr1)

        def t_ce(t):
            s_pe.inc(t.transpose(ps[3][0:1, 192:256], DTJ[0:64, oACS + 255:oACS + 256], IDF[0:64, 0:64]))
        k.t(t_ce)
        pe_ce = s_pe.cnt

        def s_tr1(s):
            s.wait_ge(s_pe.h, pe_tr1)
            s_act.inc(s.activation(EXAT[:], ps[3][:, 0:128], AF.Exp))
            s.wait_ge(s_pe.h, pe_ce)
            s_act.inc(s.activation(ECER[:], ps[3][0:1, 192:256], AF.Exp))
        k.s(s_tr1)
        ece1 = s_act.cnt

        def t_ecex(t):
            t.wait_ge(s_act.h, ece1)
            s_pe.inc(t.matmul(ps[3][:, 128:192], ONESROW[0:1, :], ECER[:], start=True, stop=True))
        k.t(t_ecex)
        pe_ecex = s_pe.cnt

        def s_ecex(s):
            s.wait_ge(s_pe.h, pe_ecex)
            s_act.inc(s.copy(ECEX[:], ps[3][:, 128:192]))
        k.s(s_ecex)

        dve_xt = s_dve.cnt
        pe_xt = s_pe.cnt

        def t_xt(t):
            t.wait_ge(s_act.h, conv_done)
            bar3(t)
            n = 0
            for ch in range(33):
                for lt in range(2):
                    bank = psta if n % 2 == 0 else pstb
                    if n >= 2:
                        t.wait_ge(s_dve.h, dve_xt + n - 1)
                    s_pe.inc(t.transpose(bank[:], XBC[:, ch * CH + lt * 128:ch * CH + (lt + 1) * 128], IDB[:]))
                    n += 1
        k.t(t_xt)

        def v_xt(v):
            n = 0
            for ch in range(33):
                for lt in range(2):
                    bank = psta if n % 2 == 0 else pstb
                    v.wait_ge(s_pe.h, pe_xt + n + 1)
                    if ch < 32:
                        dst = XT[:, lt * 4096 + ch * 128:lt * 4096 + (ch + 1) * 128]
                    else:
                        dst = BTOK[:, lt * 128:(lt + 1) * 128]
                    s_dve.inc(v.tensor_copy(dst, bank[:]))
                    n += 1
        k.v(v_xt)

        def t_cbt(t):
            bar3(t)
            B_ = XBC[:, 32 * CH:33 * CH]
            C_ = XBC[:, 33 * CH:34 * CH]
            t.matmul(ps[5][:, 0:128], B_[:, 0:128], C_[:, 0:128], start=True, stop=True)
            t.matmul(ps[5][:, 128:256], B_[:, 0:128], C_[:, 128:256], start=True, stop=True)
            s_pe.inc(t.matmul(ps[5][:, 256:384], B_[:, 128:256], C_[:, 128:256], start=True, stop=True))
        k.t(t_cbt)
        pe_cbt = s_pe.cnt

        def v_cbt(v):
            v.wait_ge(s_pe.h, pe_cbt)
            s_dve.inc(v.tensor_copy(CBTS, ps[5][:, 0:384]))
        k.v(v_cbt)

        def seg_cols(j):
            return slice(j * 128, (j + 1) * 128)

        if own:
            def t_p7gate(t):
                bar3(t)
            k.t(t_p7gate)
            p7_dve = s_dve.cnt
            exp_cnt = {}
            m1_cnt = {}
            ymm_cnt = {}
            for hg in range(8):
                evac_base = s_dve.cnt
                last_seg_pe = s_pe.cnt

                def g_flat(g, hg=hg):
                    g.wait_ge(s_dve.h, p7_dve)
                    if hg > 0:
                        g.wait_ge(s_pe.h, last_seg_pe)
                    s_fl.inc(g.dma_start(out=ACSF[:].rearrange("o (h l) -> o h l", h=8),
                                         in_=DTJ[hg * 8:(hg + 1) * 8, oACS:oACS + 256].unsqueeze(1)))
                    s_fl.inc(g.dma_start(out=NACSF[:].rearrange("o (h l) -> o h l", h=8),
                                         in_=NACS[hg * 8:(hg + 1) * 8, :].unsqueeze(1)))
                k.g(g_flat)
                flat_hg = s_fl.cnt

                for hh in range(8):
                    h = hg * 8 + hh
                    hb = hh * 256
                    segbank = ps[4] if h % 2 == 0 else ps[5]

                    def t_seg(t):
                        if hh == 0:
                            t.wait_ge(s_fl.h, flat_hg)
                        if h >= 2:
                            t.wait_ge(s_act.h, exp_cnt[h - 2])
                        for j, (sh, lh) in enumerate(((0, 0), (0, 1), (1, 1))):
                            t.matmul(segbank[:, seg_cols(j)],
                                     NACSF[0:1, hb + sh * 128:hb + (sh + 1) * 128],
                                     ONESROW[0:1, :], start=True, stop=False)
                            ins = t.matmul(segbank[:, seg_cols(j)], ONESROW[0:1, :],
                                           ACSF[0:1, hb + lh * 128:hb + (lh + 1) * 128],
                                           start=False, stop=(sh != lh))
                            if sh == lh:
                                ins = t.matmul(segbank[:, seg_cols(j)], IDF[:], MASKSEG[:],
                                               start=False, stop=True)
                        s_pe.inc(ins)
                    k.t(t_seg)
                    pe_seg = s_pe.cnt

                    def s_exp(s):
                        s.wait_ge(s_pe.h, pe_seg)
                        if h >= 1:
                            s.wait_ge(s_dve.h, m1_cnt[h - 1])
                        s_act.inc(s.activation(LSEG, segbank[:, 0:384], AF.Exp))
                    k.s(s_exp)
                    exp_cnt[h] = s_act.cnt

                    def v_m1(v):
                        v.wait_ge(s_act.h, exp_cnt[h])
                        if h >= 1:
                            v.wait_ge(s_pe.h, ymm_cnt[h - 1])
                        s_dve.inc(v.scalar_tensor_tensor(M1S[:, 0:256], LSEG[:, 0:256],
                                                         DTT[:, h:h + 1], CBTS[:, 0:256],
                                                         ALU.mult, ALU.mult))
                        v.wait_ge(s_dve.h, s_dve.cnt)
                        s_dve.inc(v.scalar_tensor_tensor(M1S[:, 256:384], LSEG[:, 256:384],
                                                         DTT[:, 64 + h:65 + h], CBTS[:, 256:384],
                                                         ALU.mult, ALU.mult))
                    k.v(v_m1)
                    m1_cnt[h] = s_dve.cnt

                    def t_y(t):
                        t.wait_ge(s_dve.h, m1_cnt[h])
                        if hg > 0 and hh == 0:
                            t.wait_ge(s_dve.h, evac_base)
                        hc = slice(hh * 64, (hh + 1) * 64)
                        t.matmul(ps[0][:, hc], M1S[:, 0:128], XT[:, 0 * 4096 + h * 64:0 * 4096 + (h + 1) * 64],
                                 start=True, stop=True)
                        t.matmul(ps[1][:, hc], M1S[:, 128:256], XT[:, 0 * 4096 + h * 64:0 * 4096 + (h + 1) * 64],
                                 start=True, stop=False)
                        t.matmul(ps[1][:, hc], M1S[:, 256:384], XT[:, 1 * 4096 + h * 64:1 * 4096 + (h + 1) * 64],
                                 start=False, stop=True)
                        t.matmul(ps[2][:, hc], XBC[:, 33 * CH:33 * CH + 128], RB[:, h * 64:(h + 1) * 64],
                                 start=True, stop=True)
                        s_pe.inc(t.matmul(ps[3][:, hc], XBC[:, 33 * CH + 128:34 * CH], RB[:, h * 64:(h + 1) * 64],
                                          start=True, stop=True))
                    k.t(t_y)
                    ymm_cnt[h] = s_pe.cnt
                pe_hg_end = s_pe.cnt

                def v_evac(v):
                    v.wait_ge(s_pe.h, pe_hg_end)
                    for lh in range(2):
                        ydo = ps[0] if lh == 0 else ps[1]
                        yof = ps[2] if lh == 0 else ps[3]
                        exa = EXAT[:, lh * 64 + hg * 8:lh * 64 + (hg + 1) * 8]
                        exap = exa.unsqueeze(2).broadcast_to([128, 8, 64])
                        dbp = DBC[:, hg * 8:(hg + 1) * 8].unsqueeze(2).broadcast_to([128, 8, 64])
                        xsl = XT[:, lh * 4096 + hg * 512:lh * 4096 + (hg + 1) * 512]
                        ysl = YT[:, lh * 4096 + hg * 512:lh * 4096 + (hg + 1) * 512]
                        r3 = lambda a: a.rearrange("p (a b) -> p a b", a=8)
                        v.wait_ge(s_dve.h, s_dve.cnt)
                        s_dve.inc(v.tensor_tensor(r3(TMPY[:]), r3(yof[:, 0:512]), exap, ALU.mult))
                        s_dve.inc(v.tensor_tensor(r3(TMP2[:]), r3(xsl), dbp, ALU.mult))
                        v.wait_ge(s_dve.h, s_dve.cnt)
                        s_dve.inc(v.tensor_tensor(TMPY[:], TMPY[:], ydo[:, 0:512], ALU.add))
                        v.wait_ge(s_dve.h, s_dve.cnt)
                        s_dve.inc(v.tensor_tensor(ysl, TMPY[:], TMP2[:], ALU.add))
                k.v(v_evac)

        def v_xdtd(v):
            bar3(v)
            for lt in range(2):
                dtp = DTDT[:, lt * 64:(lt + 1) * 64].unsqueeze(2).broadcast_to([128, 64, 64])
                s_dve.inc(v.tensor_tensor(
                    XT[:, lt * 4096:(lt + 1) * 4096].rearrange("p (a b) -> p a b", a=64),
                    XT[:, lt * 4096:(lt + 1) * 4096].rearrange("p (a b) -> p a b", a=64),
                    dtp, ALU.mult))
        k.v(v_xdtd)
        xdtd_done = s_dve.cnt

        for hg in range(8):
            def t_sc(t):
                t.wait_ge(s_dve.h, s_dve.cnt)
                for hh in range(8):
                    h = hg * 8 + hh
                    t.matmul(ps[4][:, hh * 64:(hh + 1) * 64], BTOK[:, 0:128],
                             XT[:, 0 * 4096 + h * 64:0 * 4096 + (h + 1) * 64], start=True, stop=False)
                    s_pe.inc(t.matmul(ps[4][:, hh * 64:(hh + 1) * 64], BTOK[:, 128:256],
                                      XT[:, 1 * 4096 + h * 64:1 * 4096 + (h + 1) * 64],
                                      start=False, stop=True))
            k.t(t_sc)
            pe_sc = s_pe.cnt

            def v_r(v):
                v.wait_ge(s_pe.h, pe_sc)
                v.wait_ge(s_act.h, s_act.cnt)
                ecp = ECEX[:, hg * 8:(hg + 1) * 8].unsqueeze(2).broadcast_to([128, 8, 64])
                rsl = R[:, hg * 512:(hg + 1) * 512]
                s_dve.inc(v.tensor_tensor(rsl.rearrange("p (a b) -> p a b", a=8),
                                          rsl.rearrange("p (a b) -> p a b", a=8), ecp, ALU.mult))
                v.wait_ge(s_dve.h, s_dve.cnt)
                s_dve.inc(v.tensor_tensor(rsl, rsl, ps[4][:, 0:512], ALU.add))
            k.v(v_r)

        def v_rb(v):
            v.wait_ge(s_dve.h, s_dve.cnt)
            s_dve.inc(v.tensor_copy(RB[:], R[:]))
        k.v(v_rb)

        if own:
            zld = {}

            def g_z(g, zb, buf):
                s_w[buf].inc(g.dma_start(
                    out=WSTR[:, buf * 4096:buf * 4096 + 4096].rearrange("p (c m) -> p c m", c=NCH),
                    in_=win_d[:, zb * 256:(zb + 1) * 256].rearrange("(c p) m -> p c m", p=128)))

            def g_z0(g):
                g.wait_ge(s_pe.h, s_pe.cnt)
                g_z(g, 0, 0)
            k.g(g_z0)
            zld[0] = (s_w[0], s_w[0].cnt)
            peZ = s_pe.cnt
            actZ = s_act.cnt
            for zb in range(16):
                buf = zb % 2
                if zb + 1 < 16:
                    def g_zn(g):
                        if zb >= 1:
                            g.wait_ge(s_pe.h, peZ + 2 * zb)
                        g_z(g, zb + 1, 1 - buf)
                    k.g(g_zn)
                    zld[zb + 1] = (s_w[1 - buf], s_w[1 - buf].cnt)

                def t_z(t):
                    sm, cv = zld[zb]
                    t.wait_ge(sm.h, cv)
                    if zb == 0:
                        t.wait_ge(s_dve.h, s_dve.cnt)
                    if zb >= 1:
                        t.wait_ge(s_act.h, actZ + 2 * zb)
                    for tt in range(2):
                        acc_mm(t, ps[3 + tt][:, 0:256],
                               [(XN[:, c * lnw + tt * 128:c * lnw + (tt + 1) * 128],
                                 WSTR[:, buf * 4096 + c * 256:buf * 4096 + (c + 1) * 256])
                                for c in range(NCH)], s_pe)
                k.t(t_z)
                pe_z = s_pe.cnt

                def s_z(s):
                    for tt in range(2):
                        s.wait_ge(s_pe.h, pe_z - 1 + tt)
                        s_act.inc(s.activation(ZS[:, tt * 4096 + zb * 256:tt * 4096 + (zb + 1) * 256],
                                               ps[3 + tt][:, 0:256], AF.Silu))
                k.s(s_z)

            gt0 = s_dve.cnt

            def v_gate(v):
                bar3(v)
                for lt in range(2):
                    ysl = YT[:, lt * 4096:(lt + 1) * 4096]
                    s_dve.inc(v.tensor_tensor(ysl, ysl, ZS[:, lt * 4096:(lt + 1) * 4096], ALU.mult))
            k.v(v_gate)
            g1 = s_dve.cnt

            def s_var(s):
                s.wait_ge(s_dve.h, g1)
                for lt in range(2):
                    s_act.inc(s.activation(ZS[:, lt * 4096:(lt + 1) * 4096],
                                           YT[:, lt * 4096:(lt + 1) * 4096], AF.Square,
                                           accum_out=VARC[:, lt:lt + 1]))
                s.wait_ge(s_act.h, s_act.cnt)
                s_act.inc(s.activation(VARC[:, 2:4], VARC[:, 0:2], AF.Sqrt,
                                       bias=CONSTS[:, 0:1], scale=1.0 / 4096.0))
            k.s(s_var)
            v1c = s_act.cnt

            def v_norm(v):
                v.wait_ge(s_act.h, v1c)
                s_dve.inc(v.reciprocal(VARC[:, 0:2], VARC[:, 2:4]))
                for lt in range(2):
                    ysl = YT[:, lt * 4096:(lt + 1) * 4096]
                    v.wait_ge(s_dve.h, s_dve.cnt)
                    s_dve.inc(v.tensor_scalar(ysl, ysl, VARC[:, lt:lt + 1], None, ALU.mult))
                    v.wait_ge(s_dve.h, s_dve.cnt)
                    s_dve.inc(v.tensor_tensor(ysl, ysl, NWB[:], ALU.mult))
            k.v(v_norm)

            pe_yt = s_pe.cnt
            dve_yt = s_dve.cnt

            def t_ytr(t):
                bar3(t)
                n = 0
                for fc in range(32):
                    for lt in range(2):
                        bank = psta if n % 2 == 0 else pstb
                        if n >= 2:
                            t.wait_ge(s_dve.h, dve_yt + n - 1)
                        s_pe.inc(t.transpose(bank[:], YT[:, lt * 4096 + fc * 128:lt * 4096 + (fc + 1) * 128], IDB[:]))
                        n += 1
            k.t(t_ytr)

            def v_ytr(v):
                n = 0
                for fc in range(32):
                    for lt in range(2):
                        bank = psta if n % 2 == 0 else pstb
                        v.wait_ge(s_pe.h, pe_yt + n + 1)
                        s_dve.inc(v.tensor_copy(YNF[:, fc * CH + lt * 128:fc * CH + (lt + 1) * 128], bank[:]))
                        n += 1
            k.v(v_ytr)
            ynf_done = s_dve.cnt

            wol = {}

            def g_wo(g, oc, buf):
                src = wout_d[:, oc * 128:(oc + 1) * 128]
                for kg in range(2):
                    s_w[buf].inc(g.dma_start(
                        out=WSTR[:, buf * 4096 + kg * 2048:buf * 4096 + (kg + 1) * 2048]
                            .rearrange("p (c m) -> p c m", c=16),
                        in_=src[kg * 2048:(kg + 1) * 2048, :].rearrange("(c p) m -> p c m", p=128)))

            def g_wo0(g):
                g.wait_ge(s_pe.h, s_pe.cnt)
                g_wo(g, 0, 0)
            k.g(g_wo0)
            wol[0] = (s_w[0], s_w[0].cnt)
            peW = s_pe.cnt
            dveW = s_dve.cnt
            stW = s_st.cnt
            for oc in range(16):
                buf = oc % 2
                if oc + 1 < 16:
                    def g_won(g):
                        if oc >= 1:
                            g.wait_ge(s_pe.h, peW + oc)
                        g_wo(g, oc + 1, 1 - buf)
                    k.g(g_won)
                    wol[oc + 1] = (s_w[1 - buf], s_w[1 - buf].cnt)

                def y_xb(y):
                    if oc == 0:
                        y.wait_ge(s_x.h, s_x.cnt)
                        y.wait_ge(s_dve.h, ynf_done)
                    if oc >= 2:
                        y.wait_ge(s_st.h, stW + 16 * oc)
                    s_xb.inc(y.dma_start(out=XB[:, buf * 256:(buf + 1) * 256],
                                        in_=xfull[oc * 128:(oc + 1) * 128, c2 * CH:(c2 + 1) * CH]))
                k.y(y_xb)
                xb_in = s_xb.cnt

                def t_wo(t):
                    sm, cv = wol[oc]
                    t.wait_ge(sm.h, cv)
                    if oc == 0:
                        t.wait_ge(s_dve.h, ynf_done)
                    if oc >= 2:
                        t.wait_ge(s_dve.h, dveW + oc - 1)
                    acc_mm(t, ps[buf][:, 0:256],
                           [(WSTR[:, buf * 4096 + c * 128:buf * 4096 + (c + 1) * 128],
                             YNF[:, c * CH:(c + 1) * CH]) for c in range(32)], s_pe)
                k.t(t_wo)
                pe_oc = s_pe.cnt

                def v_wo(v):
                    v.wait_ge(s_pe.h, pe_oc)
                    v.wait_ge(s_xb.h, xb_in)
                    s_dve.inc(v.tensor_tensor(XB[:, buf * 256:(buf + 1) * 256],
                                              XB[:, buf * 256:(buf + 1) * 256],
                                              ps[buf][:, 0:256], ALU.add))
                k.v(v_wo)
                dve_oc = s_dve.cnt

                def y_sto(y):
                    y.wait_ge(s_dve.h, dve_oc)
                    s_st.inc(y.dma_start(out=xout[oc * 128:(oc + 1) * 128, oc2 * CH:(oc2 + 1) * CH],
                                         in_=XB[:, buf * 256:(buf + 1) * 256]))
                k.y(y_sto)

    def y_fin(y):
        y.wait_ge(s_st.h, s_st.cnt)
    k.y(y_fin)

    nc._dbg = dict(XN=XN, XBC=XBC, YT=YT, YNF=YNF, R=R, ZS=ZS, XT=XT, DTJ=DTJ, ACSF=ACSF)
    k.st.close()
    return nc


def l1attn_inmaps(xs, inputs):
    cw = inputs["L1_attn_conv_w"]
    cwk = np.ascontiguousarray(cw.reshape(4, 34, 128).transpose(2, 0, 1).reshape(128, 4 * 34))
    cb = np.ascontiguousarray(inputs["L1_attn_conv_b"].reshape(34, 128).T)
    auxc = np.ascontiguousarray(np.stack([inputs["L1_attn_dt_bias"], inputs["L1_attn_A_log"]], axis=1))
    drow = np.ascontiguousarray(inputs["L1_attn_D"].reshape(1, 64))
    nwr = np.ascontiguousarray(inputs["L1_attn_norm_weight"].reshape(1, 4096))
    maps = []
    for c in range(8):
        b = c // 2
        xfull = np.ascontiguousarray(
            np.concatenate([np.asarray(xs[2 * b]), np.asarray(xs[2 * b + 1])], axis=1))
        maps.append({"xfull": xfull, "win": inputs["L1_attn_W_in"], "wout": inputs["L1_attn_W_out"],
                     "cwk": cwk, "cb": cb, "auxc": auxc, "drow": drow, "nwr": nwr})
    return maps



def build_fused(upto=99):
    k = K()
    nc = k.nc
    T, NT, CH = 2048, 1024, 256

    xres_d = nc.dram_tensor("xres", [D, NT], F32, kind="ExternalInput").ap()
    meta_d = nc.dram_tensor("meta", [1, 64], I32, kind="ExternalInput").ap()
    wg_d = nc.dram_tensor("wg", [128, 131072], I8, kind="ExternalInput").ap()
    wu_d = nc.dram_tensor("wu", [128, 131072], I8, kind="ExternalInput").ap()
    wd_d = nc.dram_tensor("wd", [128, 131072], I8, kind="ExternalInput").ap()
    sc_d = nc.dram_tensor("sc", [128, 144], F32, kind="ExternalInput").ap()
    wqkv_d = nc.dram_tensor("wqkv", [128, 49152], F32, kind="ExternalInput").ap()
    wo_d = nc.dram_tensor("wo", [128, 32768], F32, kind="ExternalInput").ap()
    attb_d = nc.dram_tensor("attb", [128, 32], F32, kind="ExternalInput").ap()
    bvr_d = nc.dram_tensor("bvr", [1, 1024], F32, kind="ExternalInput").ap()
    aggw_d = nc.dram_tensor("aggw", [D, D], F32, kind="ExternalInput").ap()
    d1h_d = nc.dram_tensor("d1h", [D, 4096], F32, kind="ExternalInput").ap()
    d2h_d = nc.dram_tensor("d2h", [4096, D], F32, kind="ExternalInput").ap()
    d1br_d = nc.dram_tensor("d1br", [1, 4096], F32, kind="ExternalInput").ap()
    bcw_d = nc.dram_tensor("bcw", [D, D], F32, kind="ExternalInput").ap()
    gw_d = nc.dram_tensor("gw", [128, 32768], F32, kind="ExternalInput").ap()
    f1c_d = nc.dram_tensor("f1c", [128, 128], F32, kind="ExternalInput").ap()
    winx_d = nc.dram_tensor("winx", [128, 37376], F32, kind="ExternalInput").ap()
    winz_d = nc.dram_tensor("winz", [128, 32768], F32, kind="ExternalInput").ap()
    wout_d = nc.dram_tensor("wout", [128, 65536], F32, kind="ExternalInput").ap()
    cwk_d = nc.dram_tensor("cwk", [128, 72], F32, kind="ExternalInput").ap()
    cb_d = nc.dram_tensor("cb", [128, 18], F32, kind="ExternalInput").ap()
    auxc_d = nc.dram_tensor("auxc", [32, 2], F32, kind="ExternalInput").ap()
    drow_d = nc.dram_tensor("drow", [1, 32], F32, kind="ExternalInput").ap()
    nwc_d = nc.dram_tensor("nwc", [128, 32], F32, kind="ExternalInput").ap()
    xout = nc.dram_tensor("xout", [D, NT], F32, kind="ExternalOutput").ap()

    scr_wg = nc.dram_tensor("scr_wg", [128, 131072], BF16, kind="Internal").ap()
    scr_wu = nc.dram_tensor("scr_wu", [128, 131072], BF16, kind="Internal").ap()
    scr_wd = nc.dram_tensor("scr_wd", [128, 131072], BF16, kind="Internal").ap()
    scr_wqkv = nc.dram_tensor("scr_wqkv", [128, 49152], BF16, kind="Internal").ap()
    scr_wo = nc.dram_tensor("scr_wo", [128, 32768], BF16, kind="Internal").ap()
    scr_wout = nc.dram_tensor("scr_wout", [128, 65536], BF16, kind="Internal").ap()
    scr_aggw = nc.dram_tensor("scr_aggw", [D, D], BF16, kind="Internal").ap()
    scr_d1h = nc.dram_tensor("scr_d1h", [D, 4096], BF16, kind="Internal").ap()
    scr_d2h = nc.dram_tensor("scr_d2h", [4096, D], BF16, kind="Internal").ap()
    scr_bcw = nc.dram_tensor("scr_bcw", [D, D], BF16, kind="Internal").ap()
    scr_gw = nc.dram_tensor("scr_gw", [128, 32768], BF16, kind="Internal").ap()
    scr_winx = nc.dram_tensor("scr_winx", [128, 37376], BF16, kind="Internal").ap()
    scr_winz = nc.dram_tensor("scr_winz", [128, 32768], BF16, kind="Internal").ap()
    SH_XN = nc.dram_tensor("sh_xn", [2, D, NT], BF16, kind="Internal", addr_space="Shared").ap()
    SH_Y = nc.dram_tensor("sh_y", [2, 2048, T], BF16, kind="Internal", addr_space="Shared").ap()
    SH_C = nc.dram_tensor("sh_c", [2, 128, 256], F32, kind="Internal", addr_space="Shared").ap()
    SH_F = nc.dram_tensor("sh_f", [2, 16], I32, kind="Internal", addr_space="Shared").ap()

    XR = k.sb("XR", 128, NCH * NT, F32)
    A1 = k.sb("A1", 128, NCH * T, BF16)
    A2 = k.sb("A2", 128, 8192, BF16)
    A3 = k.sb("A3", 128, 4096, F32)
    W1 = k.sb("W1", 128, 8192, BF16)
    W2 = k.sb("W2", 128, 4096, BF16)
    SQ = k.sb("SQ", 128, 1024, BF16)
    ROW = k.sb("ROW", 1, 2112, F32)
    ZROWB = k.sb("ZROWB", 1, 512, BF16)
    CONSTS = k.sb("CONSTS", 128, 4, F32)
    ONES = k.sb("ONES", 128, 1, F32)
    BONES = k.sb("BONES", 128, 1, BF16)
    ONESROW = k.sb("ONESROW", 1, 128, F32)
    BONESROW = k.sb("BONESROW", 1, 128, BF16)
    IDF = k.sb("IDF", 32, 32, F32)
    IDB = k.sb("IDB", 128, 128, BF16)
    MASKSEG = k.sb("MASKSEG", 128, 128, BF16)
    MASKC = k.sb("MASKC", 128, 896, BF16)
    SC = k.sb("SC", 128, 160, F32)
    ATTB = k.sb("ATTB", 128, 32, F32)
    F1C = k.sb("F1C", 128, 128, F32)
    G2B2 = k.sb("G2B2", 128, 32, F32)
    AUXC = k.sb("AUXC", 32, 2, F32)
    ACOL = k.sb("ACOL", 32, 2, F32)
    DROW = k.sb("DROW", 1, 32, F32)
    ECEX = k.sb("ECEX", 128, 32, F32)
    DBC = k.sb("DBC", 128, 32, F32)
    NWC = k.sb("NWC", 128, 32, F32)
    CWK = k.sb("CWK", 128, 72, F32)
    CBc = k.sb("CBc", 128, 18, F32)
    DTJ = k.sb("DTJ", 32, 4 * 256, F32)
    NACS = DTJ[0:32, 256:512]
    DECF = DTJ[0:32, 512:768]
    DTD = DECF
    DTT = k.sb("DTT", 128, 64, F32)
    DTDT = k.sb("DTDT", 128, 64, F32)
    EXAT = k.sb("EXAT", 128, 64, F32)
    BTOK = k.sb("BTOK", 128, 256, BF16)
    VARP = k.sb("VARP", 128, 20, F32)
    TH = k.sb("TH", 128, 16, F32)
    THB = k.sb("THB", 128, 16, BF16)
    TC = k.sb("TC", 128, 16, F32)
    TN = k.sb("TN", 128, 16, BF16)
    HC = k.sb("HC", 128, 32, BF16)
    BCT = k.sb("BCT", 128, 16, F32)
    SQT = k.sb("SQT", 128, 16, BF16)
    META = k.sb("META", 1, 64, I32)
    FL = k.sb("FL", 1, 16, I32)
    ACSF = ROW[0:1, 0:1024]
    NACSF = ROW[0:1, 1024:2048]
    ECER = ROW[0:1, 2048:2080]
    BVR = SQ[0:1, 0:1024]
    XBC = A1[:, 24576:24576 + 4608]
    RB = A1[:, 29184:31232]
    XNFS = A1[:, 0:8192]
    KS_A = A1[:, 8192:10240]
    VS_A = A1[:, 10240:12288]
    PT_A = A1[:, 12288:20480]
    YM_A = A1[:, 20480:22528]
    QS_A = A1[:, 22528:24576]
    YFULL = A1[:, 0:16384]
    XT_S = A1[:, 8192:12288]
    YT_S = A1[:, 12288:16384]
    ZS_S = A1[:, 16384:20480]
    YNFC = A1[:, 20480:24576]
    XNL = A1[:, 0:16384]
    D1BR = A2[0:1, 0:4096]
    GTL = A1[:, 16384:32768]
    HROW = A3[0:1, 0:4096]
    R = A3[:, 0:2048]
    CONVT = A3[:, 2048:2560]
    TMPY = A3[:, 2560:3072]
    TMPZ = A3[:, 3072:3584]
    XB0 = A3[:, 0:512]
    XB1 = A3[:, 512:1024]
    CBTS = A2[:, 0:384]
    LSEG = A2[:, 384:768]
    M1S = A2[:, 768:1152]
    TMP2 = A2[:, 1152:1664]
    PREB = A2[:, 1664:1664 + 18 * 259]

    ps = [k.ps(f"ps{i}", 128, 512) for i in range(6)]
    psta = k.ps("psta", 128, 128, BF16)
    pstb = k.ps("pstb", 128, 128, BF16)

    s_x = k.sem("s_x", 16)
    s_st = k.sem("s_st", 16)
    s_wsc = k.sem("s_wsc", 16)
    s_wa = [k.sem("s_wa0", 16), k.sem("s_wa1", 16)]
    s_wb = [k.sem("s_wb0", 16), k.sem("s_wb1", 16)]
    s_sh = k.sem("s_sh", 16)
    s_fl = k.sem("s_fl", 16)
    s_pl = k.sem("s_pl", 16)
    s_xs = k.sem("s_xs", 16)
    s_cb = k.sem("s_cb", 16)
    s_y = [k.sem("s_y0", 16), k.sem("s_y1", 16)]
    s_g = k.sem("s_g", 1)
    s_pe = k.sem("s_pe", 1)
    s_act = k.sem("s_act", 1)
    s_dve = k.sem("s_dve", 1)

    def bar3(eng):
        eng.wait_ge(s_pe.h, s_pe.cnt)
        eng.wait_ge(s_act.h, s_act.cnt)
        eng.wait_ge(s_dve.h, s_dve.cnt)

    k.begin()

    gstate = {}

    def g_comm_init(g):
        s_wsc.inc(g.dma_start(out=META[:], in_=meta_d))
        g.wait_ge(s_wsc.h, s_wsc.cnt)
        me = g.alloc_register()
        pr = g.alloc_register()
        g.reg_load(me, META[0:1, 0:1])
        g.reg_load(pr, META[0:1, 1:2])
        nonce = g.alloc_register()
        g.reg_load(nonce, META[0:1, 2:3])
        gstate["me"], gstate["pr"], gstate["nonce"] = me, pr, nonce
        gstate["sig"] = 0
        plreg = g.alloc_register()
        g.reg_mov(plreg, 0)
        gstate["plreg"] = plreg
        r_tgt = g.alloc_register()
        r_notdone = g.alloc_register()
        r_fv = g.alloc_register()
        gstate["tgt"], gstate["notdone"], gstate["fv"] = r_tgt, r_notdone, r_fv
    k.g(g_comm_init)

    def emit_poll(g, sig):
        tgt, notdone, fv = gstate["tgt"], gstate["notdone"], gstate["fv"]
        g.reg_load(tgt, META[0:1, 8 + sig:9 + sig])
        g.reg_mov(notdone, 1)
        with g.While(notdone):
            g.reg_add(gstate["plreg"], gstate["plreg"], 16)
            for my in (0, 1):
                with g.If_eq(gstate["pr"], my):
                    g.dma_start(out=FL[:], in_=SH_F[my:my + 1, :]).then_inc(s_pl.h, 16)
            g.wait_ge(s_pl.h, gstate["plreg"])
            g.reg_load(fv, FL[0:1, sig % 16:sig % 16 + 1])
            g.reg_alu(notdone, fv, tgt, ALU.is_lt)

    ALL_SEMS = [s_x, s_st, s_wsc, s_wa[0], s_wa[1], s_wb[0], s_wb[1], s_sh, s_fl, s_pl,
                s_g, s_pe, s_act, s_dve]

    def branch_pair(g, reg, fn):
        saves = {id(sm): sm.cnt for sm in ALL_SEMS}
        with g.If_eq(reg, 0):
            fn(0, g)
        after0 = {id(sm): sm.cnt for sm in ALL_SEMS}
        for sm in ALL_SEMS:
            sm.cnt = saves[id(sm)]
        with g.If_eq(reg, 1):
            fn(1, g)
        for sm in ALL_SEMS:
            assert sm.cnt == after0[id(sm)], "branch sem-count mismatch"

    def emit_send(pairs, producer_sems):
        gstate["sig"] += 1
        sig = gstate["sig"]

        def g_send(g):
            for sem, val in producer_sems:
                g.wait_ge(sem.h, val)
            if pairs:
                def pay(slot, g):
                    for slot_fn, src in pairs:
                        s_sh.inc(g.dma_start(out=slot_fn(slot), in_=src))
                branch_pair(g, gstate["me"], pay)
            g.wait_ge(s_sh.h, s_sh.cnt)

            def flg(slot, g):
                s_sh.inc(g.dma_start(out=SH_F[slot:slot + 1, sig % 16:sig % 16 + 1],
                                     in_=META[0:1, 8 + sig:9 + sig]))
            branch_pair(g, gstate["me"], flg)
            g.wait_ge(s_sh.h, s_sh.cnt)
        k.g(g_send)
        return sig

    def emit_recv(sig, reads, static_slot=None):
        def g_recv(g):
            emit_poll(g, sig)
            if static_slot is not None:
                for slot_fn, dst in reads:
                    s_sh.inc(g.dma_start(out=dst, in_=slot_fn(static_slot)))
            elif reads:
                def rd(slot, g):
                    for slot_fn, dst in reads:
                        s_sh.inc(g.dma_start(out=dst, in_=slot_fn(slot)))
                branch_pair(g, gstate["pr"], rd)
            else:
                s_sh.inc(g.dma_start(out=FL[:], in_=SH_F[0:1, :]))
        k.g(g_recv)
        return s_sh.cnt

    class _PL:
        pass

    def g_init(g):
        s_wsc.inc(g.dma_start(out=SC[:, 0:144], in_=sc_d))
        s_wsc.inc(g.dma_start(out=ATTB[:], in_=attb_d))
        s_wsc.inc(g.dma_start(out=BVR[:], in_=bvr_d))
        s_wsc.inc(g.dma_start(out=F1C[:], in_=f1c_d))
        s_wsc.inc(g.dma_start(out=CWK[:], in_=cwk_d))
        s_wsc.inc(g.dma_start(out=CBc[:], in_=cb_d))
        s_wsc.inc(g.dma_start(out=AUXC[:], in_=auxc_d))
        s_wsc.inc(g.dma_start(out=DROW[:], in_=drow_d))
        s_wsc.inc(g.dma_start(out=NWC[:], in_=nwc_d))
        g.memset(ONES[:], 1.0)
        g.memset(BONES[:], 1.0)
        g.memset(ONESROW[:], 1.0)
        g.memset(BONESROW[:], 1.0)
        g.memset(CONSTS[:, 0:1], 1e-5)
        g.memset(CONSTS[:, 1:2], 1e-6)
        g.memset(CONSTS[:, 2:3], -ATT_C)
        g.memset(CONSTS[:, 3:4], 1.0)
        g.memset(IDF[:], 1.0)
        g.memset(IDB[:], 1.0)
        g.memset(MASKSEG[:], 0.0)
        g.memset(ZROWB[:], 0.0)
        g.memset(RB[:], 0.0)
        s_g.inc(g.memset(MASKC[:], 0.0))
        g.wait_ge(s_g.h, s_g.cnt)
        s_g.inc(g.affine_select(MASKC[:], MASKC[:], pattern=[[1, 896]], compare_op=ALU.is_ge,
                                fill=-1e9, base=-384, channel_multiplier=-1))
        g.affine_select(IDF[:], IDF[:], pattern=[[1, 32]], compare_op=ALU.is_equal,
                        fill=0.0, base=0, channel_multiplier=-1)
        g.affine_select(IDB[:], IDB[:], pattern=[[1, 128]], compare_op=ALU.is_equal,
                        fill=0.0, base=0, channel_multiplier=-1)
        s_g.inc(g.affine_select(MASKSEG[:], MASKSEG[:], pattern=[[1, 128]], compare_op=ALU.is_ge,
                                fill=-1e9, base=0, channel_multiplier=-1))
    k.g(g_init)
    g_done = s_g.cnt
    aux_in = s_wsc.cnt

    def v_init(v):
        v.wait_ge(s_wsc.h, aux_in)
        s_dve.inc(v.tensor_scalar(SC[:, 144:160], SC[:, 128:144], 0.5, None, ALU.mult))
        s_dve.inc(v.tensor_scalar(G2B2[:, 0:16], F1C[:, 32:48], 0.5, None, ALU.mult))
        s_dve.inc(v.tensor_scalar(G2B2[:, 16:32], F1C[:, 48:64], 0.5, None, ALU.mult))
    k.v(v_init)

    def s_ainit(s):
        s.wait_ge(s_wsc.h, aux_in)
        s_act.inc(s.activation(ACOL[:, 1:2], AUXC[:, 1:2], AF.Exp))
    k.s(s_ainit)
    a1c = s_act.cnt

    def v_ainit(v):
        v.wait_ge(s_act.h, a1c)
        s_dve.inc(v.tensor_scalar(ACOL[:, 0:1], ACOL[:, 1:2], -1.0, None, ALU.mult))
    k.v(v_ainit)

    def t_dbc(t):
        t.wait_ge(s_wsc.h, aux_in)
        t.wait_ge(s_g.h, g_done)
        s_pe.inc(t.matmul(ps[0][:, 0:32], ONESROW[0:1, :], DROW[0:1, :], start=True, stop=True))
    k.t(t_dbc)
    pe_dbc = s_pe.cnt

    def s_dbc(s):
        s.wait_ge(s_pe.h, pe_dbc)
        s_act.inc(s.copy(DBC[:], ps[0][:, 0:32]))
    k.s(s_dbc)

    def y_xres(y):
        for c in range(NCH):
            s_x.inc(y.dma_start(out=XR[:, c * NT:(c + 1) * NT],
                                in_=xres_d[c * 128:(c + 1) * 128, :]))
    k.y(y_xres)
    xr_in = s_x.cnt

    def t_xr_gate(t):
        t.wait_ge(s_x.h, xr_in)
    k.t(t_xr_gate)

    SUBLAYER = [0]

    def finish_if(idx):
        if upto == idx:
            def y_out(y):
                y.wait_ge(s_pe.h, s_pe.cnt)
                y.wait_ge(s_dve.h, s_dve.cnt)
                y.wait_ge(s_act.h, s_act.cnt)
                for c in range(NCH):
                    s_st.inc(y.dma_start(out=xout[c * 128:(c + 1) * 128, :],
                                         in_=XR[:, c * NT:(c + 1) * NT]))
                y.wait_ge(s_st.h, s_st.cnt)
            k.y(y_out)
            k.st.close()
            return True
        return False

    def emit_l0ffn(inst=0):
        for st in range(2):
            stw = 512
            Xc = [XR[:, c * NT + st * stw:c * NT + (st + 1) * stw] for c in range(NCH)]
            XNc = [A2[:, c * stw:(c + 1) * stw] for c in range(NCH)]

            def t_g0(t):
                bar3(t)
                t.wait_ge(s_g.h, g_done)
            k.t(t_g0)

            class RowPair:
                def __getitem__(self, idx):
                    part, free = idx
                    lo, hi = free.start, free.stop
                    if hi <= stw:
                        return ps[0][part, lo:hi]
                    return ps[2][part, lo - stw:hi - stw]
            ln_done = emit_ln(k, D, CONSTS[0:1, 0:1], Xc, XNc, stw, RowPair(), ps[1], SQ, ROW,
                              ONES, BONES, ONESROW, s_pe, s_act, s_dve)

            wblk = {}

            def g_wblk(y, fb, buf):
                for mo, mat in enumerate((scr_wg, scr_wu)):
                    s_wa[buf].inc(y.dma_start(
                        out=W1[:, buf * 4096 + mo * 2048:buf * 4096 + (mo + 1) * 2048],
                        in_=mat[:, fb * 2048:(fb + 1) * 2048]))

            def g_w0(y):
                y.wait_ge(s_pe.h, s_pe.cnt)
                y.wait_ge(s_cb.h, CAST_AT["wuA"])
                g_wblk(y, 0, 0)
            k.y(g_w0)
            wblk[0] = s_wa[0].cnt

            peA0 = s_pe.cnt
            actA0 = s_act.cnt
            dveA0 = s_dve.cnt
            for fb in range(64):
                buf = fb % 2
                if fb + 1 < 64:
                    def g_pref(y):
                        y.wait_ge(s_pe.h, peA0 + 2 * fb)
                        if fb + 1 == 32:
                            y.wait_ge(s_cb.h, CAST_AT["wuB"])
                        g_wblk(y, fb + 1, 1 - buf)
                    k.y(g_pref)
                    wblk[fb + 1] = s_wa[1 - buf].cnt

                def t_A(t):
                    t.wait_ge(s_wa[buf].h, wblk[fb])
                    if fb == 0:
                        t.wait_ge(s_dve.h, ln_done)
                    fc = fb
                    pg = ps[3] if fc % 2 == 0 else ps[5]
                    pu = ps[4] if fc % 2 == 0 else ps[2]
                    if fc >= 2:
                        t.wait_ge(s_act.h, actA0 + fc - 1)
                        t.wait_ge(s_dve.h, dveA0 + fc - 1)
                    base = buf * 4096
                    acc_mm(t, pg[:, 0:stw],
                           [(W1[:, base + kc * 128:base + (kc + 1) * 128],
                             XNc[kc]) for kc in range(NCH)], s_pe)
                    acc_mm(t, pu[:, 0:stw],
                           [(W1[:, base + 2048 + kc * 128:base + 2048 + (kc + 1) * 128],
                             XNc[kc]) for kc in range(NCH)], s_pe)
                k.t(t_A)

                def s_A(sc):
                    fc = fb
                    pg = ps[3] if fc % 2 == 0 else ps[5]
                    sc.wait_ge(s_pe.h, peA0 + 2 * fb + 1)
                    if fc >= 2:
                        sc.wait_ge(s_dve.h, dveA0 + fc - 1)
                    s_act.inc(sc.activation(SQ[:, (fc % 2) * 512:(fc % 2) * 512 + 512],
                                            pg[:, 0:stw], AF.Silu, scale=SC[:, fc:fc + 1]))
                k.s(s_A)

                def v_A(v):
                    fc = fb
                    pu = ps[4] if fc % 2 == 0 else ps[2]
                    v.wait_ge(s_pe.h, peA0 + 2 * fb + 2)
                    v.wait_ge(s_act.h, actA0 + fc + 1)
                    s_dve.inc(v.scalar_tensor_tensor(
                        A1[:, fc * stw:(fc + 1) * stw], pu[:, 0:stw], SC[:, 64 + fc:64 + fc + 1],
                        SQ[:, (fc % 2) * 512:(fc % 2) * 512 + 512], ALU.mult, ALU.mult))
                k.v(v_A)
            fuse_done = s_dve.cnt

            peB0 = s_pe.cnt
            evac_cnt = s_dve.cnt
            for ocg in range(4):
                wg8 = {}

                def g_wd0(y, ocg=ocg):
                    y.wait_ge(s_pe.h, s_pe.cnt)
                    if ocg == 0:
                        y.wait_ge(s_cb.h, CAST_AT["wd"])
                    s_wa[0].inc(y.dma_start(
                        out=W1[:, 0:4096],
                        in_=scr_wd[:, (ocg * 64) * 512:(ocg * 64 + 8) * 512]))
                k.y(g_wd0)
                wg8[0] = s_wa[0].cnt
                peG0 = s_pe.cnt
                for kg8 in range(8):
                    buf2 = kg8 % 2
                    if kg8 + 1 < 8:
                        def g_wd(y, ocg=ocg, kg8=kg8, buf2=buf2):
                            if kg8 >= 1:
                                y.wait_ge(s_pe.h, peG0 + 8 * kg8)
                            s_wa[1 - buf2].inc(y.dma_start(
                                out=W1[:, (1 - buf2) * 4096:(2 - buf2) * 4096],
                                in_=scr_wd[:, (ocg * 64 + (kg8 + 1) * 8) * 512:
                                           (ocg * 64 + (kg8 + 2) * 8) * 512]))
                        k.y(g_wd)
                        wg8[kg8 + 1] = s_wa[1 - buf2].cnt

                    def t_B(t, kg8=kg8, buf2=buf2):
                        t.wait_ge(s_wa[buf2].h, wg8[kg8])
                        if kg8 == 0:
                            t.wait_ge(s_dve.h, fuse_done if ocg == 0 else evac_cnt)
                        for kcl in range(8):
                            kc = kg8 * 8 + kcl
                            ins = None
                            for ocl in range(4):
                                ins = t.matmul(ps[ocl][:, 0:stw],
                                               W1[:, buf2 * 4096 + kcl * 512 + ocl * 128:
                                                  buf2 * 4096 + kcl * 512 + (ocl + 1) * 128],
                                               A1[:, kc * stw:(kc + 1) * stw],
                                               start=(kc == 0), stop=(kc == 63))
                            s_pe.inc(ins)
                    k.t(t_B)
                peG_done = s_pe.cnt

                def v_B(v):
                    v.wait_ge(s_pe.h, peG_done)
                    for ocl in range(4):
                        c = ocg * 4 + ocl
                        s_dve.inc(v.scalar_tensor_tensor(
                            XR[:, c * NT + st * stw:c * NT + (st + 1) * stw],
                            ps[ocl][:, 0:stw], SC[:, 144 + c:144 + c + 1],
                            XR[:, c * NT + st * stw:c * NT + (st + 1) * stw], ALU.mult, ALU.add))
                k.v(v_B)
                evac_cnt = s_dve.cnt

    def emit_l0attn():
        for st in range(2):
            stw = 512
            Xc = [XR[:, c * NT + st * stw:c * NT + (st + 1) * stw] for c in range(NCH)]
            XNc = [A2[:, c * stw:(c + 1) * stw] for c in range(NCH)]

            def t_g0(t):
                bar3(t)
                t.wait_ge(s_sh.h, s_sh.cnt)
            k.t(t_g0)

            class RowPair:
                def __getitem__(self, idx):
                    part, free = idx
                    lo, hi = free.start, free.stop
                    if hi <= stw:
                        return ps[0][part, lo:hi]
                    return ps[2][part, lo - stw:hi - stw]
            ln_done = emit_ln(k, D, CONSTS[0:1, 0:1], Xc, XNc, stw, RowPair(), ps[1], SQ, ROW,
                              ONES, BONES, ONESROW, s_pe, s_act, s_dve)
            emit_send([(lambda slot, st=st, cc=c2: SH_XN[slot, cc * 128:(cc + 1) * 128,
                                                         st * stw:(st + 1) * stw],
                        A2[:, c2 * stw:(c2 + 1) * stw])
                       for c2 in range(NCH)], [(s_dve, ln_done)])
        sig_xn = gstate["sig"]

        def g_bvr(g):
            g.wait_ge(s_act.h, s_act.cnt)
            g.wait_ge(s_pe.h, s_pe.cnt)
            s_wsc.inc(g.dma_start(out=BVR, in_=bvr_d))
        k.g(g_bvr)
        bvr_in = s_wsc.cnt

        xn_ready = emit_recv(sig_xn, [])

        scale = float(128 ** -0.5)
        wq_ld = {}

        def wreg(buf, mi):
            if buf == 0:
                return W1[:, mi * 2048:(mi + 1) * 2048]
            return (W1[:, 6144:8192], W2[:, 0:2048], W2[:, 2048:4096])[mi]

        def g_whead(y, h, buf):
            sems = s_wa[0] if buf == 0 else s_wb[0]
            for mi in range(3):
                sems.inc(y.dma_start(
                    out=wreg(buf, mi),
                    in_=scr_wqkv[:, (h * 3 + mi) * 2048:(h * 3 + mi + 1) * 2048]))

        def g_w0(y):
            y.wait_ge(s_pe.h, s_pe.cnt)
            y.wait_ge(s_cb.h, CAST_AT["wqkv"])
            g_whead(y, 0, 0)
        k.y(g_w0)
        wq_ld[0] = (s_wa[0], s_wa[0].cnt)

        def xnfb(st4):
            return XNFS if st4 % 2 == 0 else PT_A

        def g_xnf(y, st4, gatepe):
            y.wait_ge(s_pe.h, gatepe)
            slot = st4 // 2
            off = (st4 % 2) * 512
            buf = xnfb(st4)
            for c2 in range(NCH):
                s_xs.inc(y.dma_start(out=buf[:, c2 * 512:(c2 + 1) * 512],
                                     in_=SH_XN[slot, c2 * 128:(c2 + 1) * 128, off:off + 512]))

        prev_qkv2 = 0
        for h in range(8):
            buf = h % 2
            head_pe0 = s_pe.cnt
            if h + 1 < 8:
                def g_pref(y):
                    y.wait_ge(s_pe.h, s_pe.cnt)
                    g_whead(y, h + 1, 1 - buf)
                k.y(g_pref)

                def g_gap(g):
                    cast_gap(g, 2)
                k.g(g_gap)
                wq_ld[h + 1] = ((s_wa[0] if 1 - buf == 0 else s_wb[0]),
                                (s_wa[0] if 1 - buf == 0 else s_wb[0]).cnt)

            ld_at = {}

            def emit_ld(st4, gate):
                def y_ld(y, st4=st4, gate=gate):
                    if h == 0:
                        y.wait_ge(s_sh.h, xn_ready)
                    g_xnf(y, st4, gate)
                k.y(y_ld)
                ld_at[st4] = s_xs.cnt

            emit_ld(0, prev_qkv2)
            emit_ld(1, head_pe0)
            qkv_pe = {}
            for st4 in range(4):
                bufX = xnfb(st4)

                def t_qkv(t, st4=st4, bufX=bufX):
                    sm, cv = wq_ld[h]
                    t.wait_ge(sm.h, cv)
                    t.wait_ge(s_xs.h, ld_at[st4])
                    t.wait_ge(s_act.h, s_act.cnt)
                    t.wait_ge(s_dve.h, s_dve.cnt)
                    if h == 0 and st4 == 0:
                        t.wait_ge(s_sh.h, xn_ready)
                        t.wait_ge(s_wsc.h, bvr_in)
                    acc_mm(t, ps[0][:, 0:512],
                           [(wreg(buf, 1)[:, c * 128:(c + 1) * 128], bufX[:, c * 512:(c + 1) * 512])
                            for c in range(NCH)], s_pe)
                    acc_mm(t, ps[1][:, 0:512],
                           [(wreg(buf, 0)[:, c * 128:(c + 1) * 128], bufX[:, c * 512:(c + 1) * 512])
                            for c in range(NCH)], s_pe)
                    for tt in range(4):
                        for c in range(NCH):
                            t.matmul(ps[2][:, tt * 128:(tt + 1) * 128],
                                     bufX[:, c * 512 + tt * 128:c * 512 + (tt + 1) * 128],
                                     wreg(buf, 2)[:, c * 128:(c + 1) * 128],
                                     start=(c == 0), stop=False)
                        s_pe.inc(t.matmul(ps[2][:, tt * 128:(tt + 1) * 128], BONESROW[:],
                                          BVR[0:1, h * 128:(h + 1) * 128], start=False, stop=True))
                k.t(t_qkv)
                pe_q = s_pe.cnt
                qkv_pe[st4] = pe_q
                if st4 <= 1:
                    emit_ld(st4 + 2, pe_q)

                def s_kq(sc, st4=st4, pe_q=pe_q):
                    sc.wait_ge(s_pe.h, pe_q - 5)
                    s_act.inc(sc.activation(KS_A[:, st4 * 512:(st4 + 1) * 512], ps[0][:, 0:512],
                                            AF.Identity, bias=ATTB[:, 8 + h:8 + h + 1]))
                    sc.wait_ge(s_pe.h, pe_q - 4)
                    s_act.inc(sc.activation(QS_A[:, st4 * 512:(st4 + 1) * 512],
                                            ps[1][:, 0:512], AF.Identity, bias=ATTB[:, h:h + 1]))
                k.s(s_kq)

                def v_v(v, st4=st4, pe_q=pe_q):
                    v.wait_ge(s_pe.h, pe_q)
                    for tt in range(4):
                        s_dve.inc(v.tensor_copy(VS_A[:, st4 * 512 + tt * 128:st4 * 512 + (tt + 1) * 128],
                                                ps[2][:, tt * 128:(tt + 1) * 128]))
                k.v(v_v)
            prev_qkv2 = qkv_pe[2]
            kv_act = s_act.cnt
            kv_dve = s_dve.cnt

            for qt in range(4):
                kcmax = 4 * (qt + 1)
                peS0 = s_pe.cnt
                actS0 = s_act.cnt

                def t_sc(t, qt=qt, kcmax=kcmax, peS0=peS0, actS0=actS0):
                    if qt == 0:
                        t.wait_ge(s_act.h, kv_act)
                        t.wait_ge(s_dve.h, kv_dve)
                    for kc in range(kcmax):
                        bank = ps[3] if kc % 2 == 0 else ps[4]
                        if kc >= 2:
                            t.wait_ge(s_act.h, actS0 + kc - 1)
                        m = kc - qt * 4
                        diag = 0 <= m <= 3
                        ins = t.matmul(bank[:, 0:512], KS_A[:, kc * 128:(kc + 1) * 128],
                                       QS_A[:, qt * 512:(qt + 1) * 512], start=True, stop=not diag)
                        if diag:
                            ins = t.matmul(bank[:, 0:512], IDB[:],
                                           MASKC[:, (3 - m) * 128:(3 - m) * 128 + 512],
                                           start=False, stop=True)
                        s_pe.inc(ins)
                k.t(t_sc)

                def s_exp(sc, qt=qt, kcmax=kcmax, peS0=peS0):
                    for kc in range(kcmax):
                        bank = ps[3] if kc % 2 == 0 else ps[4]
                        sc.wait_ge(s_pe.h, peS0 + kc + 1)
                        s_act.inc(sc.activation(PT_A[:, kc * 512:(kc + 1) * 512], bank[:, 0:512],
                                                AF.Exp, bias=CONSTS[:, 2:3], scale=scale))
                k.s(s_exp)
                exp_done = s_act.cnt

                def t_den(t, kcmax=kcmax, exp_done=exp_done):
                    t.wait_ge(s_act.h, exp_done)
                    acc_mm(t, ps[5][0:1, 0:512],
                           [(BONES[:, 0:1], PT_A[:, kc * 512:(kc + 1) * 512]) for kc in range(kcmax)],
                           s_pe)
                k.t(t_den)
                pe_den = s_pe.cnt

                def v_rc(v, pe_den=pe_den):
                    v.wait_ge(s_pe.h, pe_den)
                    s_dve.inc(v.tensor_scalar(ROW[0:1, 0:512], ps[5][0:1, 0:512], 1e-30, None, ALU.add))
                    v.wait_ge(s_dve.h, s_dve.cnt)
                    s_dve.inc(v.reciprocal(ROW[0:1, 512:1024], ROW[0:1, 0:512]))
                k.v(v_rc)
                dve_rc = s_dve.cnt

                def t_bc(t, dve_rc=dve_rc):
                    t.wait_ge(s_dve.h, dve_rc)
                    s_pe.inc(t.matmul(ps[5][:, 0:512], ONESROW[0:1, :], ROW[0:1, 512:1024],
                                      start=True, stop=True))
                k.t(t_bc)
                pe_bc = s_pe.cnt

                def s_rb(sc, pe_bc=pe_bc):
                    sc.wait_ge(s_pe.h, pe_bc)
                    s_act.inc(sc.copy(TMPY[:], ps[5][:, 0:512]))
                k.s(s_rb)
                act_rb = s_act.cnt

                def t_av(t, qt=qt, kcmax=kcmax):
                    acc_mm(t, ps[2][:, 0:512],
                           [(VS_A[:, kc * 128:(kc + 1) * 128], PT_A[:, kc * 512:(kc + 1) * 512])
                            for kc in range(kcmax)], s_pe)
                k.t(t_av)
                pe_av = s_pe.cnt

                def v_nm(v, qt=qt, pe_av=pe_av, act_rb=act_rb):
                    v.wait_ge(s_pe.h, pe_av)
                    v.wait_ge(s_act.h, act_rb)
                    if qt == 0:
                        v.wait_ge(s_sh.h, s_sh.cnt)
                    s_dve.inc(v.tensor_tensor(YM_A[:, qt * 512:(qt + 1) * 512], ps[2][:, 0:512],
                                              TMPY[:], ALU.mult))
                k.v(v_nm)
            ym_done = s_dve.cnt
            emit_send([(lambda slot, h=h: SH_Y[slot, h * 128:(h + 1) * 128, :], YM_A[:])],
                      [(s_dve, ym_done)])
        sig_y = gstate["sig"]

        def g_yf(g):
            g.wait_ge(s_pe.h, s_pe.cnt)
            g.wait_ge(s_act.h, s_act.cnt)
            g.wait_ge(s_dve.h, s_dve.cnt)
            emit_poll(g, sig_y)
            def rd(my, g):
                for c in range(NCH):
                    s_sh.inc(g.dma_start(
                        out=YFULL[:, c * 1024:(c + 1) * 1024],
                        in_=SH_Y[c // 8, (c % 8) * 128:(c % 8 + 1) * 128,
                                 my * 1024:(my + 1) * 1024]))
            branch_pair(g, gstate["me"], rd)
        k.g(g_yf)
        yf_in = s_sh.cnt

        for ocg in range(4):
            wo_base = s_pe.cnt

            def g_wo(y, ocg=ocg):
                y.wait_ge(s_pe.h, s_pe.cnt)
                if ocg == 0:
                    y.wait_ge(s_cb.h, CAST_AT["wo"])
                for kg in range(2):
                    s_wa[0].inc(y.dma_start(
                        out=W1[:, kg * 4096:(kg + 1) * 4096],
                        in_=scr_wo[:, ocg * 8192 + kg * 4096:ocg * 8192 + (kg + 1) * 4096]))
            k.y(g_wo)
            wo_in = s_wa[0].cnt

            for tt in range(2):
                def t_wo(t, ocg=ocg, tt=tt, wo_in=wo_in):
                    t.wait_ge(s_wa[0].h, wo_in)
                    if ocg == 0 and tt == 0:
                        t.wait_ge(s_sh.h, yf_in)
                    if ocg > 0 or tt > 0:
                        t.wait_ge(s_dve.h, s_dve.cnt)
                    for ocl in range(4):
                        acc_mm(t, ps[ocl][:, 0:512],
                               [(W1[:, (c // 8) * 4096 + (c % 8) * 512 + ocl * 128:
                                     (c // 8) * 4096 + (c % 8) * 512 + (ocl + 1) * 128],
                                 YFULL[:, c * 1024 + tt * 512:c * 1024 + (tt + 1) * 512])
                                for c in range(NCH)], s_pe)
                k.t(t_wo)
                pe_wo = s_pe.cnt

                def v_wo(v, ocg=ocg, tt=tt, pe_wo=pe_wo):
                    v.wait_ge(s_pe.h, pe_wo)
                    for ocl in range(4):
                        c = ocg * 4 + ocl
                        s_dve.inc(v.scalar_tensor_tensor(
                            XR[:, c * NT + tt * 512:c * NT + (tt + 1) * 512],
                            ps[ocl][:, 0:512], ATTB[:, 16 + c:16 + c + 1],
                            XR[:, c * NT + tt * 512:c * NT + (tt + 1) * 512], ALU.add, ALU.add))
                k.v(v_wo)


    def emit_l1ffn():
        for st in range(2):
            stw = 512
            Xc = [XR[:, c * NT + st * stw:c * NT + (st + 1) * stw] for c in range(NCH)]
            XNc = [XNL[:, c * 1024 + st * stw:c * 1024 + (st + 1) * stw] for c in range(NCH)]

            def t_g0(t):
                bar3(t)
                t.wait_ge(s_sh.h, s_sh.cnt)
                t.wait_ge(s_fl.h, s_fl.cnt)
            k.t(t_g0)

            class RowPair:
                def __getitem__(self, idx):
                    part, free = idx
                    lo, hi = free.start, free.stop
                    if hi <= stw:
                        return ps[0][part, lo:hi]
                    return ps[2][part, lo - stw:hi - stw]
            emit_ln(k, D, CONSTS[0:1, 0:1], Xc, XNc, stw, RowPair(), ps[1], SQ, ROW,
                    ONES, BONES, ONESROW, s_pe, s_act, s_dve)
        ln_own = s_dve.cnt

        def v_xl(v):
            v.wait_ge(s_dve.h, ln_own)
            s_dve.inc(v.tensor_copy(BCT[:], XNL[:, 1023:16384:1024]))
        k.v(v_xl)
        xl_done = s_dve.cnt
        emit_send([(lambda slot: SH_C[slot, :, 0:16], BCT[:])], [(s_dve, xl_done)])
        sig_xl = gstate["sig"]
        xl_in = emit_recv(sig_xl, [(lambda slot: SH_C[slot, :, 0:16], BCT[:])], static_slot=1)

        pe_gate_pre = s_pe.cnt

        def v_thb0(v):
            v.wait_ge(s_sh.h, xl_in)
            s_dve.inc(v.tensor_copy(THB[:], BCT[:]))
        k.v(v_thb0)
        thb0 = s_dve.cnt

        def mv_row(wdram, nkc, col_lo, width, rhs_cols_fn, gate_dve, bias_row=None, ready=None,
                   pe_gate=None):
            ld = {}
            g0 = s_pe.cnt if pe_gate is None else pe_gate

            def g_l0(y):
                y.wait_ge(s_pe.h, g0)
                if ready is not None:
                    y.wait_ge(s_cb.h, ready)
                s_y[0].inc(y.dma_start(out=W2[:, 0:width], in_=wdram[0:128, col_lo:col_lo + width]))
            k.y(g_l0)
            ld[0] = (s_y[0], s_y[0].cnt)
            pe0 = s_pe.cnt
            for kc in range(nkc):
                buf = kc % 2
                if kc + 1 < nkc:
                    def g_l(y, kc=kc, buf=buf):
                        if kc >= 1:
                            y.wait_ge(s_pe.h, pe0 + kc)
                        s_y[1 - buf].inc(y.dma_start(
                            out=W2[:, (1 - buf) * 2048:(1 - buf) * 2048 + width],
                            in_=wdram[(kc + 1) * 128:(kc + 2) * 128, col_lo:col_lo + width]))
                    k.y(g_l)
                    ld[kc + 1] = (s_y[1 - buf], s_y[1 - buf].cnt)

                def t_mv(t, kc=kc, buf=buf):
                    sm, cv = ld[kc]
                    t.wait_ge(sm.h, cv)
                    if kc == 0:
                        t.wait_ge(s_dve.h, gate_dve)
                        t.wait_ge(s_act.h, s_act.cnt)
                    ins = None
                    for nb in range((width + 511) // 512):
                        w = min(512, width - nb * 512)
                        ins = t.matmul(ps[nb][0:1, 0:w], rhs_cols_fn(kc),
                                       W2[:, buf * 2048 + nb * 512:buf * 2048 + nb * 512 + w],
                                       start=(kc == 0), stop=False)
                    s_pe.inc(ins)
                k.t(t_mv)

            def t_bias(t):
                ins = None
                for nb in range((width + 511) // 512):
                    w = min(512, width - nb * 512)
                    rhs = bias_row(nb, w) if bias_row is not None else ZROWB[0:1, 0:w]
                    ins = t.matmul(ps[nb][0:1, 0:w], BONES[0:1, 0:1], rhs, start=False, stop=True)
                s_pe.inc(ins)
            k.t(t_bias)
            return s_pe.cnt

        def row_to_cols(dst_cols):
            pe_now = s_pe.cnt

            def v_ev(v):
                v.wait_ge(s_pe.h, pe_now)
                for nb in range(4):
                    s_dve.inc(v.tensor_copy(ROW[0:1, nb * 512:(nb + 1) * 512], ps[nb][0:1, 0:512]))
            k.v(v_ev)
            dv = s_dve.cnt

            def t_r2c(t):
                t.wait_ge(s_dve.h, dv)
                for c in range(16):
                    s_pe.inc(t.matmul(ps[4][:, 32 + c:33 + c], ROW[0:1, c * 128:(c + 1) * 128],
                                      ONESROW[0:1, 0:1], start=True, stop=True))
            k.t(t_r2c)
            pe2 = s_pe.cnt

            def v_cp(v):
                v.wait_ge(s_pe.h, pe2)
                s_dve.inc(v.tensor_copy(dst_cols[:], ps[4][:, 32:48]))
            k.v(v_cp)
            return s_dve.cnt

        def g_d1br(g):
            g.wait_ge(s_pe.h, s_pe.cnt)
            g.wait_ge(s_act.h, s_act.cnt)
            s_wsc.inc(g.dma_start(out=D1BR, in_=d1br_d))
        k.g(g_d1br)
        d1br_in = s_wsc.cnt

        gwl = {}
        peG = {}

        def g_gw(y, oc, gate_pe):
            y.wait_ge(s_pe.h, gate_pe)
            if oc == 0:
                y.wait_ge(s_cb.h, CAST_AT["gw"])
            dstbase = (oc % 2) * 4096
            s_y[oc % 2].inc(y.dma_start(
                out=W1[:, dstbase:dstbase + 2048],
                in_=scr_gw[:, oc * 2048:(oc + 1) * 2048]))

        k.y(lambda y: g_gw(y, 0, s_pe.cnt))
        gwl[0] = (s_y[0], s_y[0].cnt)
        actG0 = s_act.cnt
        for oc in range(16):
            buf = oc % 2
            if oc + 1 < 16:
                gate_pe = peG.get(oc - 1, s_pe.cnt)
                k.y(lambda y, oc=oc, gate_pe=gate_pe: g_gw(y, oc + 1, gate_pe))
                gwl[oc + 1] = (s_y[1 - buf], s_y[1 - buf].cnt)

            def t_gw(t, oc=oc, buf=buf):
                sm, cv = gwl[oc]
                t.wait_ge(sm.h, cv)
                if oc == 0:
                    t.wait_ge(s_dve.h, ln_own)
                if oc >= 1:
                    t.wait_ge(s_act.h, actG0 + 2 * oc)
                for j in range(2):
                    acc_mm(t, ps[2 * (j % 2)][:, 0:512] if False else ps[j][:, 0:512],
                           [(W1[:, buf * 4096 + c * 128:buf * 4096 + (c + 1) * 128],
                             XNL[:, c * 1024 + j * 512:c * 1024 + (j + 1) * 512])
                            for c in range(NCH)], s_pe)
            k.t(t_gw)
            peG[oc] = s_pe.cnt

            def s_sig(sc, oc=oc):
                for j in range(2):
                    sc.wait_ge(s_pe.h, peG[oc] - 1 + j)
                    s_act.inc(sc.activation(GTL[:, oc * 1024 + j * 512:oc * 1024 + (j + 1) * 512],
                                            ps[j][:, 0:512], AF.Sigmoid,
                                            bias=F1C[:, 96 + oc:96 + oc + 1]))
            k.s(s_sig)
        act_gates = s_act.cnt

        pe_after_gate = s_pe.cnt

        agg_done = mv_row(scr_aggw, 16, 0, 2048, lambda kc: THB[:, kc:kc + 1], thb0,
                          ready=CAST_AT["aggw"], pe_gate=pe_gate_pre)
        fl_t0 = row_to_cols(TC)

        def v_t0(v):
            v.wait_ge(s_dve.h, fl_t0)
            s_dve.inc(v.tensor_tensor(TH[:], TC[:], F1C[:, 0:16], ALU.add))
            v.wait_ge(s_dve.h, s_dve.cnt)
            s_dve.inc(v.tensor_copy(THB[:], TH[:]))
        k.v(v_t0)

        for step in range(4):
            def t_ts(t):
                t.wait_ge(s_dve.h, s_dve.cnt)
                s_pe.inc(t.matmul(ps[4][0:1, 0:16], ONES[:, 0:1], TH[:], start=True, stop=True))
            k.t(t_ts)
            pe_ts = s_pe.cnt

            def v_tm(v, pe_ts=pe_ts):
                v.wait_ge(s_pe.h, pe_ts)
                s_dve.inc(v.reduce_sum(ROW[0:1, 0:1], ps[4][0:1, 0:16], axis=mybir.AxisListType.X))
                v.wait_ge(s_dve.h, s_dve.cnt)
                s_dve.inc(v.tensor_scalar(ROW[0:1, 1:2], ROW[0:1, 0:1], -1.0 / D, None, ALU.mult))
            k.v(v_tm)
            dve_tm = s_dve.cnt

            def t_bm(t, dve_tm=dve_tm):
                t.wait_ge(s_dve.h, dve_tm)
                s_pe.inc(t.matmul(ps[5][:, 0:1], ONESROW[0:1, :], ROW[0:1, 1:2], start=True, stop=True))
            k.t(t_bm)
            pe_bm = s_pe.cnt

            def v_ct(v, pe_bm=pe_bm):
                v.wait_ge(s_pe.h, pe_bm)
                s_dve.inc(v.tensor_scalar(TC[:], TH[:], ps[5][:, 0:1], None, ALU.add))
            k.v(v_ct)
            dve_ct = s_dve.cnt

            def s_sq2(sc, dve_ct=dve_ct):
                sc.wait_ge(s_dve.h, dve_ct)
                s_act.inc(sc.activation(SQT[:], TC[:], AF.Square))
            k.s(s_sq2)
            act_sq2 = s_act.cnt

            def t_ss(t, act_sq2=act_sq2):
                t.wait_ge(s_act.h, act_sq2)
                s_pe.inc(t.matmul(ps[4][0:1, 16:32], BONES[:, 0:1], SQT[:], start=True, stop=True))
            k.t(t_ss)
            pe_ss = s_pe.cnt

            def v_vv(v, pe_ss=pe_ss):
                v.wait_ge(s_pe.h, pe_ss)
                s_dve.inc(v.reduce_sum(ROW[0:1, 2:3], ps[4][0:1, 16:32], axis=mybir.AxisListType.X))
            k.v(v_vv)
            dve_vv = s_dve.cnt

            def s_sr(sc, dve_vv=dve_vv):
                sc.wait_ge(s_dve.h, dve_vv)
                s_act.inc(sc.activation(ROW[0:1, 3:4], ROW[0:1, 2:3], AF.Sqrt,
                                        bias=CONSTS[0:1, 1:2], scale=1.0 / D))
            k.s(s_sr)
            act_sr = s_act.cnt

            def v_rc2(v, act_sr=act_sr):
                v.wait_ge(s_act.h, act_sr)
                s_dve.inc(v.reciprocal(ROW[0:1, 4:5], ROW[0:1, 3:4]))
            k.v(v_rc2)
            dve_rc2 = s_dve.cnt

            def t_br(t, dve_rc2=dve_rc2):
                t.wait_ge(s_dve.h, dve_rc2)
                s_pe.inc(t.matmul(ps[5][:, 1:2], ONESROW[0:1, :], ROW[0:1, 4:5], start=True, stop=True))
            k.t(t_br)
            pe_br = s_pe.cnt

            def v_tn2(v, pe_br=pe_br):
                v.wait_ge(s_pe.h, pe_br)
                s_dve.inc(v.tensor_scalar(TC[:], TC[:], ps[5][:, 1:2], None, ALU.mult))
                v.wait_ge(s_dve.h, s_dve.cnt)
                s_dve.inc(v.scalar_tensor_tensor(TN[:], TC[:], 1.0, F1C[:, 64:80], ALU.mult, ALU.mult))
                v.wait_ge(s_dve.h, s_dve.cnt)
                s_dve.inc(v.tensor_tensor(TN[:], TN[:], F1C[:, 80:96], ALU.add))
            k.v(v_tn2)
            tn_ok = s_dve.cnt

            for hhalf in range(2):
                d1_done = mv_row(scr_d1h, 16, hhalf * 2048, 2048,
                                 lambda kc: TN[:, kc:kc + 1], tn_ok, ready=CAST_AT["d1h"],
                                 bias_row=lambda nb, w, hh=hhalf: D1BR[0:1, hh * 2048 + nb * 512:hh * 2048 + nb * 512 + w])
                if hhalf == 0:
                    def s_g1(sc, d1_done=d1_done):
                        sc.wait_ge(s_pe.h, d1_done)
                        sc.wait_ge(s_wsc.h, d1br_in)
                        for nb in range(4):
                            s_act.inc(sc.activation(HROW[0:1, nb * 512:(nb + 1) * 512],
                                                    ps[nb][0:1, 0:512], AF.Gelu))
                    k.s(s_g1)
                else:
                    def s_g2(sc, d1_done=d1_done):
                        sc.wait_ge(s_pe.h, d1_done)
                        for nb in range(4):
                            s_act.inc(sc.activation(HROW[0:1, 2048 + nb * 512:2048 + (nb + 1) * 512],
                                                    ps[nb][0:1, 0:512], AF.Gelu))
                    k.s(s_g2)
            gel_done = s_act.cnt

            def t_hc(t, gel_done=gel_done):
                t.wait_ge(s_act.h, gel_done)
                for c in range(32):
                    s_pe.inc(t.matmul(ps[5][:, 32 + c:33 + c], HROW[0:1, c * 128:(c + 1) * 128],
                                      ONESROW[0:1, 0:1], start=True, stop=True))
            k.t(t_hc)
            pe_hc = s_pe.cnt

            def v_hcg(v, pe_hc=pe_hc):
                v.wait_ge(s_pe.h, pe_hc)
                s_dve.inc(v.tensor_copy(HC[:], ps[5][:, 32:64]))
            k.v(v_hcg)
            hc_ok = s_dve.cnt

            mv_row(scr_d2h, 32, 0, 2048, lambda kc: HC[:, kc:kc + 1], hc_ok,
                   ready=CAST_AT["d2h"])
            fl_pt = row_to_cols(BCT)

            def v_pt(v, fl_pt=fl_pt):
                v.wait_ge(s_dve.h, fl_pt)
                s_dve.inc(v.engine_nop())
            k.v(v_pt)
            pt_ok = s_dve.cnt

            emit_send([(lambda slot, step=step: SH_C[slot, :, 16 * (step + 1):16 * (step + 2)], BCT[:])],
                      [(s_dve, pt_ok)])
            sigp = gstate["sig"]
            pin = emit_recv(sigp, [(lambda slot, step=step: SH_C[slot, :, 16 * (step + 1):16 * (step + 2)], TC[:])])

            def v_upd(v, pin=pin):
                v.wait_ge(s_sh.h, pin)
                s_dve.inc(v.tensor_tensor(TC[:], TC[:], BCT[:], ALU.add))
                v.wait_ge(s_dve.h, s_dve.cnt)
                s_dve.inc(v.tensor_tensor(TC[:], TC[:], F1C[:, 16:32], ALU.add))
                v.wait_ge(s_dve.h, s_dve.cnt)
                s_dve.inc(v.tensor_tensor(TH[:], TH[:], TC[:], ALU.add))
                v.wait_ge(s_dve.h, s_dve.cnt)
                s_dve.inc(v.tensor_copy(THB[:], TH[:]))
            k.v(v_upd)

        mv_row(scr_bcw, 16, 0, 2048, lambda kc: THB[:, kc:kc + 1], s_dve.cnt,
               ready=CAST_AT["bcw"])
        fl_bc = row_to_cols(BCT)

        def v_bcf(v):
            v.wait_ge(s_dve.h, fl_bc)
            s_dve.inc(v.tensor_tensor(BCT[:], BCT[:], F1C[:, 112:128], ALU.add))
        k.v(v_bcf)
        bc_ok = s_dve.cnt


        def v_ot(v):
            v.wait_ge(s_act.h, act_gates)
            v.wait_ge(s_dve.h, bc_ok)
            for oc in range(16):
                v.wait_ge(s_dve.h, s_dve.cnt)
                s_dve.inc(v.scalar_tensor_tensor(
                    XNL[:, oc * 1024:(oc + 1) * 1024], GTL[:, oc * 1024:(oc + 1) * 1024],
                    BCT[:, oc:oc + 1], XNL[:, oc * 1024:(oc + 1) * 1024], ALU.mult, ALU.add))
        k.v(v_ot)
        ot_done = s_dve.cnt

        for st in range(2):
            stw = 512

            def t_g2(t):
                t.wait_ge(s_dve.h, ot_done)
            k.t(t_g2)

            class RowPair2:
                def __getitem__(self, idx):
                    part, free = idx
                    lo, hi = free.start, free.stop
                    if hi <= stw:
                        return ps[0][part, lo:hi]
                    return ps[2][part, lo - stw:hi - stw]
            Xc = [XNL[:, c * 1024 + st * stw:c * 1024 + (st + 1) * stw] for c in range(NCH)]
            XNc2 = [GTL[:, c * 1024 + st * stw:c * 1024 + (st + 1) * stw] for c in range(NCH)]
            emit_ln(k, D, CONSTS[0:1, 1:2], Xc, XNc2, stw, RowPair2(), ps[1], SQ, ROW,
                    BONES, BONES, ONESROW, s_pe, s_act, s_dve)
        lnf = s_dve.cnt

        def v_app(v):
            v.wait_ge(s_dve.h, lnf)
            for c in range(NCH):
                for j in range(2):
                    v.wait_ge(s_dve.h, s_dve.cnt)
                    s_dve.inc(v.tensor_scalar(TMPZ[:], GTL[:, c * 1024 + j * 512:c * 1024 + (j + 1) * 512],
                                              G2B2[:, c:c + 1], G2B2[:, 16 + c:16 + c + 1],
                                              ALU.mult, ALU.add))
                    v.wait_ge(s_dve.h, s_dve.cnt)
                    s_dve.inc(v.tensor_tensor(XR[:, c * NT + j * 512:c * NT + (j + 1) * 512],
                                              XR[:, c * NT + j * 512:c * NT + (j + 1) * 512],
                                              TMPZ[:], ALU.add))
        k.v(v_app)

    def emit_ssd():
        for st in range(2):
            stw = 512
            Xc = [XR[:, c * NT + st * stw:c * NT + (st + 1) * stw] for c in range(NCH)]
            XNc = [A2[:, c * stw:(c + 1) * stw] for c in range(NCH)]

            def t_g0(t):
                bar3(t)
                t.wait_ge(s_sh.h, s_sh.cnt)
                t.wait_ge(s_fl.h, s_fl.cnt)
            k.t(t_g0)

            class RowPair:
                def __getitem__(self, idx):
                    part, free = idx
                    lo, hi = free.start, free.stop
                    if hi <= stw:
                        return ps[0][part, lo:hi]
                    return ps[2][part, lo - stw:hi - stw]
            ln_done = emit_ln(k, D, CONSTS[0:1, 0:1], Xc, XNc, stw, RowPair(), ps[1], SQ, ROW,
                              ONES, BONES, ONESROW, s_pe, s_act, s_dve)
            emit_send([(lambda slot, st=st, cc=c2: SH_XN[slot, cc * 128:(cc + 1) * 128,
                                                         st * stw:(st + 1) * stw],
                        A2[:, c2 * stw:(c2 + 1) * stw])
                       for c2 in range(NCH)], [(s_dve, ln_done)])
        sig_xn2 = gstate["sig"]
        xn2_ready = emit_recv(sig_xn2, [])

        def g_rz(g):
            g.wait_ge(s_dve.h, s_dve.cnt)
            g.wait_ge(s_pe.h, s_pe.cnt)
            g.memset(R[:], 0.0)
            s_g.inc(g.memset(RB, 0.0))
        k.g(g_rz)
        rz_ok = s_g.cnt

        oV, oA, oE, oL = [j * 256 for j in range(4)]
        oMX, oDA = oA, oE
        oDT, oACS = oV, oL
        CH2 = 256

        yw_cnt = 0
        xnf_free = [s_pe.cnt, s_pe.cnt]
        w1_free = [s_pe.cnt]
        w2_free = [s_pe.cnt]

        def g_wx(y, ch, buf):
            wd_ = 128 if ch < 18 else 32
            s_y[buf].inc(y.dma_start(
                out=W1[:, buf * 4096:buf * 4096 + 16 * wd_],
                in_=scr_winx[:, ch * 2048:ch * 2048 + 16 * wd_]))

        pre_state = {}

        def emit_p1p2_loads(c2):
            slot_t = c2 // 4
            off_t = (c2 % 4) * 256
            XNF = A1[:, (c2 % 2) * 4096:(c2 % 2) * 4096 + 4096]

            def g_xn(y, c2=c2, slot_t=slot_t, off_t=off_t, XNF=XNF):
                y.wait_ge(s_pe.h, xnf_free[c2 % 2])
                if c2 == 0:
                    y.wait_ge(s_sh.h, xn2_ready)
                    y.wait_ge(s_g.h, rz_ok)
                for c in range(NCH):
                    s_xs.inc(y.dma_start(out=XNF[:, c * 256:(c + 1) * 256],
                                         in_=SH_XN[slot_t, c * 128:(c + 1) * 128, off_t:off_t + 256]))
            k.y(g_xn)
            wload = {}

            def g_wx0(y, c2=c2):
                y.wait_ge(s_pe.h, w1_free[0])
                if c2 == 0:
                    y.wait_ge(s_cb.h, CAST_AT["winx"])
                g_wx(y, 0, 0)
            k.y(g_wx0)
            wload[0] = (s_y[0], s_y[0].cnt)
            k.y(lambda y: g_wx(y, 1, 1))
            wload[1] = (s_y[1], s_y[1].cnt)
            pre_state[c2] = dict(xn_in=s_xs.cnt, wload=wload, XNF=XNF)

        emit_p1p2_loads(0)
        for c2 in range(8):
            st_pre = pre_state.pop(c2)
            XNF = st_pre["XNF"]
            xn_in = st_pre["xn_in"]
            wload = st_pre["wload"]

            peP2 = s_pe.cnt
            dveP2 = s_dve.cnt
            for ch in range(19):
                buf = ch % 2
                wd_ = 128 if ch < 18 else 32
                if 1 <= ch < 18:
                    def g_wn(y, ch=ch, buf=buf):
                        y.wait_ge(s_pe.h, peP2 + ch)
                        g_wx(y, ch + 1, 1 - buf)
                    k.y(g_wn)
                    wload[ch + 1] = (s_y[1 - buf], s_y[1 - buf].cnt)

                def t_px(t, ch=ch, buf=buf, wd_=wd_):
                    sm, cv = wload[ch]
                    t.wait_ge(sm.h, cv)
                    if ch == 0:
                        t.wait_ge(s_xs.h, xn_in)
                        t.wait_ge(s_act.h, s_act.cnt)
                        t.wait_ge(s_dve.h, s_dve.cnt)
                    if ch >= 2:
                        t.wait_ge(s_dve.h, dveP2 + ch - 1)
                    acc_mm(t, ps[3 + buf][0:wd_, 0:CH2],
                           [(W1[:, buf * 4096 + c * wd_:buf * 4096 + (c + 1) * wd_],
                             XNF[:, c * 256:(c + 1) * 256]) for c in range(NCH)], s_pe)
                k.t(t_px)
                pe_ch = s_pe.cnt

                def v_px(v, ch=ch, buf=buf, pe_ch=pe_ch):
                    v.wait_ge(s_pe.h, pe_ch)
                    if ch < 18:
                        s_dve.inc(v.tensor_copy(PREB[:, ch * 259 + 3:(ch + 1) * 259], ps[3 + buf][:, 0:CH2]))
                    else:
                        s_dve.inc(v.tensor_copy(DTJ[:, oV:oV + 256], ps[3 + buf][0:32, 0:CH2]))
                k.v(v_px)

            w1_free[0] = s_pe.cnt
            if c2 + 1 < 8:
                emit_p1p2_loads(c2 + 1)

            def v_halo(v, c2=c2):
                v.wait_ge(s_dve.h, s_dve.cnt)
                if c2 == 0:
                    s_dve.inc(v.memset(PREB.rearrange("p (c w) -> p c w", c=18)[:, :, 0:3], 0.0))
                else:
                    s_dve.inc(v.engine_nop())
            k.v(v_halo)

            dve_c3 = s_dve.cnt
            act_c3 = s_act.cnt

            def v_conv(v):
                v.wait_ge(s_dve.h, dve_c3)
                for ch in range(18):
                    base = ch * 259
                    cb2 = (ch % 2) * 256
                    if ch >= 2:
                        v.wait_ge(s_act.h, act_c3 + ch - 1)
                    s_dve.inc(v.tensor_scalar(CONVT[:, cb2:cb2 + 256], PREB[:, base:base + 256],
                                              CWK[:, ch:ch + 1], None, ALU.mult))
                    for kk in range(1, 4):
                        v.wait_ge(s_dve.h, s_dve.cnt)
                        s_dve.inc(v.scalar_tensor_tensor(CONVT[:, cb2:cb2 + 256],
                                                         PREB[:, base + kk:base + kk + 256],
                                                         CWK[:, kk * 18 + ch:kk * 18 + ch + 1],
                                                         CONVT[:, cb2:cb2 + 256], ALU.mult, ALU.add))
            k.v(v_conv)

            def s_conv(s):
                for ch in range(18):
                    cb2 = (ch % 2) * 256
                    s.wait_ge(s_dve.h, dve_c3 + 4 * (ch + 1))
                    s_act.inc(s.activation(XBC[:, ch * CH2:(ch + 1) * CH2], CONVT[:, cb2:cb2 + 256],
                                           AF.Silu, bias=CBc[:, ch:ch + 1]))
            k.s(s_conv)
            conv_done = s_act.cnt

            def v_halo2(v):
                v.wait_ge(s_act.h, conv_done)
                s_dve.inc(v.tensor_copy(
                    PREB.rearrange("p (c w) -> p c w", c=18)[:, :, 0:3],
                    PREB.rearrange("p (c w) -> p c w", c=18)[:, :, 256:259]))
            k.v(v_halo2)

            def s_dt1(s):
                bar3(s)
                s_act.inc(s.activation(DTJ[0:32, oV:oV + 256], DTJ[0:32, oV:oV + 256],
                                       AF.Identity, bias=AUXC[:, 0:1]))
                s.wait_ge(s_act.h, s_act.cnt)
                s_act.inc(s.activation(DTJ[0:32, oA:oA + 256], DTJ[0:32, oV:oV + 256], AF.Abs))
                s.wait_ge(s_act.h, s_act.cnt)
                s_act.inc(s.activation(DTJ[0:32, oE:oE + 256], DTJ[0:32, oA:oA + 256],
                                       AF.Exp, scale=-1.0))
                s.wait_ge(s_act.h, s_act.cnt)
                s_act.inc(s.activation(DTJ[0:32, oL:oL + 256], DTJ[0:32, oE:oE + 256],
                                       AF.Ln, bias=CONSTS[0:32, 3:4]))
            k.s(s_dt1)
            dt1 = s_act.cnt

            def v_dt2(v):
                v.wait_ge(s_act.h, dt1)
                s_dve.inc(v.tensor_scalar(DTJ[0:32, oMX:oMX + 256], DTJ[0:32, oV:oV + 256],
                                          0.0, None, ALU.max))
                v.wait_ge(s_dve.h, s_dve.cnt)
                s_dve.inc(v.tensor_tensor(DTJ[0:32, oDT:oDT + 256], DTJ[0:32, oMX:oMX + 256],
                                          DTJ[0:32, oL:oL + 256], ALU.add))
                v.wait_ge(s_dve.h, s_dve.cnt)
                s_dve.inc(v.tensor_scalar(DTJ[0:32, oDT:oDT + 256], DTJ[0:32, oDT:oDT + 256],
                                          100.0, None, ALU.min))
                v.wait_ge(s_dve.h, s_dve.cnt)
                s_dve.inc(v.tensor_scalar(DTJ[0:32, oDA:oDA + 256], DTJ[0:32, oDT:oDT + 256],
                                          ACOL[:, 0:1], None, ALU.mult))
                v.wait_ge(s_dve.h, s_dve.cnt)
                ones_ap = ONES[0:32, 0:1].unsqueeze(2).broadcast_to([32, 1, 256]).squeeze(1)
                s_dve.inc(v.tensor_tensor_scan(DTJ[0:32, oACS:oACS + 256], ones_ap,
                                               DTJ[0:32, oDA:oDA + 256], 0.0, ALU.mult, ALU.add))
                v.wait_ge(s_dve.h, s_dve.cnt)
                s_dve.inc(v.tensor_scalar(NACS[:], DTJ[0:32, oACS:oACS + 256], -1.0, None, ALU.mult))
                v.wait_ge(s_dve.h, s_dve.cnt)
                s_dve.inc(v.tensor_scalar(DECF[:], DTJ[0:32, oACS:oACS + 256],
                                          DTJ[0:32, oACS + 255:oACS + 256], -1.0,
                                          ALU.subtract, ALU.mult))
            k.v(v_dt2)
            dt2 = s_dve.cnt

            def s_dt3(s):
                s.wait_ge(s_dve.h, dt2)
                s_act.inc(s.activation(DECF[:], DECF[:], AF.Exp))
            k.s(s_dt3)
            dt3 = s_act.cnt

            def v_dt4(v):
                v.wait_ge(s_act.h, dt3)
                s_dve.inc(v.tensor_tensor(DTD[:], DTJ[0:32, oDT:oDT + 256], DECF[:], ALU.mult))
            k.v(v_dt4)

            zld = {}

            def g_z(y, zb, buf):
                s_y[buf].inc(y.dma_start(
                    out=W2[:, buf * 2048:buf * 2048 + 2048],
                    in_=scr_winz[:, zb * 2048:(zb + 1) * 2048]))

            def g_z0(y):
                y.wait_ge(s_pe.h, w2_free[0])
                y.wait_ge(s_cb.h, CAST_AT["winz"])
                g_z(y, 0, 0)
            k.y(g_z0)
            zld[0] = (s_y[0], s_y[0].cnt)
            peZ = s_pe.cnt
            actZ = s_act.cnt
            for zb in range(16):
                buf = zb % 2
                if zb + 1 < 16:
                    def g_zn(y, zb=zb, buf=buf):
                        if zb >= 1:
                            y.wait_ge(s_pe.h, peZ + 2 * zb)
                        g_z(y, zb + 1, 1 - buf)
                    k.y(g_zn)
                    zld[zb + 1] = (s_y[1 - buf], s_y[1 - buf].cnt)

                def t_z(t, zb=zb, buf=buf):
                    sm, cv = zld[zb]
                    t.wait_ge(sm.h, cv)
                    if zb == 0:
                        t.wait_ge(s_dve.h, dveP2 + 19)
                    if zb >= 1:
                        t.wait_ge(s_act.h, actZ + 2 * zb)
                    for tt in range(2):
                        acc_mm(t, ps[2 * buf + tt][:, 0:128],
                               [(XNF[:, c * 256 + tt * 128:c * 256 + (tt + 1) * 128],
                                 W2[:, buf * 2048 + c * 128:buf * 2048 + (c + 1) * 128])
                                for c in range(NCH)], s_pe)
                k.t(t_z)
                pe_z = s_pe.cnt

                def s_z(s, zb=zb, buf=buf, pe_z=pe_z):
                    for tt in range(2):
                        s.wait_ge(s_pe.h, pe_z - 1 + tt)
                        s_act.inc(s.activation(ZS_S[:, tt * 2048 + zb * 128:tt * 2048 + (zb + 1) * 128],
                                               ps[2 * buf + tt][:, 0:128], AF.Silu))
                k.s(s_z)

            w2_free[0] = s_pe.cnt
            xnf_free[c2 % 2] = s_pe.cnt

            def t_tr(t):
                bar3(t)
                for lt in range(2):
                    t.transpose(ps[5][:, lt * 32:(lt + 1) * 32],
                                DTJ[0:32, oDT + lt * 128:oDT + (lt + 1) * 128], IDF[0:32, 0:32])
                    t.transpose(ps[5][:, 64 + lt * 32:64 + (lt + 1) * 32],
                                DTD[:, lt * 128:(lt + 1) * 128], IDF[0:32, 0:32])
                    s_pe.inc(t.transpose(ps[4][:, lt * 32:(lt + 1) * 32],
                                         DTJ[0:32, oACS + lt * 128:oACS + (lt + 1) * 128], IDF[0:32, 0:32]))
                s_pe.inc(t.transpose(ps[4][0:1, 64:96], DTJ[0:32, oACS + 255:oACS + 256], IDF[0:32, 0:32]))
            k.t(t_tr)
            pe_tr1 = s_pe.cnt

            def v_tr1(v):
                v.wait_ge(s_pe.h, pe_tr1)
                s_dve.inc(v.tensor_copy(DTT[:, 0:64], ps[5][:, 0:64]))
                s_dve.inc(v.tensor_copy(DTDT[:, 0:64], ps[5][:, 64:128]))
            k.v(v_tr1)

            def s_tr1(s):
                s.wait_ge(s_pe.h, pe_tr1)
                s_act.inc(s.activation(EXAT[:, 0:64], ps[4][:, 0:64], AF.Exp))
                s_act.inc(s.activation(ECER[:], ps[4][0:1, 64:96], AF.Exp))
            k.s(s_tr1)
            ece1 = s_act.cnt

            def t_ecex(t):
                t.wait_ge(s_act.h, ece1)
                s_pe.inc(t.matmul(ps[4][:, 96:128], ONESROW[0:1, :], ECER[:], start=True, stop=True))
            k.t(t_ecex)
            pe_ecex = s_pe.cnt

            def s_ecex(s):
                s.wait_ge(s_pe.h, pe_ecex)
                s_act.inc(s.copy(ECEX[:], ps[4][:, 96:128]))
            k.s(s_ecex)

            dve_xt = s_dve.cnt
            pe_xt = s_pe.cnt

            def t_xt(t):
                t.wait_ge(s_act.h, s_act.cnt)
                t.wait_ge(s_dve.h, s_dve.cnt)
                n = 0
                for ch in range(17):
                    for lt in range(2):
                        bank = psta if n % 2 == 0 else pstb
                        if n >= 2:
                            t.wait_ge(s_dve.h, dve_xt + n - 1)
                        s_pe.inc(t.transpose(bank[:], XBC[:, ch * CH2 + lt * 128:ch * CH2 + (lt + 1) * 128], IDB[:]))
                        n += 1
            k.t(t_xt)

            def v_xt(v):
                n = 0
                for ch in range(17):
                    for lt in range(2):
                        bank = psta if n % 2 == 0 else pstb
                        v.wait_ge(s_pe.h, pe_xt + n + 1)
                        if ch < 16:
                            dst = XT_S[:, lt * 2048 + ch * 128:lt * 2048 + (ch + 1) * 128]
                        else:
                            dst = BTOK[:, lt * 128:(lt + 1) * 128]
                        s_dve.inc(v.tensor_copy(dst, bank[:]))
                        n += 1
            k.v(v_xt)

            def t_cbt(t):
                t.wait_ge(s_dve.h, s_dve.cnt)
                B_ = XBC[:, 16 * CH2:17 * CH2]
                C_ = XBC[:, 17 * CH2:18 * CH2]
                t.matmul(ps[5][:, 0:128], B_[:, 0:128], C_[:, 0:128], start=True, stop=True)
                t.matmul(ps[5][:, 128:256], B_[:, 0:128], C_[:, 128:256], start=True, stop=True)
                s_pe.inc(t.matmul(ps[5][:, 256:384], B_[:, 128:256], C_[:, 128:256], start=True, stop=True))
            k.t(t_cbt)
            pe_cbt = s_pe.cnt

            def v_cbt(v):
                v.wait_ge(s_pe.h, pe_cbt)
                s_dve.inc(v.tensor_copy(CBTS, ps[5][:, 0:384]))
            k.v(v_cbt)

            def seg_cols(j):
                return slice(j * 128, (j + 1) * 128)

            def t_p7gate(t):
                t.wait_ge(s_dve.h, s_dve.cnt)
            k.t(t_p7gate)
            p7_dve = s_dve.cnt
            exp_cnt = {}
            m1_cnt = {}
            ymm_cnt = {}
            flat_of = {}
            for fg in range(8):
                last_seg_pe = s_pe.cnt

                def g_flat(g, fg=fg, last_seg_pe=last_seg_pe):
                    g.wait_ge(s_dve.h, p7_dve)
                    if fg > 0:
                        g.wait_ge(s_pe.h, last_seg_pe)
                    s_fl.inc(g.dma_start(out=ACSF[:].rearrange("o (h l) -> o h l", h=4),
                                         in_=DTJ[fg * 4:(fg + 1) * 4, oACS:oACS + 256].unsqueeze(1)))
                    s_fl.inc(g.dma_start(out=NACSF[:].rearrange("o (h l) -> o h l", h=4),
                                         in_=NACS[fg * 4:(fg + 1) * 4, :].unsqueeze(1)))
                k.g(g_flat)
                flat_of[fg] = s_fl.cnt

                for hh4 in range(4):
                    h = fg * 4 + hh4
                    hg8 = h // 8
                    hh8 = h % 8
                    hb = hh4 * 256
                    segbank = ps[4] if h % 2 == 0 else ps[5]

                    def t_seg(t, h=h, hh4=hh4, hb=hb, segbank=segbank, fg=fg):
                        if hh4 == 0:
                            t.wait_ge(s_fl.h, flat_of[fg])
                        if h >= 2:
                            t.wait_ge(s_act.h, exp_cnt[h - 2])
                        for j, (sh, lh) in enumerate(((0, 0), (0, 1), (1, 1))):
                            t.matmul(segbank[:, seg_cols(j)],
                                     NACSF[0:1, hb + sh * 128:hb + (sh + 1) * 128],
                                     ONESROW[0:1, :], start=True, stop=False)
                            ins = t.matmul(segbank[:, seg_cols(j)], ONESROW[0:1, :],
                                           ACSF[0:1, hb + lh * 128:hb + (lh + 1) * 128],
                                           start=False, stop=(sh != lh))
                            if sh == lh:
                                ins = t.matmul(segbank[:, seg_cols(j)], IDB[:], MASKSEG[:],
                                               start=False, stop=True)
                        s_pe.inc(ins)
                    k.t(t_seg)
                    pe_seg = s_pe.cnt

                    def s_exp(s, h=h, segbank=segbank, pe_seg=pe_seg):
                        s.wait_ge(s_pe.h, pe_seg)
                        if h >= 1:
                            s.wait_ge(s_dve.h, m1_cnt[h - 1])
                        s_act.inc(s.activation(LSEG, segbank[:, 0:384], AF.Exp))
                    k.s(s_exp)
                    exp_cnt[h] = s_act.cnt

                    def v_m1(v, h=h):
                        v.wait_ge(s_act.h, exp_cnt[h])
                        if h >= 1:
                            v.wait_ge(s_pe.h, ymm_cnt[h - 1])
                        s_dve.inc(v.scalar_tensor_tensor(M1S[:, 0:256], LSEG[:, 0:256],
                                                         DTT[:, h:h + 1], CBTS[:, 0:256],
                                                         ALU.mult, ALU.mult))
                        v.wait_ge(s_dve.h, s_dve.cnt)
                        s_dve.inc(v.scalar_tensor_tensor(M1S[:, 256:384], LSEG[:, 256:384],
                                                         DTT[:, 32 + h:33 + h], CBTS[:, 256:384],
                                                         ALU.mult, ALU.mult))
                    k.v(v_m1)
                    m1_cnt[h] = s_dve.cnt

                    def t_y(t, h=h, hg8=hg8, hh8=hh8):
                        t.wait_ge(s_dve.h, m1_cnt[h])
                        if hh8 == 0 and hg8 > 0:
                            t.wait_ge(s_dve.h, s_dve.cnt)
                        hc = slice(hh8 * 64, (hh8 + 1) * 64)
                        t.matmul(ps[0][:, hc], M1S[:, 0:128], XT_S[:, 0 * 2048 + h * 64:0 * 2048 + (h + 1) * 64],
                                 start=True, stop=True)
                        t.matmul(ps[1][:, hc], M1S[:, 128:256], XT_S[:, 0 * 2048 + h * 64:0 * 2048 + (h + 1) * 64],
                                 start=True, stop=False)
                        t.matmul(ps[1][:, hc], M1S[:, 256:384], XT_S[:, 1 * 2048 + h * 64:1 * 2048 + (h + 1) * 64],
                                 start=False, stop=True)
                        t.matmul(ps[2][:, hc], XBC[:, 17 * CH2:17 * CH2 + 128], RB[:, h * 64:(h + 1) * 64],
                                 start=True, stop=True)
                        s_pe.inc(t.matmul(ps[3][:, hc], XBC[:, 17 * CH2 + 128:18 * CH2], RB[:, h * 64:(h + 1) * 64],
                                          start=True, stop=True))
                    k.t(t_y)
                    ymm_cnt[h] = s_pe.cnt

                    if hh8 == 7:
                        pe_hg_end = s_pe.cnt

                        def v_evac(v, hg8=hg8, pe_hg_end=pe_hg_end):
                            v.wait_ge(s_pe.h, pe_hg_end)
                            for lh in range(2):
                                ydo = ps[0] if lh == 0 else ps[1]
                                yof = ps[2] if lh == 0 else ps[3]
                                exa = EXAT[:, lh * 32 + hg8 * 8:lh * 32 + (hg8 + 1) * 8]
                                exap = exa.unsqueeze(2).broadcast_to([128, 8, 64])
                                dbp = DBC[:, hg8 * 8:(hg8 + 1) * 8].unsqueeze(2).broadcast_to([128, 8, 64])
                                xsl = XT_S[:, lh * 2048 + hg8 * 512:lh * 2048 + (hg8 + 1) * 512]
                                ysl = YT_S[:, lh * 2048 + hg8 * 512:lh * 2048 + (hg8 + 1) * 512]
                                r3 = lambda a: a.rearrange("p (a b) -> p a b", a=8)
                                v.wait_ge(s_dve.h, s_dve.cnt)
                                s_dve.inc(v.tensor_tensor(r3(TMPY[:]), r3(yof[:, 0:512]), exap, ALU.mult))
                                s_dve.inc(v.tensor_tensor(r3(TMP2), r3(xsl), dbp, ALU.mult))
                                v.wait_ge(s_dve.h, s_dve.cnt)
                                s_dve.inc(v.tensor_tensor(TMPY[:], TMPY[:], ydo[:, 0:512], ALU.add))
                                v.wait_ge(s_dve.h, s_dve.cnt)
                                s_dve.inc(v.tensor_tensor(ysl, TMPY[:], TMP2, ALU.add))
                        k.v(v_evac)

            def v_xdtd(v):
                v.wait_ge(s_pe.h, s_pe.cnt)
                v.wait_ge(s_dve.h, s_dve.cnt)
                for lt in range(2):
                    dtp = DTDT[:, lt * 32:(lt + 1) * 32].unsqueeze(2).broadcast_to([128, 32, 64])
                    v.wait_ge(s_dve.h, s_dve.cnt)
                    s_dve.inc(v.tensor_tensor(
                        XT_S[:, lt * 2048:(lt + 1) * 2048].rearrange("p (a b) -> p a b", a=32),
                        XT_S[:, lt * 2048:(lt + 1) * 2048].rearrange("p (a b) -> p a b", a=32),
                        dtp, ALU.mult))
            k.v(v_xdtd)

            for hg in range(4):
                def t_sc(t, hg=hg):
                    t.wait_ge(s_dve.h, s_dve.cnt)
                    for hh in range(8):
                        h = hg * 8 + hh
                        t.matmul(ps[4][:, hh * 64:(hh + 1) * 64], BTOK[:, 0:128],
                                 XT_S[:, 0 * 2048 + h * 64:0 * 2048 + (h + 1) * 64], start=True, stop=False)
                        s_pe.inc(t.matmul(ps[4][:, hh * 64:(hh + 1) * 64], BTOK[:, 128:256],
                                          XT_S[:, 1 * 2048 + h * 64:1 * 2048 + (h + 1) * 64],
                                          start=False, stop=True))
                k.t(t_sc)
                pe_sc = s_pe.cnt

                def v_r(v, hg=hg, pe_sc=pe_sc):
                    v.wait_ge(s_pe.h, pe_sc)
                    v.wait_ge(s_act.h, s_act.cnt)
                    ecp = ECEX[:, hg * 8:(hg + 1) * 8].unsqueeze(2).broadcast_to([128, 8, 64])
                    rsl = R[:, hg * 512:(hg + 1) * 512]
                    s_dve.inc(v.tensor_tensor(rsl.rearrange("p (a b) -> p a b", a=8),
                                              rsl.rearrange("p (a b) -> p a b", a=8), ecp, ALU.mult))
                    v.wait_ge(s_dve.h, s_dve.cnt)
                    s_dve.inc(v.tensor_tensor(rsl, rsl, ps[4][:, 0:512], ALU.add))
                k.v(v_r)

            def v_rb(v):
                v.wait_ge(s_dve.h, s_dve.cnt)
                s_dve.inc(v.tensor_copy(RB, R[:]))
            k.v(v_rb)

            g1base = s_dve.cnt

            def v_gate(v):
                v.wait_ge(s_pe.h, s_pe.cnt)
                v.wait_ge(s_act.h, s_act.cnt)
                for lt in range(2):
                    ysl = YT_S[:, lt * 2048:(lt + 1) * 2048]
                    v.wait_ge(s_dve.h, s_dve.cnt)
                    s_dve.inc(v.tensor_tensor(ysl, ysl, ZS_S[:, lt * 2048:(lt + 1) * 2048], ALU.mult))
            k.v(v_gate)
            g1 = s_dve.cnt

            def s_var(s, c2=c2):
                s.wait_ge(s_dve.h, g1)
                for lt in range(2):
                    s_act.inc(s.activation(ZS_S[:, lt * 2048:(lt + 1) * 2048],
                                           YT_S[:, lt * 2048:(lt + 1) * 2048], AF.Square,
                                           accum_out=VARP[:, c2 * 2 + lt:c2 * 2 + lt + 1]))
            k.s(s_var)
            var_ok = s_act.cnt

            pe_yt = s_pe.cnt
            dve_yt = s_dve.cnt

            def t_ytr(t, var_ok=var_ok):
                t.wait_ge(s_act.h, var_ok)
                n = 0
                for fc in range(16):
                    for lt in range(2):
                        bank = psta if n % 2 == 0 else pstb
                        if n >= 2:
                            t.wait_ge(s_dve.h, dve_yt + n - 1)
                        s_pe.inc(t.transpose(bank[:], YT_S[:, lt * 2048 + fc * 128:lt * 2048 + (fc + 1) * 128], IDB[:]))
                        n += 1
            k.t(t_ytr)

            def v_ytr(v):
                n = 0
                for fc in range(16):
                    for lt in range(2):
                        bank = psta if n % 2 == 0 else pstb
                        v.wait_ge(s_pe.h, pe_yt + n + 1)
                        s_dve.inc(v.tensor_scalar(YNFC[:, fc * 256 + lt * 128:fc * 256 + (lt + 1) * 128],
                                                  bank[:], NWC[:, fc:fc + 1], None, ALU.mult))
                        n += 1
            k.v(v_ytr)
            ynfc_ok = s_dve.cnt

            def g_yw(g, c2=c2, ynfc_ok=ynfc_ok):
                g.wait_ge(s_dve.h, ynfc_ok)

                def wr(my, g):
                    for fc in range(16):
                        s_sh.inc(g.dma_start(
                            out=SH_Y[my, fc * 128:(fc + 1) * 128, c2 * 256:(c2 + 1) * 256],
                            in_=YNFC[:, fc * 256:(fc + 1) * 256]))
                branch_pair(g, gstate["me"], wr)
            k.g(g_yw)

        def v_vp(v):
            v.wait_ge(s_act.h, s_act.cnt)
            s_dve.inc(v.engine_nop())
        k.v(v_vp)
        vp_ok = s_dve.cnt
        emit_send([(lambda slot: SH_C[slot, :, 144:160], VARP[:, 0:16])], [(s_dve, vp_ok)])
        sig_yf = gstate["sig"]

        def g_yr(g):
            emit_poll(g, sig_yf)
            def rd1(my, g):
                s_sh.inc(g.dma_start(out=BCT[:], in_=SH_C[my, :, 144:160]))
            branch_pair(g, gstate["pr"], rd1)

            def rd2(my, g):
                for c in range(32):
                    s_sh.inc(g.dma_start(
                        out=A1[:, c * 1024:(c + 1) * 1024],
                        in_=SH_Y[c // 16, (c % 16) * 128:(c % 16 + 1) * 128,
                                 my * 1024:(my + 1) * 1024]))
            branch_pair(g, gstate["me"], rd2)
        k.g(g_yr)
        yr_in = s_sh.cnt

        def v_var2(v):
            v.wait_ge(s_sh.h, yr_in)
            s_dve.inc(v.tensor_tensor(VARP[:, 0:16], VARP[:, 0:16], BCT[:], ALU.add))
        k.v(v_var2)
        var2 = s_dve.cnt

        def g_vflat(g):
            g.wait_ge(s_dve.h, var2)
            def vf(my, g):
                s_fl.inc(g.dma_start(
                    out=ROW[0:1, 0:1024].rearrange("o (p c) -> o p c", p=128),
                    in_=VARP[:, my * 8:(my + 1) * 8].unsqueeze(1)))
            branch_pair(g, gstate["me"], vf)
        k.g(g_vflat)
        vf_in = s_fl.cnt

        def s_rstd(s):
            s.wait_ge(s_fl.h, vf_in)
            s_act.inc(s.activation(ROW[0:1, 1024:2048], ROW[0:1, 0:1024], AF.Sqrt,
                                   bias=CONSTS[0:1, 0:1], scale=1.0 / 4096.0))
        k.s(s_rstd)
        rstd1 = s_act.cnt

        def v_rstd(v):
            v.wait_ge(s_act.h, rstd1)
            s_dve.inc(v.reciprocal(ROW[0:1, 0:1024], ROW[0:1, 1024:2048]))
        k.v(v_rstd)
        rstd2 = s_dve.cnt

        def t_rbc(t):
            t.wait_ge(s_dve.h, rstd2)
            ins = None
            for c in range(8):
                ins = t.matmul(ps[4 + c // 4][:, (c % 4) * 128:(c % 4 + 1) * 128],
                               ONESROW[0:1, :], ROW[0:1, c:c + 1017:8], start=True, stop=True)
            s_pe.inc(ins)
        k.t(t_rbc)
        pe_rbc = s_pe.cnt

        def v_nrm(v):
            v.wait_ge(s_pe.h, pe_rbc)
            for c in range(32):
                for j in range(2):
                    v.wait_ge(s_dve.h, s_dve.cnt)
                    s_dve.inc(v.tensor_tensor(A1[:, c * 1024 + j * 512:c * 1024 + (j + 1) * 512],
                                              A1[:, c * 1024 + j * 512:c * 1024 + (j + 1) * 512],
                                              ps[4 + j][:, 0:512], ALU.mult))
        k.v(v_nrm)
        nrm_ok = s_dve.cnt

        for ocg in range(8):
            def g_wo2(y, ocg=ocg):
                y.wait_ge(s_pe.h, s_pe.cnt)
                if ocg == 0:
                    y.wait_ge(s_cb.h, CAST_AT["wout"])
                for kg in range(4):
                    s_wa[0].inc(y.dma_start(
                        out=W1[:, kg * 2048:(kg + 1) * 2048],
                        in_=scr_wout[:, ocg * 8192 + kg * 2048:ocg * 8192 + (kg + 1) * 2048]))
            k.y(g_wo2)
            wo2_in = s_wa[0].cnt

            for tt in range(2):
                def t_wo2(t, ocg=ocg, tt=tt, wo2_in=wo2_in):
                    t.wait_ge(s_wa[0].h, wo2_in)
                    if ocg == 0 and tt == 0:
                        t.wait_ge(s_dve.h, nrm_ok)
                    else:
                        t.wait_ge(s_dve.h, s_dve.cnt)
                    for ocl in range(2):
                        acc_mm(t, ps[ocl][:, 0:512],
                               [(W1[:, (c // 8) * 2048 + (c % 8) * 256 + ocl * 128:
                                     (c // 8) * 2048 + (c % 8) * 256 + (ocl + 1) * 128],
                                 A1[:, c * 1024 + tt * 512:c * 1024 + (tt + 1) * 512])
                                for c in range(32)], s_pe)
                k.t(t_wo2)
                pe_wo2 = s_pe.cnt

                def v_wo2(v, ocg=ocg, tt=tt, pe_wo2=pe_wo2):
                    v.wait_ge(s_pe.h, pe_wo2)
                    for ocl in range(2):
                        c = ocg * 2 + ocl
                        s_dve.inc(v.tensor_tensor(
                            XR[:, c * NT + tt * 512:c * NT + (tt + 1) * 512],
                            XR[:, c * NT + tt * 512:c * NT + (tt + 1) * 512],
                            ps[ocl][:, 0:512], ALU.add))
                k.v(v_wo2)

    CAST_AT = {}

    def emit_cast_front():
        def g_cast(g):
            for name, src, dst in (("wgA", wg_d[:, 0:65536], scr_wg[:, 0:65536]),
                                   ("wuA", wu_d[:, 0:65536], scr_wu[:, 0:65536]),
                                   ("wgB", wg_d[:, 65536:131072], scr_wg[:, 65536:131072]),
                                   ("wuB", wu_d[:, 65536:131072], scr_wu[:, 65536:131072]),
                                   ("wd", wd_d, scr_wd), ("wqkv", wqkv_d, scr_wqkv),
                                   ("wo", wo_d, scr_wo), ("aggw", aggw_d, scr_aggw),
                                   ("d1h", d1h_d, scr_d1h), ("bcw", bcw_d, scr_bcw)):
                g.wait_ge(s_cb.h, s_cb.cnt)
                s_cb.inc(g.dma_start(out=dst, in_=src))
                CAST_AT[name] = s_cb.cnt
        k.g(g_cast)

    _gap_pieces = []
    for name, src, dst, rows in (("d2h", d2h_d, scr_d2h, 4096), ("gw", gw_d, scr_gw, 128),
                                 ("winx", winx_d, scr_winx, 128), ("winz", winz_d, scr_winz, 128),
                                 ("wout", wout_d, scr_wout, 128)):
        if rows == 128:
            w = src.shape[1]
            for j in range(0, w, 16384):
                ww = min(16384, w - j)
                _gap_pieces.append((name, src[:, j:j + ww], dst[:, j:j + ww]))
        else:
            for r in range(0, rows, 1024):
                _gap_pieces.append((name, src[r:r + 1024, :], dst[r:r + 1024, :]))
    _gap_state = {"i": 0}

    def cast_gap(g, n=2):
        for _ in range(n):
            i = _gap_state["i"]
            if i >= len(_gap_pieces):
                return
            name, srcap, dstap = _gap_pieces[i]
            g.wait_ge(s_cb.h, s_cb.cnt)
            s_cb.inc(g.dma_start(out=dstap, in_=srcap))
            CAST_AT[name] = s_cb.cnt
            _gap_state["i"] = i + 1

    def cast_gap_drain():
        def g_dr(g):
            cast_gap(g, len(_gap_pieces))
        k.g(g_dr)

    nc._dbg = dict(XR=XR, A1=A1, A2=A2, A3=A3, TH=TH, THB=THB, BCT=BCT, TN=TN, HC=HC,
                   TC=TC, ROW=ROW, XNL=XNL, GTL=GTL, VARP=VARP, DTJ=DTJ)

    SUBLAYER[0] = 0
    emit_cast_front()
    emit_l0ffn(0)
    SUBLAYER[0] += 1
    if finish_if(SUBLAYER[0]):
        return nc
    emit_l0attn()
    cast_gap_drain()
    SUBLAYER[0] += 1
    if finish_if(SUBLAYER[0]):
        return nc
    emit_l0ffn(1)
    SUBLAYER[0] += 1
    if finish_if(SUBLAYER[0]):
        return nc
    emit_l1ffn()
    SUBLAYER[0] += 1
    if finish_if(SUBLAYER[0]):
        return nc
    emit_ssd()
    SUBLAYER[0] += 1
    if finish_if(SUBLAYER[0]):
        return nc
    emit_l1ffn()
    SUBLAYER[0] += 1
    if finish_if(SUBLAYER[0]):
        return nc

    def y_fin(y):
        y.wait_ge(s_pe.h, s_pe.cnt)
        y.wait_ge(s_dve.h, s_dve.cnt)
        y.wait_ge(s_act.h, s_act.cnt)
        for c in range(NCH):
            s_st.inc(y.dma_start(out=xout[c * 128:(c + 1) * 128, :], in_=XR[:, c * NT:(c + 1) * NT]))
        y.wait_ge(s_st.h, s_st.cnt)
    k.y(y_fin)

    k.st.close()
    return nc


_NONCE_BASE = [0]


def fused_inmaps(inputs, nonce=None):
    import time
    if nonce is None:
        nonce = int(time.time_ns() // 1_000_000) % (1 << 29)
    x = inputs["L0_attn_x"].astype(np.float32)
    xs = shard_fm(x)
    cat = np.concatenate
    asc = np.ascontiguousarray

    def colify(v, n=16):
        return asc(v.reshape(n, 128).T)

    sc = asc(cat([inputs["L0_ffn_s_gate"].reshape(64, 128).T, inputs["L0_ffn_s_up"].reshape(64, 128).T,
                  inputs["L0_ffn_s_down"].reshape(16, 128).T], axis=1))
    f1c = asc(cat([colify(inputs["L1_ffn_agg_b"]), colify(inputs["L1_ffn_d2_b"]),
                   colify(inputs["L1_ffn_out_gamma"]), colify(inputs["L1_ffn_out_beta"]),
                   colify(inputs["L1_ffn_pn_gamma"]), colify(inputs["L1_ffn_pn_beta"]),
                   colify(inputs["L1_ffn_gate_b"]), colify(inputs["L1_ffn_bc_b"])], axis=1))
    cw = inputs["L1_attn_conv_w"]
    cb = inputs["L1_attn_conv_b"]
    nw = inputs["L1_attn_norm_weight"]

    wgt = asc(inputs["L0_ffn_w_gate"].reshape(16, 128, 64, 128)
              .transpose(1, 2, 0, 3).reshape(128, 131072))
    wut = asc(inputs["L0_ffn_w_up"].reshape(16, 128, 64, 128)
              .transpose(1, 2, 0, 3).reshape(128, 131072))
    wdt = asc(inputs["L0_ffn_w_down"].reshape(64, 128, 4, 512)
              .transpose(1, 2, 0, 3).reshape(128, 131072))
    wot = asc(inputs["L0_attn_Wo"].reshape(2, 8, 128, 4, 512)
              .transpose(2, 3, 0, 1, 4).reshape(128, 32768))
    gwt = asc(inputs["L1_ffn_gate_w"].reshape(16, 128, 16, 128)
              .transpose(1, 2, 0, 3).reshape(128, 32768))
    woutt = asc(inputs["L1_attn_W_out"].reshape(4, 8, 128, 8, 256)
                .transpose(2, 3, 0, 1, 4).reshape(128, 65536))
    maps = []
    for c in range(8):
        b, me = c // 2, c % 2
        meta = np.zeros((1, 64), np.int32)
        meta[0, 0] = me
        meta[0, 1] = 1 - me
        meta[0, 2] = nonce
        for j in range(56):
            meta[0, 8 + j] = nonce + j
        hsl = slice(me * 8, (me + 1) * 8)
        wq = inputs["L0_attn_Wq"][hsl].transpose(1, 0, 2).reshape(D, 1024)
        wk = inputs["L0_attn_Wk"][hsl].transpose(1, 0, 2).reshape(D, 1024)
        wv = inputs["L0_attn_Wv"][hsl].transpose(1, 0, 2).reshape(D, 1024)
        wqkv = asc(np.stack([wq, wk, wv]).reshape(3, 16, 128, 8, 128)
                   .transpose(2, 3, 0, 1, 4).reshape(128, 49152))
        attb = np.zeros((128, 32), np.float32)
        attb[:, 0:8] = inputs["L0_attn_bq"][hsl].T
        attb[:, 8:16] = inputs["L0_attn_bk"][hsl].T
        attb[:, 16:32] = inputs["L0_attn_bo"].reshape(16, 128).T
        bvr = asc(inputs["L0_attn_bv"][hsl].reshape(1, 1024))
        d1h = asc(inputs["L1_ffn_d1_w"][:, me * 4096:(me + 1) * 4096])
        d2h = asc(inputs["L1_ffn_d2_w"][me * 4096:(me + 1) * 4096, :])
        d1br = asc(inputs["L1_ffn_d1_b"][me * 4096:(me + 1) * 4096].reshape(1, 4096))
        win_f = inputs["L1_attn_W_in"]
        xbc = cat([win_f[:, 4096 + me * 2048:4096 + (me + 1) * 2048],
                   win_f[:, 8192:8448]], axis=1)
        winx = asc(cat([xbc.reshape(16, 128, 18, 128).transpose(1, 2, 0, 3).reshape(128, 36864),
                        win_f[:, 8448 + me * 32:8448 + (me + 1) * 32]
                        .reshape(16, 128, 32).transpose(1, 0, 2).reshape(128, 512)], axis=1))
        winz = asc(win_f[:, me * 2048:(me + 1) * 2048]
                   .reshape(16, 128, 16, 128).transpose(1, 2, 0, 3).reshape(128, 32768))
        cwk = np.zeros((128, 72), np.float32)
        cbt = np.zeros((128, 18), np.float32)
        for kk in range(4):
            cwk[:, kk * 18:kk * 18 + 16] = cw[kk, me * 2048:(me + 1) * 2048].reshape(16, 128).T
            cwk[:, kk * 18 + 16:kk * 18 + 18] = cw[kk, 4096:4352].reshape(2, 128).T
        cbt[:, 0:16] = cb[me * 2048:(me + 1) * 2048].reshape(16, 128).T
        cbt[:, 16:18] = cb[4096:4352].reshape(2, 128).T
        auxc = asc(np.stack([inputs["L1_attn_dt_bias"][me * 32:(me + 1) * 32],
                             inputs["L1_attn_A_log"][me * 32:(me + 1) * 32]], axis=1))
        drow = asc(inputs["L1_attn_D"][me * 32:(me + 1) * 32].reshape(1, 32))
        nwc = asc(nw[me * 2048:(me + 1) * 2048].reshape(16, 128).T)
        nwc2 = np.zeros((128, 32), np.float32)
        nwc2[:, 0:16] = nwc
        maps.append({
            "xres": np.asarray(xs[c]), "meta": meta,
            "wg": wgt, "wu": wut, "wd": wdt,
            "sc": sc, "wqkv": wqkv, "wo": wot, "attb": attb, "bvr": bvr,
            "aggw": inputs["L1_ffn_agg_w"], "d1h": d1h, "d2h": d2h, "d1br": d1br,
            "bcw": inputs["L1_ffn_bc_w"], "gw": gwt, "f1c": f1c,
            "winx": winx, "winz": winz, "wout": woutt, "cwk": cwk, "cb": cbt,
            "auxc": auxc, "drow": drow, "nwc": nwc2,
        })
    return maps


# version 17
# speedup vs baseline: 1.1372x; 1.1372x over previous
import numpy as np
import ml_dtypes
from contextlib import ExitStack
import concourse.bass as bass
import concourse.mybir as mybir
from concourse.bass_utils import run_bass_kernel_spmd

F32 = mybir.dt.float32
BF16 = mybir.dt.bfloat16
I8 = mybir.dt.int8
I32 = mybir.dt.int32
AF = mybir.ActivationFunctionType
ALU = mybir.AluOpType
BF = ml_dtypes.bfloat16

D = 2048
NCH = 16
ATT_C = 24.0


class Sem:
    __slots__ = ("h", "step", "cnt")

    def __init__(self, h, step):
        self.h, self.step, self.cnt = h, step, 0

    def inc(self, instr, n=1):
        instr.then_inc(self.h, self.step * n)
        self.cnt += self.step * n
        return self.cnt


class K:
    def __init__(self):
        self.nc = bass.Bass("TRN2", target_bir_lowering=False)
        self.st = ExitStack()
        self.block = None

    def sem(self, name, step=1):
        return Sem(self.st.enter_context(self.nc.semaphore(name)), step)

    def sb(self, name, p, f, dtype):
        return self.st.enter_context(self.nc.sbuf_tensor(name, [p, f], dtype))

    def ps(self, name, p, f, dtype=F32):
        return self.st.enter_context(self.nc.psum_tensor(name, [p, f], dtype))

    def begin(self):
        self.block = self.st.enter_context(self.nc.Block())

    def t(self, fn): self.block.tensor(fn)
    def v(self, fn): self.block.vector(fn)
    def s(self, fn): self.block.scalar(fn)
    def g(self, fn): self.block.gpsimd(fn)
    def y(self, fn): self.block.sync(fn)


def acc_mm(t, out, pairs, sem=None):
    n = len(pairs)
    ins = None
    for i, (l, r) in enumerate(pairs):
        ins = t.matmul(out, l, r, start=(i == 0), stop=(i == n - 1))
    if sem is not None:
        sem.inc(ins)
    return ins


def emit_ln(k, T, eps_ap, Xc, XNc, stw, ps_a, ps_b, SQ, ROW, ONES, BONES, ONESROW,
            s_pe, s_act, s_dve, nch=NCH):
    act0, dve0 = s_act.cnt, s_dve.cnt

    def t_mean(t):
        t.wait_ge(s_act.h, act0)
        for c in range(nch):
            ins = t.matmul(ps_a[0:1, 0:stw], ONES[:, 0:1], Xc[c],
                           start=(c == 0), stop=(c == nch - 1))
        s_pe.inc(ins)
    k.t(t_mean)
    pe_mean = s_pe.cnt

    def s_mean(s):
        s.wait_ge(s_pe.h, pe_mean)
        s_act.inc(s.mul(ROW[0:1, 0:stw], ps_a[0:1, 0:stw], -1.0 / T))
    k.s(s_mean)
    act_mean = s_act.cnt

    def t_bcm(t):
        t.wait_ge(s_act.h, act_mean)
        t.wait_ge(s_dve.h, dve0)
        s_pe.inc(t.matmul(ps_b[:, 0:stw], ONESROW[0:1, :], ROW[0:1, 0:stw], start=True, stop=True))
    k.t(t_bcm)
    pe_bcm = s_pe.cnt

    def v_center(v):
        v.wait_ge(s_pe.h, pe_bcm)
        for c in range(nch):
            ins = v.tensor_tensor(XNc[c], Xc[c], ps_b[:, 0:stw], ALU.add)
        s_dve.inc(ins)
    k.v(v_center)
    dve_center = s_dve.cnt

    act_sq0 = s_act.cnt
    pe_sq0 = s_pe.cnt

    def s_square(s):
        s.wait_ge(s_dve.h, dve_center)
        for c in range(nch):
            if c >= 2:
                s.wait_ge(s_pe.h, pe_sq0 + c - 1)
            s_act.inc(s.activation(SQ[:, (c % 2) * stw:(c % 2 + 1) * stw],
                                   XNc[c], AF.Square))
    k.s(s_square)

    def t_sumsq(t):
        for c in range(nch):
            t.wait_ge(s_act.h, act_sq0 + c + 1)
            s_pe.inc(t.matmul(ps_a[0:1, stw:2 * stw], BONES[:, 0:1],
                              SQ[:, (c % 2) * stw:(c % 2 + 1) * stw],
                              start=(c == 0), stop=(c == nch - 1)))
    k.t(t_sumsq)
    pe_sumsq = s_pe.cnt

    def s_rstd(s):
        s.wait_ge(s_pe.h, pe_sumsq)
        s_act.inc(s.activation(ROW[0:1, stw:2 * stw], ps_a[0:1, stw:2 * stw], AF.Sqrt,
                               bias=eps_ap, scale=1.0 / T))
    k.s(s_rstd)
    act_rstd = s_act.cnt

    def v_recip(v):
        v.wait_ge(s_act.h, act_rstd)
        s_dve.inc(v.reciprocal(ROW[0:1, 0:stw], ROW[0:1, stw:2 * stw]))
    k.v(v_recip)
    dve_recip = s_dve.cnt

    def t_bcr(t):
        t.wait_ge(s_dve.h, dve_recip)
        s_pe.inc(t.matmul(ps_b[:, 0:stw], ONESROW[0:1, :], ROW[0:1, 0:stw], start=True, stop=True))
    k.t(t_bcr)
    pe_bcr = s_pe.cnt

    def v_norm(v):
        v.wait_ge(s_pe.h, pe_bcr)
        for c in range(nch):
            ins = v.tensor_tensor(XNc[c], XNc[c], ps_b[:, 0:stw], ALU.mult)
        s_dve.inc(ins)
    k.v(v_norm)
    return s_dve.cnt


def build_l0ffn(ntok=1024, stw=512):
    k = K()
    nc = k.nc
    nst = ntok // stw
    xres = nc.dram_tensor("xres", [D, ntok], F32, kind="ExternalInput").ap()
    wg_d = nc.dram_tensor("wg", [D, 8192], I8, kind="ExternalInput").ap()
    wu_d = nc.dram_tensor("wu", [D, 8192], I8, kind="ExternalInput").ap()
    wd_d = nc.dram_tensor("wd", [8192, D], I8, kind="ExternalInput").ap()
    sg_d = nc.dram_tensor("sg", [128, 64], F32, kind="ExternalInput").ap()
    su_d = nc.dram_tensor("su", [128, 64], F32, kind="ExternalInput").ap()
    sd_d = nc.dram_tensor("sd", [128, 16], F32, kind="ExternalInput").ap()
    xout = nc.dram_tensor("xout", [D, ntok], F32, kind="ExternalOutput").ap()

    X = k.sb("X", 128, NCH * stw, F32)
    XN = k.sb("XN", 128, NCH * stw, BF16)
    FUSE = k.sb("FUSE", 128, 64 * stw, BF16)
    WG = k.sb("WG", 128, 2 * 8192, BF16)
    WU = k.sb("WU", 128, 2 * 8192, BF16)
    WD = k.sb("WD", 128, 2 * 1024, BF16)
    G = k.sb("G", 128, 2 * stw, BF16)
    SQ = k.sb("SQ", 128, 2 * stw, BF16)
    SC = k.sb("SC", 128, 256, F32)
    ONES = k.sb("ONES", 128, 1, F32)
    CONSTS = k.sb("CONSTS", 128, 2, F32)
    BONES = k.sb("BONES", 128, 1, BF16)
    ONESROW = k.sb("ONESROW", 1, 128, F32)
    ROW = k.sb("ROW", 1, 2 * stw, F32)

    ps = [k.ps(f"ps{i}", 128, 512) for i in range(8)]

    s_x = k.sem("s_x", 16)
    s_st = k.sem("s_st", 16)
    s_wsc = k.sem("s_wsc", 16)
    s_wa = [k.sem("s_wa0", 16), k.sem("s_wa1", 16)]
    s_wb = [k.sem("s_wb0", 16), k.sem("s_wb1", 16)]
    s_g = k.sem("s_g", 1)
    s_pe = k.sem("s_pe", 1)
    s_act = k.sem("s_act", 1)
    s_dve = k.sem("s_dve", 1)

    k.begin()

    def g_init(g):
        s_wsc.inc(g.dma_start(out=SC[:, 0:64], in_=sg_d))
        s_wsc.inc(g.dma_start(out=SC[:, 64:128], in_=su_d))
        s_wsc.inc(g.dma_start(out=SC[:, 128:144], in_=sd_d))
        g.memset(ONES[:], 1.0)
        g.memset(CONSTS[:, 0:1], 1e-5)
        g.memset(BONES[:], 1.0)
        s_g.inc(g.memset(ONESROW[:], 1.0))
    k.g(g_init)
    w_scales, g_ones = s_wsc.cnt, s_g.cnt

    def v_init(v):
        v.wait_ge(s_wsc.h, w_scales)
        s_dve.inc(v.tensor_scalar(SC[:, 192:208], SC[:, 128:144], 0.5, None, ALU.mult))
    k.v(v_init)
    v_sdh = s_dve.cnt

    def g_wblk(g, fb, buf):
        for mat, dst in ((wg_d, WG), (wu_d, WU)):
            src = mat[:, fb * 512:(fb + 1) * 512]
            for kg in range(4):
                s_wa[buf].inc(g.dma_start(
                    out=dst[:, buf * 8192 + kg * 2048:buf * 8192 + (kg + 1) * 2048]
                        .rearrange("p (kc m) -> p kc m", kc=4),
                    in_=src[kg * 512:(kg + 1) * 512, :].rearrange("(kc p) m -> p kc m", p=128)))

    store_done = 0
    bufA_owner = [None, None]
    bufB_owner = None
    for st in range(nst):
        def y_load(y):
            y.wait_ge(s_st.h, store_done)
            for c in range(NCH):
                s_x.inc(y.dma_start(out=X[:, c * stw:(c + 1) * stw],
                                    in_=xres[c * 128:(c + 1) * 128, st * stw:(st + 1) * stw]))
        k.y(y_load)
        x_loaded = s_x.cnt

        def t_gate_ln(t):
            t.wait_ge(s_x.h, x_loaded)
            t.wait_ge(s_g.h, g_ones)
            if st > 0:
                t.wait_ge(s_dve.h, s_dve.cnt)
        k.t(t_gate_ln)

        class RowPair:
            def __getitem__(self, idx):
                part, free = idx
                lo, hi = free.start, free.stop
                if hi <= stw:
                    return ps[0][part, lo:hi]
                return ps[2][part, lo - stw:hi - stw]
        Xc = [X[:, c * stw:(c + 1) * stw] for c in range(NCH)]
        XNc = [XN[:, c * stw:(c + 1) * stw] for c in range(NCH)]
        ln_done = emit_ln(k, D, CONSTS[0:1, 0:1], Xc, XNc, stw, RowPair(), ps[1], SQ, ROW,
                          ONES, BONES, ONESROW, s_pe, s_act, s_dve)

        wblk_loaded = {}

        def g_w0(g):
            if bufA_owner[0] is not None:
                g.wait_ge(s_pe.h, bufA_owner[0])
            g_wblk(g, 0, 0)
        k.g(g_w0)
        wblk_loaded[0] = (s_wa[0], s_wa[0].cnt)

        peA0 = s_pe.cnt
        actA0 = s_act.cnt
        dveA0 = s_dve.cnt
        for fb in range(16):
            buf = fb % 2
            if fb + 1 < 16:
                def g_pref(g):
                    if fb >= 1:
                        g.wait_ge(s_pe.h, peA0 + 8 * fb)
                    elif bufA_owner[1] is not None:
                        g.wait_ge(s_pe.h, bufA_owner[1])
                    g_wblk(g, fb + 1, 1 - buf)
                k.g(g_pref)
                wblk_loaded[fb + 1] = (s_wa[1 - buf], s_wa[1 - buf].cnt)

            def t_A(t):
                sm, cv = wblk_loaded[fb]
                t.wait_ge(sm.h, cv)
                if fb == 0:
                    t.wait_ge(s_dve.h, ln_done)
                    t.wait_ge(s_dve.h, v_sdh)
                for oc in range(4):
                    fc = fb * 4 + oc
                    pg = ps[3 + 2 * (oc % 2)]
                    pu = ps[4 + 2 * (oc % 2)]
                    if fc >= 2:
                        t.wait_ge(s_act.h, actA0 + fc - 1)
                        t.wait_ge(s_dve.h, dveA0 + fc - 1)
                    base = buf * 8192
                    acc_mm(t, pg[:, 0:stw],
                           [(WG[:, base + kc * 512 + oc * 128:base + kc * 512 + (oc + 1) * 128],
                             XN[:, kc * stw:(kc + 1) * stw]) for kc in range(NCH)], s_pe)
                    acc_mm(t, pu[:, 0:stw],
                           [(WU[:, base + kc * 512 + oc * 128:base + kc * 512 + (oc + 1) * 128],
                             XN[:, kc * stw:(kc + 1) * stw]) for kc in range(NCH)], s_pe)
            k.t(t_A)

            def s_A(s):
                for oc in range(4):
                    fc = fb * 4 + oc
                    pg = ps[3 + 2 * (oc % 2)]
                    s.wait_ge(s_pe.h, peA0 + 8 * fb + 2 * oc + 1)
                    if fc >= 2:
                        s.wait_ge(s_dve.h, dveA0 + fc - 1)
                    s_act.inc(s.activation(G[:, (fc % 2) * stw:(fc % 2 + 1) * stw], pg[:, 0:stw],
                                           AF.Silu, scale=SC[:, fc:fc + 1]))
            k.s(s_A)

            def v_A(v):
                for oc in range(4):
                    fc = fb * 4 + oc
                    pu = ps[4 + 2 * (oc % 2)]
                    v.wait_ge(s_pe.h, peA0 + 8 * fb + 2 * oc + 2)
                    v.wait_ge(s_act.h, actA0 + fc + 1)
                    s_dve.inc(v.scalar_tensor_tensor(
                        FUSE[:, fc * stw:(fc + 1) * stw], pu[:, 0:stw], SC[:, 64 + fc:64 + fc + 1],
                        G[:, (fc % 2) * stw:(fc % 2 + 1) * stw], ALU.mult, ALU.mult))
            k.v(v_A)
        bufA_owner = [peA0 + 8 * 15, peA0 + 8 * 16]
        fuse_done = s_dve.cnt

        peB0 = s_pe.cnt
        wd_loaded = {}
        evac_cnt = s_dve.cnt
        for ocb in range(2):
            for kc in range(64):
                it = ocb * 64 + kc

                def g_wd(g):
                    if it >= 2:
                        g.wait_ge(s_pe.h, peB0 + it - 1)
                    elif bufB_owner is not None:
                        g.wait_ge(s_pe.h, bufB_owner)
                    s_wb[it % 2].inc(g.dma_start(
                        out=WD[:, (it % 2) * 1024:(it % 2 + 1) * 1024],
                        in_=wd_d[kc * 128:(kc + 1) * 128, ocb * 1024:(ocb + 1) * 1024]))
                k.g(g_wd)
                wd_loaded[it] = (s_wb[it % 2], s_wb[it % 2].cnt)

                def t_B(t):
                    sm, cv = wd_loaded[it]
                    t.wait_ge(sm.h, cv)
                    if it == 0:
                        t.wait_ge(s_dve.h, fuse_done)
                    if kc == 0 and ocb == 1:
                        t.wait_ge(s_dve.h, evac_cnt)
                    for ocl in range(8):
                        ins = t.matmul(ps[ocl][:, 0:stw],
                                       WD[:, (it % 2) * 1024 + ocl * 128:(it % 2) * 1024 + (ocl + 1) * 128],
                                       FUSE[:, kc * stw:(kc + 1) * stw],
                                       start=(kc == 0), stop=(kc == 63))
                    s_pe.inc(ins)
                k.t(t_B)
            peB_done = s_pe.cnt

            def v_B(v):
                v.wait_ge(s_pe.h, peB_done)
                for ocl in range(8):
                    c = ocb * 8 + ocl
                    s_dve.inc(v.scalar_tensor_tensor(
                        X[:, c * stw:(c + 1) * stw], ps[ocl][:, 0:stw], SC[:, 192 + c:192 + c + 1],
                        X[:, c * stw:(c + 1) * stw], ALU.mult, ALU.add))
            k.v(v_B)
            evac_cnt = s_dve.cnt

        bufB_owner = s_pe.cnt

        def y_store(y):
            y.wait_ge(s_dve.h, evac_cnt)
            for c in range(NCH):
                s_st.inc(y.dma_start(out=xout[c * 128:(c + 1) * 128, st * stw:(st + 1) * stw],
                                    in_=X[:, c * stw:(c + 1) * stw]))
            y.wait_ge(s_st.h, s_st.cnt)
        k.y(y_store)
        store_done = s_st.cnt

    nc._dbg = dict(X=X, XN=XN, FUSE=FUSE)
    k.st.close()
    return nc


def shard_fm(x):
    out = []
    for c in range(8):
        b, h = c // 2, c % 2
        out.append(np.ascontiguousarray(x[b, h * 1024:(h + 1) * 1024, :].T))
    return out


def unshard_fm(shards):
    x = np.empty((4, 2048, 2048), np.float32)
    for c in range(8):
        b, h = c // 2, c % 2
        x[b, h * 1024:(h + 1) * 1024, :] = np.asarray(shards[c]).T
    return x


def l0ffn_inmaps(xs, inputs):
    sg = np.ascontiguousarray(inputs["L0_ffn_s_gate"].reshape(64, 128).T)
    su = np.ascontiguousarray(inputs["L0_ffn_s_up"].reshape(64, 128).T)
    sd = np.ascontiguousarray(inputs["L0_ffn_s_down"].reshape(16, 128).T)
    return [{"xres": xs[c], "wg": inputs["L0_ffn_w_gate"], "wu": inputs["L0_ffn_w_up"],
             "wd": inputs["L0_ffn_w_down"], "sg": sg, "su": su, "sd": sd} for c in range(8)]


def _run_halved(builder, inmaps_fn, xs, inputs):
    maps = inmaps_fn(xs, inputs)
    for half in (0, 1):
        nc = builder(half)
        sel = [2 * b + half for b in range(4)]
        res = run_bass_kernel_spmd(nc, [maps[c] for c in sel], core_ids=list(range(4)))
        for b in range(4):
            xs[2 * b + half] = np.asarray(res.results[b]["xout"])
    return xs


def kernel(**inputs):
    nc = build_fused()
    maps = fused_inmaps(inputs)
    res = run_bass_kernel_spmd(nc, maps, core_ids=list(range(8)))
    xs = [np.asarray(res.results[c]["xout"]) for c in range(8)]
    return unshard_fm(xs)


if __name__ == "__main__":
    pass


def build_l0attn(half, T=2048, ntok=1024, stw=512, lnw=256):
    k = K()
    nc = k.nc
    own0 = half * ntok
    xfull = nc.dram_tensor("xfull", [D, T], F32, kind="ExternalInput").ap()
    wq_d = nc.dram_tensor("wq", [16, D, 128], F32, kind="ExternalInput").ap()
    wk_d = nc.dram_tensor("wk", [16, D, 128], F32, kind="ExternalInput").ap()
    wv_d = nc.dram_tensor("wv", [16, D, 128], F32, kind="ExternalInput").ap()
    wo_d = nc.dram_tensor("wo", [D, D], F32, kind="ExternalInput").ap()
    bqkvo_d = nc.dram_tensor("bqkvo", [128, 64], F32, kind="ExternalInput").ap()
    bvr_d = nc.dram_tensor("bvr", [1, D], F32, kind="ExternalInput").ap()
    xres = nc.dram_tensor("xres", [D, ntok], F32, kind="ExternalInput").ap()
    xout = nc.dram_tensor("xout", [D, ntok], F32, kind="ExternalOutput").ap()

    X = k.sb("X", 128, NCH * lnw, F32)
    XN = k.sb("XN", 128, NCH * T, BF16)
    QS = k.sb("QS", 128, ntok, BF16)
    KS = k.sb("KS", 128, T, BF16)
    VS = k.sb("VS", 128, 16 * 128, BF16)
    PT = k.sb("PT", 128, 16 * 512, BF16)
    Y = k.sb("Y", 128, 16 * ntok, BF16)
    WQ = k.sb("WQ", 128, 2 * 2048, BF16)
    WK = k.sb("WK", 128, 2 * 2048, BF16)
    WV = k.sb("WV", 128, 2 * 2048, BF16)
    WO = k.sb("WO", 128, 2 * 2048, BF16)
    XR = k.sb("XR", 128, 2 * 1024, F32)
    BIAS = k.sb("BIAS", 128, 64, F32)
    BVR = k.sb("BVR", 1, D, BF16)
    BONESROW = k.sb("BONESROW", 1, 128, BF16)
    RB = k.sb("RB", 128, 512, F32)
    MASK = k.sb("MASK", 128, 4 * 512, BF16)
    IDB = k.sb("IDB", 128, 128, BF16)
    SQ = k.sb("SQ", 128, 2 * lnw, BF16)
    CONSTS = k.sb("CONSTS", 128, 2, F32)
    ONES = k.sb("ONES", 128, 1, F32)
    BONES = k.sb("BONES", 128, 1, BF16)
    ONESROW = k.sb("ONESROW", 1, 128, F32)
    ROW = k.sb("ROW", 1, 4 * stw, F32)

    ps = [k.ps(f"ps{i}", 128, 512) for i in range(8)]

    s_x = k.sem("s_x", 16)
    s_xr = [k.sem("s_xr0", 16), k.sem("s_xr1", 16)]
    s_st = k.sem("s_st", 16)
    s_wsc = k.sem("s_wsc", 16)
    s_wh = [k.sem("s_wh0", 16), k.sem("s_wh1", 16)]
    s_wo = [k.sem("s_wo0", 16), k.sem("s_wo1", 16)]
    s_g = k.sem("s_g", 1)
    s_pe = k.sem("s_pe", 1)
    s_act = k.sem("s_act", 1)
    s_dve = k.sem("s_dve", 1)

    k.begin()

    def g_init(g):
        s_wsc.inc(g.dma_start(out=BIAS[:, 0:64], in_=bqkvo_d))
        s_wsc.inc(g.dma_start(out=BVR[:], in_=bvr_d))
        g.memset(ONES[:], 1.0)
        g.memset(BONES[:], 1.0)
        g.memset(ONESROW[:], 1.0)
        g.memset(BONESROW[:], 1.0)
        g.memset(CONSTS[:, 0:1], 1e-5)
        g.memset(CONSTS[:, 1:2], -ATT_C)
        g.memset(IDB[:], 1.0)
        s_g.inc(g.memset(MASK[:], 0.0))
        g.wait_ge(s_g.h, s_g.cnt)
        s_g.inc(g.affine_select(IDB[:], IDB[:], pattern=[[1, 128]], compare_op=ALU.is_equal,
                                fill=0.0, base=0, channel_multiplier=-1))
        g.wait_ge(s_g.h, s_g.cnt)
        for m in range(4):
            s_g.inc(g.affine_select(MASK[:, m * 512:(m + 1) * 512], MASK[:, m * 512:(m + 1) * 512],
                                    pattern=[[1, 512]], compare_op=ALU.is_ge, fill=-1e9,
                                    base=-m * 128, channel_multiplier=-1))
    k.g(g_init)
    g_done = s_g.cnt
    biases_in = s_wsc.cnt

    for sti in range(T // lnw):
        def y_load(y):
            if sti > 0:
                y.wait_ge(s_pe.h, s_pe.cnt)
                y.wait_ge(s_dve.h, s_dve.cnt)
            for c in range(NCH):
                s_x.inc(y.dma_start(out=X[:, c * lnw:(c + 1) * lnw],
                                    in_=xfull[c * 128:(c + 1) * 128, sti * lnw:(sti + 1) * lnw]))
        k.y(y_load)
        x_loaded = s_x.cnt

        def t_gate(t):
            t.wait_ge(s_x.h, x_loaded)
            t.wait_ge(s_g.h, g_done)
        k.t(t_gate)

        class RowPair:
            def __getitem__(self, idx):
                part, free = idx
                lo, hi = free.start, free.stop
                if hi <= lnw:
                    return ps[0][part, lo:hi]
                return ps[2][part, lo - lnw:hi - lnw]
        Xc = [X[:, c * lnw:(c + 1) * lnw] for c in range(NCH)]
        XNc = [XN[:, c * T + sti * lnw:c * T + (sti + 1) * lnw] for c in range(NCH)]
        emit_ln(k, D, CONSTS[0:1, 0:1], Xc, XNc, lnw, RowPair(), ps[1], SQ, ROW,
                ONES, BONES, ONESROW, s_pe, s_act, s_dve)
    ln_all = s_dve.cnt

    bufW_owner = [None, None]

    def g_whead(g, h, buf):
        for mat, dst in ((wq_d, WQ), (wk_d, WK), (wv_d, WV)):
            src = mat[h]
            for kg in range(2):
                s_wh[buf].inc(g.dma_start(
                    out=dst[:, buf * 2048 + kg * 1024:buf * 2048 + (kg + 1) * 1024]
                        .rearrange("p (c m) -> p c m", c=8),
                    in_=src[kg * 1024:(kg + 1) * 1024, :].rearrange("(c p) m -> p c m", p=128)))

    k.g(lambda g: g_whead(g, 0, 0))
    wh_loaded = {0: (s_wh[0], s_wh[0].cnt)}
    scale = float(128 ** -0.5)

    for h in range(16):
        buf = h % 2
        wbase = buf * 2048
        if h + 1 < 16:
            def g_pref(g):
                if bufW_owner[1 - buf] is not None:
                    g.wait_ge(s_pe.h, bufW_owner[1 - buf])
                g_whead(g, h + 1, 1 - buf)
            k.g(g_pref)
            wh_loaded[h + 1] = (s_wh[1 - buf], s_wh[1 - buf].cnt)

        prev_act, prev_dve, pe0 = s_act.cnt, s_dve.cnt, s_pe.cnt
        peQ = [pe0 + 1, pe0 + 2]
        peK = [pe0 + 3, pe0 + 4, pe0 + 5, pe0 + 6]
        peV = [pe0 + 7 + tt for tt in range(16)]
        actQK = [prev_act + 1 + j for j in range(6)]
        dveV = [prev_dve + 1 + tt for tt in range(16)]
        qps = [ps[0], ps[3]]
        kps = [ps[1], ps[4], ps[5], ps[6]]

        def t_qkv(t):
            sm, cv = wh_loaded[h]
            t.wait_ge(sm.h, cv)
            if h == 0:
                t.wait_ge(s_dve.h, ln_all)
                t.wait_ge(s_wsc.h, biases_in)
            else:
                t.wait_ge(s_act.h, prev_act)
                t.wait_ge(s_dve.h, prev_dve)
            for j in range(2):
                acc_mm(t, qps[j][:, 0:stw],
                       [(WQ[:, wbase + c * 128:wbase + (c + 1) * 128],
                         XN[:, c * T + own0 + j * stw:c * T + own0 + (j + 1) * stw])
                        for c in range(NCH)], s_pe)
            for j in range(4):
                acc_mm(t, kps[j][:, 0:stw],
                       [(WK[:, wbase + c * 128:wbase + (c + 1) * 128],
                         XN[:, c * T + j * stw:c * T + (j + 1) * stw]) for c in range(NCH)], s_pe)
            for tt in range(16):
                pv = ps[2] if tt % 2 == 0 else ps[7]
                if tt >= 2:
                    t.wait_ge(s_dve.h, dveV[tt - 2])
                for c in range(NCH):
                    t.matmul(pv[:, 0:128], XN[:, c * T + tt * 128:c * T + (tt + 1) * 128],
                             WV[:, wbase + c * 128:wbase + (c + 1) * 128],
                             start=(c == 0), stop=False)
                s_pe.inc(t.matmul(pv[:, 0:128], BONESROW[:], BVR[0:1, h * 128:(h + 1) * 128],
                                  start=False, stop=True))
        k.t(t_qkv)
        bufW_owner[buf] = s_pe.cnt
        assert s_pe.cnt == peV[15]

        def s_qk(sc):
            for j in range(2):
                sc.wait_ge(s_pe.h, peQ[j])
                s_act.inc(sc.activation(QS[:, j * stw:(j + 1) * stw], qps[j][:, 0:stw],
                                        AF.Identity, bias=BIAS[:, h:h + 1]))
            for j in range(4):
                sc.wait_ge(s_pe.h, peK[j])
                s_act.inc(sc.activation(KS[:, j * stw:(j + 1) * stw], kps[j][:, 0:stw],
                                        AF.Identity, bias=BIAS[:, 16 + h:16 + h + 1]))
        k.s(s_qk)
        assert s_act.cnt == actQK[5]

        def v_v(v):
            for tt in range(16):
                pv = ps[2] if tt % 2 == 0 else ps[7]
                v.wait_ge(s_pe.h, peV[tt])
                s_dve.inc(v.tensor_copy(VS[:, tt * 128:(tt + 1) * 128], pv[:, 0:128]))
        k.v(v_v)
        assert s_dve.cnt == dveV[15]
        evacA_act, evacA_dve = s_act.cnt, s_dve.cnt

        for qt in range(2):
            q0g = own0 + qt * stw
            kcmax = (q0g + stw) // 128
            peS0 = s_pe.cnt
            actS0 = s_act.cnt

            def t_scores(t):
                t.wait_ge(s_act.h, evacA_act)
                t.wait_ge(s_dve.h, evacA_dve)
                for kc in range(kcmax):
                    bank = ps[3] if kc % 2 == 0 else ps[4]
                    if kc >= 2:
                        t.wait_ge(s_act.h, actS0 + kc - 1)
                    m = kc - q0g // 128
                    diag = 0 <= m <= 3
                    ins = t.matmul(bank[:, 0:stw], KS[:, kc * 128:(kc + 1) * 128],
                                   QS[:, qt * stw:(qt + 1) * stw], start=True, stop=not diag)
                    if diag:
                        ins = t.matmul(bank[:, 0:stw], IDB[:], MASK[:, m * 512:(m + 1) * 512],
                                       start=False, stop=True)
                    s_pe.inc(ins)
            k.t(t_scores)

            def s_exp(sc):
                for kc in range(kcmax):
                    bank = ps[3] if kc % 2 == 0 else ps[4]
                    sc.wait_ge(s_pe.h, peS0 + kc + 1)
                    s_act.inc(sc.activation(PT[:, kc * stw:(kc + 1) * stw], bank[:, 0:stw],
                                            AF.Exp, bias=CONSTS[:, 1:2], scale=scale))
            k.s(s_exp)
            exp_done = s_act.cnt

            def t_denom(t):
                t.wait_ge(s_act.h, exp_done)
                acc_mm(t, ps[5][0:1, 0:stw],
                       [(BONES[:, 0:1], PT[:, kc * stw:(kc + 1) * stw]) for kc in range(kcmax)],
                       s_pe)
            k.t(t_denom)
            pe_denom = s_pe.cnt

            def v_recip(v):
                v.wait_ge(s_pe.h, pe_denom)
                s_dve.inc(v.tensor_scalar(ROW[0:1, 2 * stw:3 * stw], ps[5][0:1, 0:stw], 1e-30,
                                          None, ALU.add))
                v.wait_ge(s_dve.h, s_dve.cnt)
                s_dve.inc(v.reciprocal(ROW[0:1, 3 * stw:4 * stw], ROW[0:1, 2 * stw:3 * stw]))
            k.v(v_recip)
            dve_recip = s_dve.cnt

            def t_bcast(t):
                t.wait_ge(s_dve.h, dve_recip)
                s_pe.inc(t.matmul(ps[6][:, 0:stw], ONESROW[0:1, :], ROW[0:1, 3 * stw:4 * stw],
                                  start=True, stop=True))
            k.t(t_bcast)
            pe_bc = s_pe.cnt

            def s_rb(sc):
                sc.wait_ge(s_pe.h, pe_bc)
                s_act.inc(sc.copy(RB[:, 0:stw], ps[6][:, 0:stw]))
            k.s(s_rb)
            act_rb = s_act.cnt

            def t_av(t):
                acc_mm(t, ps[7][:, 0:stw],
                       [(VS[:, kc * 128:(kc + 1) * 128], PT[:, kc * stw:(kc + 1) * stw])
                        for kc in range(kcmax)], s_pe)
            k.t(t_av)
            pe_av = s_pe.cnt

            def v_norm2(v):
                v.wait_ge(s_pe.h, pe_av)
                v.wait_ge(s_act.h, act_rb)
                s_dve.inc(v.tensor_tensor(Y[:, h * ntok + qt * stw:h * ntok + (qt + 1) * stw],
                                          ps[7][:, 0:stw], RB[:, 0:stw], ALU.mult))
            k.v(v_norm2)
            evacA_act, evacA_dve = s_act.cnt, s_dve.cnt

    heads_done_pe, heads_done_act, heads_done_dve = s_pe.cnt, s_act.cnt, s_dve.cnt

    def g_wo(g, oc, buf):
        src = wo_d[:, oc * 128:(oc + 1) * 128]
        for kg in range(2):
            s_wo[buf].inc(g.dma_start(
                out=WO[:, buf * 2048 + kg * 1024:buf * 2048 + (kg + 1) * 1024]
                    .rearrange("p (c m) -> p c m", c=8),
                in_=src[kg * 1024:(kg + 1) * 1024, :].rearrange("(c p) m -> p c m", p=128)))

    k.g(lambda g: g_wo(g, 0, 0))
    wo_loaded = {0: (s_wo[0], s_wo[0].cnt)}
    xr_loaded = {}

    def y_xr0(y):
        s_xr[0].inc(y.dma_start(out=XR[:, 0:1024], in_=xres[0:128, :]))
    k.y(y_xr0)
    xr_loaded[0] = (s_xr[0], s_xr[0].cnt)

    peW0 = s_pe.cnt
    dveW0 = s_dve.cnt
    st_done = {}
    for oc in range(16):
        buf = oc % 2
        if oc + 1 < 16:
            def g_wpref(g):
                if oc >= 1:
                    g.wait_ge(s_pe.h, peW0 + 2 * oc)
                g_wo(g, oc + 1, 1 - buf)
            k.g(g_wpref)
            wo_loaded[oc + 1] = (s_wo[1 - buf], s_wo[1 - buf].cnt)

            def y_xr(y):
                if oc >= 1:
                    y.wait_ge(s_st.h, st_done[oc - 1])
                s_xr[1 - buf].inc(y.dma_start(out=XR[:, (1 - buf) * 1024:(2 - buf) * 1024],
                                              in_=xres[(oc + 1) * 128:(oc + 2) * 128, :]))
            k.y(y_xr)
            xr_loaded[oc + 1] = (s_xr[1 - buf], s_xr[1 - buf].cnt)

        def t_wo(t):
            sm, cv = wo_loaded[oc]
            t.wait_ge(sm.h, cv)
            if oc == 0:
                t.wait_ge(s_dve.h, heads_done_dve)
                t.wait_ge(s_act.h, heads_done_act)
            if oc >= 2:
                t.wait_ge(s_dve.h, dveW0 + 2 * (oc - 1))
            for j in range(2):
                acc_mm(t, ps[2 * buf + j][:, 0:stw],
                       [(WO[:, buf * 2048 + c * 128:buf * 2048 + (c + 1) * 128],
                         Y[:, c * ntok + j * stw:c * ntok + (j + 1) * stw]) for c in range(NCH)],
                       s_pe)
        k.t(t_wo)
        pe_oc = s_pe.cnt

        def v_wo(v):
            sm, cv = xr_loaded[oc]
            v.wait_ge(sm.h, cv)
            for j in range(2):
                v.wait_ge(s_pe.h, pe_oc - 1 + j)
                s_dve.inc(v.scalar_tensor_tensor(
                    XR[:, buf * 1024 + j * stw:buf * 1024 + (j + 1) * stw],
                    ps[2 * buf + j][:, 0:stw], BIAS[:, 48 + oc:48 + oc + 1],
                    XR[:, buf * 1024 + j * stw:buf * 1024 + (j + 1) * stw], ALU.add, ALU.add))
        k.v(v_wo)
        dve_oc = s_dve.cnt

        def y_sto(y):
            y.wait_ge(s_dve.h, dve_oc)
            s_st.inc(y.dma_start(out=xout[oc * 128:(oc + 1) * 128, :], in_=XR[:, buf * 1024:(buf + 1) * 1024]))
        k.y(y_sto)
        st_done[oc] = s_st.cnt

    def y_fin(y):
        y.wait_ge(s_st.h, s_st.cnt)
    k.y(y_fin)

    nc._dbg = dict(XN=XN, Y=Y, QS=QS, KS=KS, VS=VS, PT=PT)
    k.st.close()
    return nc


def l0attn_inmaps(xs, inputs):
    wq, wk, wv = inputs["L0_attn_Wq"], inputs["L0_attn_Wk"], inputs["L0_attn_Wv"]
    bqkvo = np.concatenate([inputs["L0_attn_bq"].T, inputs["L0_attn_bk"].T,
                            inputs["L0_attn_bv"].T,
                            inputs["L0_attn_bo"].reshape(16, 128).T], axis=1)
    bqkvo = np.ascontiguousarray(bqkvo)
    bvr = np.ascontiguousarray(inputs["L0_attn_bv"].reshape(1, 2048))
    maps = []
    for c in range(8):
        b, hlf = c // 2, c % 2
        xfull = np.ascontiguousarray(
            np.concatenate([np.asarray(xs[2 * b]), np.asarray(xs[2 * b + 1])], axis=1))
        maps.append({"xfull": xfull, "wq": wq, "wk": wk, "wv": wv, "wo": inputs["L0_attn_Wo"],
                     "bqkvo": bqkvo, "bvr": bvr, "xres": xs[c]})
    return maps


def build_l1ffn(half, T=2048, ntok=1024, lnw=256):
    k = K()
    nc = k.nc
    own0 = half * ntok
    xfull = nc.dram_tensor("xfull", [D, T], F32, kind="ExternalInput").ap()
    aggw_d = nc.dram_tensor("aggw", [D, D], F32, kind="ExternalInput").ap()
    d1w_d = nc.dram_tensor("d1w", [D, 8192], F32, kind="ExternalInput").ap()
    d2w_d = nc.dram_tensor("d2w", [8192, D], F32, kind="ExternalInput").ap()
    bcw_d = nc.dram_tensor("bcw", [D, D], F32, kind="ExternalInput").ap()
    gw_d = nc.dram_tensor("gw", [D, D], F32, kind="ExternalInput").ap()
    cols_d = nc.dram_tensor("cols", [128, 8 * 16 + 64], F32, kind="ExternalInput").ap()
    xout = nc.dram_tensor("xout", [D, ntok], F32, kind="ExternalOutput").ap()

    X = k.sb("X", 128, NCH * lnw, F32)
    XB = k.sb("XB", 128, 2 * 1024, F32)
    XN = k.sb("XN", 128, NCH * T, BF16)
    GT = k.sb("GT", 128, NCH * ntok, BF16)
    OT = k.sb("OT", 128, NCH * ntok, BF16)
    W1 = k.sb("W1", 128, 2 * 8192, BF16)
    W2 = k.sb("W2", 128, 2 * 2048, BF16)
    TH = k.sb("TH", 128, 16, F32)
    THB = k.sb("THB", 128, 16, BF16)
    TC = k.sb("TC", 128, 16, F32)
    TN = k.sb("TN", 128, 16, BF16)
    HB = k.sb("HB", 128, 64, BF16)
    SQT = k.sb("SQT", 128, 16, BF16)
    BCT = k.sb("BCT", 128, 16, F32)
    G2B2 = k.sb("G2B2", 128, 32, F32)
    COLS = k.sb("COLS", 128, 8 * 16 + 64, F32)
    TMP = k.sb("TMP", 128, 1024, F32)
    SQ = k.sb("SQ", 128, 2 * lnw, BF16)
    CONSTS = k.sb("CONSTS", 128, 2, F32)
    ONES = k.sb("ONES", 128, 1, F32)
    BONES = k.sb("BONES", 128, 1, BF16)
    ONESROW = k.sb("ONESROW", 1, 128, F32)
    ROW = k.sb("ROW", 1, 4 * lnw, F32)

    ps = [k.ps(f"ps{i}", 128, 512) for i in range(8)]

    s_x = k.sem("s_x", 16)
    s_xb = [k.sem("s_xb0", 16), k.sem("s_xb1", 16)]
    s_st = k.sem("s_st", 16)
    s_wsc = k.sem("s_wsc", 16)
    s_w1 = [k.sem("s_w10", 16), k.sem("s_w11", 16)]
    s_w2 = [k.sem("s_w20", 16), k.sem("s_w21", 16)]
    s_g = k.sem("s_g", 1)
    s_pe = k.sem("s_pe", 1)
    s_act = k.sem("s_act", 1)
    s_dve = k.sem("s_dve", 1)

    CA, CD2, CGM, CBT, CPG, CPB, CGB, CBC = (COLS[:, i * 16:(i + 1) * 16] for i in range(8))
    CD1 = COLS[:, 128:192]

    k.begin()

    def g_init(g):
        s_wsc.inc(g.dma_start(out=COLS[:], in_=cols_d))
        g.memset(ONES[:], 1.0)
        g.memset(BONES[:], 1.0)
        g.memset(ONESROW[:], 1.0)
        g.memset(CONSTS[:, 0:1], 1e-5)
        s_g.inc(g.memset(CONSTS[:, 1:2], 1e-6))
    k.g(g_init)
    g_done, cols_in = s_g.cnt, s_wsc.cnt

    def v_g2b2(v):
        v.wait_ge(s_wsc.h, cols_in)
        s_dve.inc(v.tensor_scalar(G2B2[:, 0:16], CGM, 0.5, None, ALU.mult))
        s_dve.inc(v.tensor_scalar(G2B2[:, 16:32], CBT, 0.5, None, ALU.mult))
    k.v(v_g2b2)
    g2_done = s_dve.cnt

    for sti in range(T // lnw):
        def y_load(y):
            if sti > 0:
                y.wait_ge(s_pe.h, s_pe.cnt)
                y.wait_ge(s_dve.h, s_dve.cnt)
            for c in range(NCH):
                s_x.inc(y.dma_start(out=X[:, c * lnw:(c + 1) * lnw],
                                    in_=xfull[c * 128:(c + 1) * 128, sti * lnw:(sti + 1) * lnw]))
        k.y(y_load)
        x_loaded = s_x.cnt

        def t_gate(t):
            t.wait_ge(s_x.h, x_loaded)
            t.wait_ge(s_g.h, g_done)
        k.t(t_gate)

        class RowPair:
            def __getitem__(self, idx):
                part, free = idx
                lo, hi = free.start, free.stop
                if hi <= lnw:
                    return ps[0][part, lo:hi]
                return ps[2][part, lo - lnw:hi - lnw]
        Xc = [X[:, c * lnw:(c + 1) * lnw] for c in range(NCH)]
        XNc = [XN[:, c * T + sti * lnw:c * T + (sti + 1) * lnw] for c in range(NCH)]
        emit_ln(k, D, CONSTS[0:1, 0:1], Xc, XNc, lnw, RowPair(), ps[1], SQ, ROW,
                ONES, BONES, ONESROW, s_pe, s_act, s_dve)
    ln_all = s_dve.cnt

    def emit_matvec(dram, nkc, noc, rhs_cols, out_ps, wbuf, wsems, gate_dve=None, gate_act=None):
        ncols = nkc * 128
        loaded = {}
        pe_gate = s_pe.cnt
        pe_base = s_pe.cnt

        def load(g, oc, buf):
            wsems[buf].inc(g.dma_start(
                out=wbuf[:, buf * ncols:(buf + 1) * ncols].rearrange("p (c m) -> p c m", c=nkc),
                in_=dram[:, oc * 128:(oc + 1) * 128].rearrange("(c p) m -> p c m", p=128)))

        def g_l0(g):
            g.wait_ge(s_pe.h, pe_gate)
            load(g, 0, 0)
        k.g(g_l0)
        loaded[0] = (wsems[0], wsems[0].cnt)
        for oc in range(noc):
            buf = oc % 2
            if oc + 1 < noc:
                def g_l(g):
                    if oc >= 1:
                        g.wait_ge(s_pe.h, pe_base + oc)
                    else:
                        g.wait_ge(s_pe.h, pe_gate)
                    load(g, oc + 1, 1 - buf)
                k.g(g_l)
                loaded[oc + 1] = (wsems[1 - buf], wsems[1 - buf].cnt)

            def t_mv(t):
                sm, cv = loaded[oc]
                t.wait_ge(sm.h, cv)
                if oc == 0:
                    if gate_dve is not None:
                        t.wait_ge(s_dve.h, gate_dve)
                    if gate_act is not None:
                        t.wait_ge(s_act.h, gate_act)
                acc_mm(t, out_ps[:, oc:oc + 1],
                       [(wbuf[:, buf * ncols + kc * 128:buf * ncols + (kc + 1) * 128],
                         rhs_cols(kc)) for kc in range(nkc)], s_pe)
            k.t(t_mv)
        return s_pe.cnt

    agg_done = emit_matvec(aggw_d, NCH, 16, lambda kc: XN[:, kc * T + T - 1:kc * T + T],
                           ps[3][:, 0:16], W2, s_w2, gate_dve=ln_all)

    def v_agg(v):
        v.wait_ge(s_pe.h, agg_done)
        v.wait_ge(s_wsc.h, cols_in)
        s_dve.inc(v.tensor_tensor(TH[:], ps[3][:, 0:16], CA, ALU.add))
    k.v(v_agg)

    prev_gelu = s_act.cnt
    for step in range(4):
        def t_tsum(t):
            t.wait_ge(s_dve.h, s_dve.cnt)
            s_pe.inc(t.matmul(ps[4][0:1, 0:16], ONES[:, 0:1], TH[:], start=True, stop=True))
        k.t(t_tsum)
        pe_tsum = s_pe.cnt

        def v_tmean(v):
            v.wait_ge(s_pe.h, pe_tsum)
            s_dve.inc(v.reduce_sum(ROW[0:1, 0:1], ps[4][0:1, 0:16], axis=mybir.AxisListType.X))
            v.wait_ge(s_dve.h, s_dve.cnt)
            s_dve.inc(v.tensor_scalar(ROW[0:1, 1:2], ROW[0:1, 0:1], -1.0 / D, None, ALU.mult))
        k.v(v_tmean)
        dve_tmean = s_dve.cnt

        def t_bcm(t):
            t.wait_ge(s_dve.h, dve_tmean)
            s_pe.inc(t.matmul(ps[5][:, 0:1], ONESROW[0:1, :], ROW[0:1, 1:2], start=True, stop=True))
        k.t(t_bcm)
        pe_bcm = s_pe.cnt

        def v_cent(v):
            v.wait_ge(s_pe.h, pe_bcm)
            s_dve.inc(v.tensor_scalar(TC[:], TH[:], ps[5][:, 0:1], None, ALU.add))
        k.v(v_cent)
        dve_cent = s_dve.cnt

        def s_sq(s):
            s.wait_ge(s_dve.h, dve_cent)
            s_act.inc(s.activation(SQT[:], TC[:], AF.Square))
        k.s(s_sq)
        act_sq = s_act.cnt

        def t_ssum(t):
            t.wait_ge(s_act.h, act_sq)
            s_pe.inc(t.matmul(ps[4][0:1, 16:32], BONES[:, 0:1], SQT[:], start=True, stop=True))
        k.t(t_ssum)
        pe_ssum = s_pe.cnt

        def v_var(v):
            v.wait_ge(s_pe.h, pe_ssum)
            s_dve.inc(v.reduce_sum(ROW[0:1, 2:3], ps[4][0:1, 16:32], axis=mybir.AxisListType.X))
        k.v(v_var)
        dve_var = s_dve.cnt

        def s_sqrt(s):
            s.wait_ge(s_dve.h, dve_var)
            s_act.inc(s.activation(ROW[0:1, 3:4], ROW[0:1, 2:3], AF.Sqrt,
                                   bias=CONSTS[0:1, 1:2], scale=1.0 / D))
        k.s(s_sqrt)
        act_sqrt = s_act.cnt

        def v_rcp(v):
            v.wait_ge(s_act.h, act_sqrt)
            s_dve.inc(v.reciprocal(ROW[0:1, 4:5], ROW[0:1, 3:4]))
        k.v(v_rcp)
        dve_rcp = s_dve.cnt

        def t_bcr(t):
            t.wait_ge(s_dve.h, dve_rcp)
            s_pe.inc(t.matmul(ps[5][:, 1:2], ONESROW[0:1, :], ROW[0:1, 4:5], start=True, stop=True))
        k.t(t_bcr)
        pe_bcr = s_pe.cnt

        def v_tn(v):
            v.wait_ge(s_pe.h, pe_bcr)
            s_dve.inc(v.tensor_scalar(TC[:], TC[:], ps[5][:, 1:2], None, ALU.mult))
            v.wait_ge(s_dve.h, s_dve.cnt)
            s_dve.inc(v.scalar_tensor_tensor(TN[:], TC[:], 1.0, CPG, ALU.mult, ALU.mult))
            v.wait_ge(s_dve.h, s_dve.cnt)
            s_dve.inc(v.tensor_tensor(TN[:], TN[:], CPB, ALU.add))
        k.v(v_tn)
        tn_done = s_dve.cnt

        d1_done = emit_matvec(d1w_d, NCH, 64, lambda kc: TN[:, kc:kc + 1], ps[6][:, 0:64],
                              W2, s_w2, gate_dve=tn_done, gate_act=prev_gelu)

        def s_gelu(s):
            s.wait_ge(s_pe.h, d1_done)
            for oc in range(64):
                s_act.inc(s.activation(HB[:, oc:oc + 1], ps[6][:, oc:oc + 1], AF.Gelu,
                                       bias=CD1[:, oc:oc + 1]))
        k.s(s_gelu)
        prev_gelu = s_act.cnt

        d2_done = emit_matvec(d2w_d, 64, 16, lambda kc: HB[:, kc:kc + 1], ps[7][:, 0:16],
                              W1, s_w1, gate_act=prev_gelu)

        def v_d2(v):
            v.wait_ge(s_pe.h, d2_done)
            s_dve.inc(v.tensor_tensor(TC[:], ps[7][:, 0:16], CD2, ALU.add))
            v.wait_ge(s_dve.h, s_dve.cnt)
            s_dve.inc(v.tensor_tensor(TH[:], TH[:], TC[:], ALU.add))
        k.v(v_d2)

    def v_thb(v):
        v.wait_ge(s_dve.h, s_dve.cnt)
        s_dve.inc(v.tensor_copy(THB[:], TH[:]))
    k.v(v_thb)
    thb_done = s_dve.cnt

    bc_done = emit_matvec(bcw_d, NCH, 16, lambda kc: THB[:, kc:kc + 1], ps[3][:, 0:16],
                          W2, s_w2, gate_dve=thb_done)

    def v_bc(v):
        v.wait_ge(s_pe.h, bc_done)
        s_dve.inc(v.tensor_tensor(BCT[:], ps[3][:, 0:16], CBC, ALU.add))
    k.v(v_bc)
    bc_ready = s_dve.cnt

    gw_loaded = {}

    def g_gw(g, oc, buf):
        g.wait_ge(s_pe.h, s_pe.cnt if oc <= 1 else gw_pe[oc - 2])
        src = gw_d[:, oc * 128:(oc + 1) * 128]
        for kg in range(2):
            s_w2[buf].inc(g.dma_start(
                out=W2[:, buf * 2048 + kg * 1024:buf * 2048 + (kg + 1) * 1024]
                    .rearrange("p (c m) -> p c m", c=8),
                in_=src[kg * 1024:(kg + 1) * 1024, :].rearrange("(c p) m -> p c m", p=128)))

    gw_pe = {}
    k.g(lambda g: g_gw(g, 0, 0))
    gw_loaded[0] = (s_w2[0], s_w2[0].cnt)
    actG0 = s_act.cnt
    dveG0 = s_dve.cnt
    for oc in range(16):
        buf = oc % 2
        if oc + 1 < 16:
            k.g(lambda g: g_gw(g, oc + 1, 1 - buf))
            gw_loaded[oc + 1] = (s_w2[1 - buf], s_w2[1 - buf].cnt)

        def t_gw(t):
            sm, cv = gw_loaded[oc]
            t.wait_ge(sm.h, cv)
            if oc <= 1:
                t.wait_ge(s_dve.h, bc_ready)
            if oc >= 2:
                t.wait_ge(s_act.h, actG0 + 2 * (oc - 1))
            for j in range(2):
                acc_mm(t, ps[2 * buf + j][:, 0:512],
                       [(W2[:, buf * 2048 + c * 128:buf * 2048 + (c + 1) * 128],
                         XN[:, c * T + own0 + j * 512:c * T + own0 + (j + 1) * 512])
                        for c in range(NCH)], s_pe)
        k.t(t_gw)
        gw_pe[oc] = s_pe.cnt
        pe_oc = s_pe.cnt

        def s_sig(s):
            for j in range(2):
                s.wait_ge(s_pe.h, pe_oc - 1 + j)
                s_act.inc(s.activation(GT[:, oc * ntok + j * 512:oc * ntok + (j + 1) * 512],
                                       ps[2 * buf + j][:, 0:512], AF.Sigmoid,
                                       bias=CGB[:, oc:oc + 1]))
        k.s(s_sig)
        act_oc = s_act.cnt

        def v_ot(v):
            v.wait_ge(s_act.h, act_oc)
            if oc == 0:
                v.wait_ge(s_dve.h, bc_ready)
            s_dve.inc(v.scalar_tensor_tensor(
                OT[:, oc * ntok:(oc + 1) * ntok], GT[:, oc * ntok:(oc + 1) * ntok],
                BCT[:, oc:oc + 1], XN[:, oc * T + own0:oc * T + own0 + ntok],
                ALU.mult, ALU.add))
        k.v(v_ot)
    ot_done = s_dve.cnt

    for sti in range(ntok // lnw):
        def t_gate2(t):
            t.wait_ge(s_dve.h, ot_done)
        k.t(t_gate2)

        class RowPair2:
            def __getitem__(self, idx):
                part, free = idx
                lo, hi = free.start, free.stop
                if hi <= lnw:
                    return ps[0][part, lo:hi]
                return ps[2][part, lo - lnw:hi - lnw]
        Xc = [OT[:, c * ntok + sti * lnw:c * ntok + (sti + 1) * lnw] for c in range(NCH)]
        XNc = [GT[:, c * ntok + sti * lnw:c * ntok + (sti + 1) * lnw] for c in range(NCH)]
        emit_ln(k, D, CONSTS[0:1, 1:2], Xc, XNc, lnw, RowPair2(), ps[1], SQ, ROW,
                BONES, BONES, ONESROW, s_pe, s_act, s_dve)
    lnf_done = s_dve.cnt

    st_done = {}
    for c in range(NCH):
        buf = c % 2

        def y_xb(y):
            if c >= 2:
                y.wait_ge(s_st.h, st_done[c - 1])
            s_xb[buf].inc(y.dma_start(out=XB[:, buf * 1024:(buf + 1) * 1024],
                                      in_=xfull[c * 128:(c + 1) * 128, own0:own0 + ntok]))
        k.y(y_xb)
        xb_in = (s_xb[buf], s_xb[buf].cnt)

        def v_app(v):
            if c == 0:
                v.wait_ge(s_dve.h, lnf_done)
                v.wait_ge(s_dve.h, g2_done)
            sm, cv = xb_in
            v.wait_ge(sm.h, cv)
            s_dve.inc(v.tensor_scalar(TMP[:], GT[:, c * ntok:(c + 1) * ntok],
                                      G2B2[:, c:c + 1], G2B2[:, 16 + c:16 + c + 1],
                                      ALU.mult, ALU.add))
            v.wait_ge(s_dve.h, s_dve.cnt)
            s_dve.inc(v.tensor_tensor(XB[:, buf * 1024:(buf + 1) * 1024],
                                      XB[:, buf * 1024:(buf + 1) * 1024], TMP[:], ALU.add))
        k.v(v_app)
        dve_c = s_dve.cnt

        def y_sto(y):
            y.wait_ge(s_dve.h, dve_c)
            s_st.inc(y.dma_start(out=xout[c * 128:(c + 1) * 128, :],
                                 in_=XB[:, buf * 1024:(buf + 1) * 1024]))
        k.y(y_sto)
        st_done[c] = s_st.cnt

    def y_fin(y):
        y.wait_ge(s_st.h, s_st.cnt)
    k.y(y_fin)

    nc._dbg = dict(XN=XN, GT=GT, OT=OT, TH=TH, BCT=BCT)
    k.st.close()
    return nc


def l1ffn_inmaps(xs, inputs):
    def colify(v, n=16):
        return np.ascontiguousarray(v.reshape(n, 128).T)
    cols = np.concatenate([colify(inputs["L1_ffn_agg_b"]), colify(inputs["L1_ffn_d2_b"]),
                           colify(inputs["L1_ffn_out_gamma"]), colify(inputs["L1_ffn_out_beta"]),
                           colify(inputs["L1_ffn_pn_gamma"]), colify(inputs["L1_ffn_pn_beta"]),
                           colify(inputs["L1_ffn_gate_b"]), colify(inputs["L1_ffn_bc_b"]),
                           colify(inputs["L1_ffn_d1_b"], 64)], axis=1)
    cols = np.ascontiguousarray(cols)
    maps = []
    for c in range(8):
        b = c // 2
        xfull = np.ascontiguousarray(
            np.concatenate([np.asarray(xs[2 * b]), np.asarray(xs[2 * b + 1])], axis=1))
        maps.append({"xfull": xfull, "aggw": inputs["L1_ffn_agg_w"], "d1w": inputs["L1_ffn_d1_w"],
                     "d2w": inputs["L1_ffn_d2_w"], "bcw": inputs["L1_ffn_bc_w"],
                     "gw": inputs["L1_ffn_gate_w"], "cols": cols})
    return maps


def build_l1attn(half, T=2048, ntok=1024):
    k = K()
    nc = k.nc
    CH = 256
    nchunks = (half + 1) * 4
    own_lo = half * 4
    xfull = nc.dram_tensor("xfull", [D, T], F32, kind="ExternalInput").ap()
    win_d = nc.dram_tensor("win", [D, 8512], F32, kind="ExternalInput").ap()
    wout_d = nc.dram_tensor("wout", [4096, D], F32, kind="ExternalInput").ap()
    cwk_d = nc.dram_tensor("cwk", [128, 4 * 34], F32, kind="ExternalInput").ap()
    cb_d = nc.dram_tensor("cb", [128, 34], F32, kind="ExternalInput").ap()
    auxc_d = nc.dram_tensor("auxc", [64, 2], F32, kind="ExternalInput").ap()
    drow_d = nc.dram_tensor("drow", [1, 64], F32, kind="ExternalInput").ap()
    nwr_d = nc.dram_tensor("nwr", [1, 4096], F32, kind="ExternalInput").ap()
    xout = nc.dram_tensor("xout", [D, ntok], F32, kind="ExternalOutput").ap()

    lnw = 256
    X = k.sb("X", 128, NCH * lnw, F32)
    XB = X
    XN = k.sb("XN", 128, NCH * lnw, BF16)
    PRE = k.sb("PRE", 128, 34 * (CH + 3), BF16)
    HALO = k.sb("HALO", 128, 34 * 3, BF16)
    XBC = k.sb("XBC", 128, 34 * CH, BF16)
    CONVT = k.sb("CONVT", 128, 2 * CH, F32)
    TMPY = CONVT
    ZS = k.sb("ZS", 128, 2 * 4096, BF16)
    CBTS = ZS[:, 0:384]
    LSEG = ZS[:, 384:768]
    M1S = ZS[:, 768:1152]
    XT = k.sb("XT", 128, 2 * 4096, BF16)
    YT = k.sb("YT", 128, 2 * 4096, BF16)
    YNF = k.sb("YNF", 128, 32 * CH, BF16)
    R = k.sb("R", 128, 4096, F32)
    RB = k.sb("RB", 128, 4096, BF16)
    WSTR = k.sb("WSTR", 128, 2 * 4096, BF16)
    DTJ = k.sb("DTJ", 64, 6 * 256, F32)
    NACS = k.sb("NACS", 64, 256, F32)
    DECF = k.sb("DECF", 64, 256, F32)
    DTD = k.sb("DTD", 64, 256, F32)
    ACSF = k.sb("ACSF", 1, 8 * 256, F32)
    NACSF = k.sb("NACSF", 1, 8 * 256, F32)
    DTT = k.sb("DTT", 128, 2 * 64, F32)
    DTDT = k.sb("DTDT", 128, 2 * 64, F32)
    EXAT = k.sb("EXAT", 128, 2 * 64, F32)
    ECER = k.sb("ECER", 1, 64, F32)
    ECEX = k.sb("ECEX", 128, 64, F32)
    DBC = k.sb("DBC", 128, 64, F32)
    NWB = k.sb("NWB", 128, 4096, BF16)
    VARC = k.sb("VARC", 128, 4, F32)
    CWK = k.sb("CWK", 128, 4 * 34, F32)
    CB = k.sb("CB", 128, 34, F32)
    AUXC = k.sb("AUXC", 64, 2, F32)
    DROW = k.sb("DROW", 1, 64, F32)
    ACOL = k.sb("ACOL", 64, 2, F32)
    MASKSEG = k.sb("MASKSEG", 128, 128, F32)
    IDF = k.sb("IDF", 128, 128, F32)
    IDB = k.sb("IDB", 128, 128, BF16)
    CONSTS = k.sb("CONSTS", 128, 2, F32)
    ONES = k.sb("ONES", 128, 1, F32)
    BONES = k.sb("BONES", 128, 1, BF16)
    ONESROW = k.sb("ONESROW", 1, 128, F32)
    ROW = k.sb("ROW", 1, 2 * lnw, F32)
    SQ = k.sb("SQ", 128, 2 * lnw, BF16)
    TMP2 = SQ
    BTOK = k.sb("BTOK", 128, 256, BF16)

    ps = [k.ps(f"ps{i}", 128, 512) for i in range(6)]
    psta = k.ps("psta", 128, 128, BF16)
    pstb = k.ps("pstb", 128, 128, BF16)

    s_x = k.sem("s_x", 16)
    s_nwq = k.sem("s_nwq", 16)
    s_xb = k.sem("s_xb", 16)
    s_st = k.sem("s_st", 16)
    s_wsc = k.sem("s_wsc", 16)
    s_w = [k.sem("s_w0", 16), k.sem("s_w1", 16)]
    s_fl = k.sem("s_fl", 16)
    s_g = k.sem("s_g", 1)
    s_pe = k.sem("s_pe", 1)
    s_act = k.sem("s_act", 1)
    s_dve = k.sem("s_dve", 1)

    def bar3(eng):
        eng.wait_ge(s_pe.h, s_pe.cnt)
        eng.wait_ge(s_act.h, s_act.cnt)
        eng.wait_ge(s_dve.h, s_dve.cnt)

    k.begin()

    def g_init(g):
        s_wsc.inc(g.dma_start(out=CWK[:], in_=cwk_d))
        s_wsc.inc(g.dma_start(out=CB[:], in_=cb_d))
        s_wsc.inc(g.dma_start(out=AUXC[:], in_=auxc_d))
        s_wsc.inc(g.dma_start(out=DROW[:], in_=drow_d))
        g.memset(ONES[:], 1.0)
        g.memset(BONES[:], 1.0)
        g.memset(ONESROW[:], 1.0)
        g.memset(CONSTS[:, 0:1], 1e-5)
        g.memset(CONSTS[:, 1:2], 1.0)
        g.memset(IDF[:], 1.0)
        g.memset(IDB[:], 1.0)
        g.memset(MASKSEG[:], 0.0)
        g.memset(R[:], 0.0)
        g.memset(RB[:], 0.0)
        s_g.inc(g.memset(HALO[:], 0.0))
        g.wait_ge(s_g.h, s_g.cnt)
        g.affine_select(IDF[:], IDF[:], pattern=[[1, 128]], compare_op=ALU.is_equal,
                        fill=0.0, base=0, channel_multiplier=-1)
        g.affine_select(IDB[:], IDB[:], pattern=[[1, 128]], compare_op=ALU.is_equal,
                        fill=0.0, base=0, channel_multiplier=-1)
        s_g.inc(g.affine_select(MASKSEG[:], MASKSEG[:], pattern=[[1, 128]], compare_op=ALU.is_ge,
                                fill=-1e9, base=0, channel_multiplier=-1))
    k.g(g_init)
    g_done = s_g.cnt
    aux_in = s_wsc.cnt



    def s_acol(s):
        s.wait_ge(s_wsc.h, aux_in)
        s_act.inc(s.activation(ACOL[:, 1:2], AUXC[:, 1:2], AF.Exp))
    k.s(s_acol)
    a1 = s_act.cnt

    def v_acol(v):
        v.wait_ge(s_act.h, a1)
        s_dve.inc(v.tensor_scalar(ACOL[:, 0:1], ACOL[:, 1:2], -1.0, None, ALU.mult))
    k.v(v_acol)

    def t_dbc(t):
        t.wait_ge(s_wsc.h, aux_in)
        t.wait_ge(s_g.h, g_done)
        s_pe.inc(t.matmul(ps[0][:, 0:64], ONESROW[0:1, :], DROW[0:1, :], start=True, stop=True))
    k.t(t_dbc)
    pe_dbc = s_pe.cnt

    def s_dbc(s):
        s.wait_ge(s_pe.h, pe_dbc)
        s_act.inc(s.copy(DBC[:], ps[0][:, 0:64]))
    k.s(s_dbc)

    for j in range(8):
        def y_nw(y, j=j):
            if j > 0:
                y.wait_ge(s_pe.h, s_pe.cnt)
            s_nwq.inc(y.dma_start(out=ROW[0:1, 0:512], in_=nwr_d[0:1, j * 512:(j + 1) * 512]))
        k.y(y_nw)
        nw_in = s_nwq.cnt

        def t_nw(t, j=j):
            t.wait_ge(s_nwq.h, nw_in)
            t.wait_ge(s_g.h, g_done)
            t.wait_ge(s_act.h, s_act.cnt)
            s_pe.inc(t.matmul(ps[1][:, 0:512], ONESROW[0:1, :], ROW[0:1, 0:512], start=True, stop=True))
        k.t(t_nw)
        pe_nw = s_pe.cnt

        def s_nw(s, j=j):
            s.wait_ge(s_pe.h, pe_nw)
            s_act.inc(s.copy(NWB[:, j * 512:(j + 1) * 512], ps[1][:, 0:512]))
        k.s(s_nw)
    def t_nwfin(t):
        t.wait_ge(s_act.h, s_act.cnt)
    k.t(t_nwfin)

    oV, oA, oE, oL, oDT, oACS = [j * 256 for j in range(6)]
    oMX, oDA = oA, oE

    st_done = {}
    for c2 in range(nchunks):
        own = c2 >= own_lo
        oc2 = c2 - own_lo
        def y_load(y):
            if c2 > 0:
                y.wait_ge(s_pe.h, s_pe.cnt)
                y.wait_ge(s_dve.h, s_dve.cnt)
                y.wait_ge(s_st.h, s_st.cnt)
            for c in range(NCH):
                s_x.inc(y.dma_start(out=X[:, c * lnw:(c + 1) * lnw],
                                    in_=xfull[c * 128:(c + 1) * 128, c2 * CH:(c2 + 1) * CH]))
        k.y(y_load)
        x_loaded = s_x.cnt

        def t_gate(t):
            t.wait_ge(s_x.h, x_loaded)
            t.wait_ge(s_g.h, g_done)
            bar3(t)
        k.t(t_gate)

        class RowPair:
            def __getitem__(self, idx):
                part, free = idx
                lo, hi = free.start, free.stop
                if hi <= lnw:
                    return ps[0][part, lo:hi]
                return ps[2][part, lo - lnw:hi - lnw]
        Xc = [X[:, c * lnw:(c + 1) * lnw] for c in range(NCH)]
        XNc = [XN[:, c * lnw:(c + 1) * lnw] for c in range(NCH)]
        emit_ln(k, D, CONSTS[0:1, 0:1], Xc, XNc, lnw, RowPair(), ps[1], SQ, ROW,
                ONES, BONES, ONESROW, s_pe, s_act, s_dve)

        wload = {}

        def g_wx(g, ch, buf):
            wd = 128 if ch < 34 else 64
            src = win_d[:, 4096 + ch * 128:4096 + ch * 128 + wd]
            s_w[buf].inc(g.dma_start(
                out=WSTR[:, buf * 4096:buf * 4096 + 16 * wd].rearrange("p (c m) -> p c m", c=16),
                in_=src.rearrange("(c p) m -> p c m", p=128)))

        def g_wx0(g):
            g.wait_ge(s_pe.h, s_pe.cnt)
            g_wx(g, 0, 0)
        k.g(g_wx0)
        wload[0] = (s_w[0], s_w[0].cnt)
        peP2 = s_pe.cnt
        dveP2 = s_dve.cnt
        ln_done = s_dve.cnt
        for ch in range(35):
            buf = ch % 2
            wd = 128 if ch < 34 else 64
            if ch + 1 < 35:
                def g_wn(g):
                    if ch >= 1:
                        g.wait_ge(s_pe.h, peP2 + ch)
                    g_wx(g, ch + 1, 1 - buf)
                k.g(g_wn)
                wload[ch + 1] = (s_w[1 - buf], s_w[1 - buf].cnt)

            def t_px(t):
                sm, cv = wload[ch]
                t.wait_ge(sm.h, cv)
                if ch == 0:
                    t.wait_ge(s_dve.h, ln_done)
                if ch >= 2:
                    t.wait_ge(s_dve.h, dveP2 + ch - 1)
                acc_mm(t, ps[3 + buf][0:wd, 0:CH],
                       [(WSTR[:, buf * 4096 + c * wd:buf * 4096 + (c + 1) * wd],
                         XN[:, c * lnw:(c + 1) * lnw]) for c in range(NCH)], s_pe)
            k.t(t_px)
            pe_ch = s_pe.cnt

            def v_px(v):
                v.wait_ge(s_pe.h, pe_ch)
                if ch < 34:
                    s_dve.inc(v.tensor_copy(PRE[:, ch * 259 + 3:(ch + 1) * 259], ps[3 + buf][:, 0:CH]))
                else:
                    s_dve.inc(v.tensor_copy(DTJ[:, oV:oV + 256], ps[3 + buf][0:64, 0:CH]))
            k.v(v_px)
        def v_halo(v):
            v.wait_ge(s_dve.h, s_dve.cnt)
            s_dve.inc(v.tensor_copy(
                PRE[:].rearrange("p (c w) -> p c w", c=34)[:, :, 0:3],
                HALO[:].rearrange("p (c w) -> p c w", c=34)))
        k.v(v_halo)

        dve_c3 = s_dve.cnt
        act_c3 = s_act.cnt

        def v_conv(v):
            bar3(v)
            for ch in range(34):
                base = ch * 259
                cb2 = (ch % 2) * 256
                if ch >= 2:
                    v.wait_ge(s_act.h, act_c3 + ch - 1)
                s_dve.inc(v.tensor_scalar(CONVT[:, cb2:cb2 + 256], PRE[:, base:base + 256],
                                          CWK[:, ch:ch + 1], None, ALU.mult))
                for kk in range(1, 4):
                    v.wait_ge(s_dve.h, s_dve.cnt)
                    s_dve.inc(v.scalar_tensor_tensor(CONVT[:, cb2:cb2 + 256],
                                                     PRE[:, base + kk:base + kk + 256],
                                                     CWK[:, kk * 34 + ch:kk * 34 + ch + 1],
                                                     CONVT[:, cb2:cb2 + 256], ALU.mult, ALU.add))
        k.v(v_conv)

        def s_conv(s):
            for ch in range(34):
                cb2 = (ch % 2) * 256
                s.wait_ge(s_dve.h, dve_c3 + 4 * (ch + 1))
                s_act.inc(s.activation(XBC[:, ch * CH:(ch + 1) * CH], CONVT[:, cb2:cb2 + 256],
                                       AF.Silu, bias=CB[:, ch:ch + 1]))
        k.s(s_conv)
        conv_done = s_act.cnt

        def v_halo2(v):
            v.wait_ge(s_dve.h, s_dve.cnt)
            v.wait_ge(s_act.h, conv_done)
            s_dve.inc(v.tensor_copy(
                HALO[:].rearrange("p (c w) -> p c w", c=34),
                PRE[:].rearrange("p (c w) -> p c w", c=34)[:, :, 256:259]))
        k.v(v_halo2)

        def s_dt1(s):
            bar3(s)
            s_act.inc(s.activation(DTJ[0:64, oV:oV + 256], DTJ[0:64, oV:oV + 256],
                                   AF.Identity, bias=AUXC[:, 0:1]))
            s.wait_ge(s_act.h, s_act.cnt)
            s_act.inc(s.activation(DTJ[0:64, oA:oA + 256], DTJ[0:64, oV:oV + 256], AF.Abs))
            s.wait_ge(s_act.h, s_act.cnt)
            s_act.inc(s.activation(DTJ[0:64, oE:oE + 256], DTJ[0:64, oA:oA + 256],
                                   AF.Exp, scale=-1.0))
            s.wait_ge(s_act.h, s_act.cnt)
            s_act.inc(s.activation(DTJ[0:64, oL:oL + 256], DTJ[0:64, oE:oE + 256],
                                   AF.Ln, bias=CONSTS[0:64, 1:2]))
        k.s(s_dt1)
        dt1 = s_act.cnt

        def v_dt2(v):
            v.wait_ge(s_act.h, dt1)
            s_dve.inc(v.tensor_scalar(DTJ[0:64, oMX:oMX + 256], DTJ[0:64, oV:oV + 256],
                                      0.0, None, ALU.max))
            v.wait_ge(s_dve.h, s_dve.cnt)
            s_dve.inc(v.tensor_tensor(DTJ[0:64, oDT:oDT + 256], DTJ[0:64, oMX:oMX + 256],
                                      DTJ[0:64, oL:oL + 256], ALU.add))
            v.wait_ge(s_dve.h, s_dve.cnt)
            s_dve.inc(v.tensor_scalar(DTJ[0:64, oDT:oDT + 256], DTJ[0:64, oDT:oDT + 256],
                                      100.0, None, ALU.min))
            v.wait_ge(s_dve.h, s_dve.cnt)
            s_dve.inc(v.tensor_scalar(DTJ[0:64, oDA:oDA + 256], DTJ[0:64, oDT:oDT + 256],
                                      ACOL[:, 0:1], None, ALU.mult))
            v.wait_ge(s_dve.h, s_dve.cnt)
            ones_ap = ONES[0:64, 0:1].unsqueeze(2).broadcast_to([64, 1, 256]).squeeze(1)
            s_dve.inc(v.tensor_tensor_scan(DTJ[0:64, oACS:oACS + 256], ones_ap,
                                           DTJ[0:64, oDA:oDA + 256], 0.0, ALU.mult, ALU.add))
            v.wait_ge(s_dve.h, s_dve.cnt)
            s_dve.inc(v.tensor_scalar(NACS[:], DTJ[0:64, oACS:oACS + 256], -1.0, None, ALU.mult))
            v.wait_ge(s_dve.h, s_dve.cnt)
            s_dve.inc(v.tensor_scalar(DECF[:], DTJ[0:64, oACS:oACS + 256],
                                      DTJ[0:64, oACS + 255:oACS + 256], -1.0,
                                      ALU.subtract, ALU.mult))
        k.v(v_dt2)
        dt2 = s_dve.cnt

        def s_dt3(s):
            s.wait_ge(s_dve.h, dt2)
            s_act.inc(s.activation(DECF[:], DECF[:], AF.Exp))
        k.s(s_dt3)
        dt3 = s_act.cnt

        def v_dt4(v):
            v.wait_ge(s_act.h, dt3)
            s_dve.inc(v.tensor_tensor(DTD[:], DTJ[0:64, oDT:oDT + 256], DECF[:], ALU.mult))
        k.v(v_dt4)

        dt4_done = s_dve.cnt

        def t_tr(t):
            bar3(t)
            for lt in range(2):
                t.transpose(ps[5][:, lt * 64:(lt + 1) * 64],
                            DTJ[0:64, oDT + lt * 128:oDT + (lt + 1) * 128], IDF[0:64, 0:64])
                t.transpose(ps[5][:, 128 + lt * 64:128 + (lt + 1) * 64],
                            DTD[:, lt * 128:(lt + 1) * 128], IDF[0:64, 0:64])
                s_pe.inc(t.transpose(ps[3][:, lt * 64:(lt + 1) * 64],
                                     DTJ[0:64, oACS + lt * 128:oACS + (lt + 1) * 128], IDF[0:64, 0:64]))
        k.t(t_tr)
        pe_tr1 = s_pe.cnt

        def v_tr1(v):
            v.wait_ge(s_pe.h, pe_tr1)
            s_dve.inc(v.tensor_copy(DTT[:], ps[5][:, 0:128]))
            s_dve.inc(v.tensor_copy(DTDT[:], ps[5][:, 128:256]))
        k.v(v_tr1)

        def t_ce(t):
            s_pe.inc(t.transpose(ps[3][0:1, 192:256], DTJ[0:64, oACS + 255:oACS + 256], IDF[0:64, 0:64]))
        k.t(t_ce)
        pe_ce = s_pe.cnt

        def s_tr1(s):
            s.wait_ge(s_pe.h, pe_tr1)
            s_act.inc(s.activation(EXAT[:], ps[3][:, 0:128], AF.Exp))
            s.wait_ge(s_pe.h, pe_ce)
            s_act.inc(s.activation(ECER[:], ps[3][0:1, 192:256], AF.Exp))
        k.s(s_tr1)
        ece1 = s_act.cnt

        def t_ecex(t):
            t.wait_ge(s_act.h, ece1)
            s_pe.inc(t.matmul(ps[3][:, 128:192], ONESROW[0:1, :], ECER[:], start=True, stop=True))
        k.t(t_ecex)
        pe_ecex = s_pe.cnt

        def s_ecex(s):
            s.wait_ge(s_pe.h, pe_ecex)
            s_act.inc(s.copy(ECEX[:], ps[3][:, 128:192]))
        k.s(s_ecex)

        dve_xt = s_dve.cnt
        pe_xt = s_pe.cnt

        def t_xt(t):
            t.wait_ge(s_act.h, conv_done)
            bar3(t)
            n = 0
            for ch in range(33):
                for lt in range(2):
                    bank = psta if n % 2 == 0 else pstb
                    if n >= 2:
                        t.wait_ge(s_dve.h, dve_xt + n - 1)
                    s_pe.inc(t.transpose(bank[:], XBC[:, ch * CH + lt * 128:ch * CH + (lt + 1) * 128], IDB[:]))
                    n += 1
        k.t(t_xt)

        def v_xt(v):
            n = 0
            for ch in range(33):
                for lt in range(2):
                    bank = psta if n % 2 == 0 else pstb
                    v.wait_ge(s_pe.h, pe_xt + n + 1)
                    if ch < 32:
                        dst = XT[:, lt * 4096 + ch * 128:lt * 4096 + (ch + 1) * 128]
                    else:
                        dst = BTOK[:, lt * 128:(lt + 1) * 128]
                    s_dve.inc(v.tensor_copy(dst, bank[:]))
                    n += 1
        k.v(v_xt)

        def t_cbt(t):
            bar3(t)
            B_ = XBC[:, 32 * CH:33 * CH]
            C_ = XBC[:, 33 * CH:34 * CH]
            t.matmul(ps[5][:, 0:128], B_[:, 0:128], C_[:, 0:128], start=True, stop=True)
            t.matmul(ps[5][:, 128:256], B_[:, 0:128], C_[:, 128:256], start=True, stop=True)
            s_pe.inc(t.matmul(ps[5][:, 256:384], B_[:, 128:256], C_[:, 128:256], start=True, stop=True))
        k.t(t_cbt)
        pe_cbt = s_pe.cnt

        def v_cbt(v):
            v.wait_ge(s_pe.h, pe_cbt)
            s_dve.inc(v.tensor_copy(CBTS, ps[5][:, 0:384]))
        k.v(v_cbt)

        def seg_cols(j):
            return slice(j * 128, (j + 1) * 128)

        if own:
            def t_p7gate(t):
                bar3(t)
            k.t(t_p7gate)
            p7_dve = s_dve.cnt
            exp_cnt = {}
            m1_cnt = {}
            ymm_cnt = {}
            for hg in range(8):
                evac_base = s_dve.cnt
                last_seg_pe = s_pe.cnt

                def g_flat(g, hg=hg):
                    g.wait_ge(s_dve.h, p7_dve)
                    if hg > 0:
                        g.wait_ge(s_pe.h, last_seg_pe)
                    s_fl.inc(g.dma_start(out=ACSF[:].rearrange("o (h l) -> o h l", h=8),
                                         in_=DTJ[hg * 8:(hg + 1) * 8, oACS:oACS + 256].unsqueeze(1)))
                    s_fl.inc(g.dma_start(out=NACSF[:].rearrange("o (h l) -> o h l", h=8),
                                         in_=NACS[hg * 8:(hg + 1) * 8, :].unsqueeze(1)))
                k.g(g_flat)
                flat_hg = s_fl.cnt

                for hh in range(8):
                    h = hg * 8 + hh
                    hb = hh * 256
                    segbank = ps[4] if h % 2 == 0 else ps[5]

                    def t_seg(t):
                        if hh == 0:
                            t.wait_ge(s_fl.h, flat_hg)
                        if h >= 2:
                            t.wait_ge(s_act.h, exp_cnt[h - 2])
                        for j, (sh, lh) in enumerate(((0, 0), (0, 1), (1, 1))):
                            t.matmul(segbank[:, seg_cols(j)],
                                     NACSF[0:1, hb + sh * 128:hb + (sh + 1) * 128],
                                     ONESROW[0:1, :], start=True, stop=False)
                            ins = t.matmul(segbank[:, seg_cols(j)], ONESROW[0:1, :],
                                           ACSF[0:1, hb + lh * 128:hb + (lh + 1) * 128],
                                           start=False, stop=(sh != lh))
                            if sh == lh:
                                ins = t.matmul(segbank[:, seg_cols(j)], IDF[:], MASKSEG[:],
                                               start=False, stop=True)
                        s_pe.inc(ins)
                    k.t(t_seg)
                    pe_seg = s_pe.cnt

                    def s_exp(s):
                        s.wait_ge(s_pe.h, pe_seg)
                        if h >= 1:
                            s.wait_ge(s_dve.h, m1_cnt[h - 1])
                        s_act.inc(s.activation(LSEG, segbank[:, 0:384], AF.Exp))
                    k.s(s_exp)
                    exp_cnt[h] = s_act.cnt

                    def v_m1(v):
                        v.wait_ge(s_act.h, exp_cnt[h])
                        if h >= 1:
                            v.wait_ge(s_pe.h, ymm_cnt[h - 1])
                        s_dve.inc(v.scalar_tensor_tensor(M1S[:, 0:256], LSEG[:, 0:256],
                                                         DTT[:, h:h + 1], CBTS[:, 0:256],
                                                         ALU.mult, ALU.mult))
                        v.wait_ge(s_dve.h, s_dve.cnt)
                        s_dve.inc(v.scalar_tensor_tensor(M1S[:, 256:384], LSEG[:, 256:384],
                                                         DTT[:, 64 + h:65 + h], CBTS[:, 256:384],
                                                         ALU.mult, ALU.mult))
                    k.v(v_m1)
                    m1_cnt[h] = s_dve.cnt

                    def t_y(t):
                        t.wait_ge(s_dve.h, m1_cnt[h])
                        if hg > 0 and hh == 0:
                            t.wait_ge(s_dve.h, evac_base)
                        hc = slice(hh * 64, (hh + 1) * 64)
                        t.matmul(ps[0][:, hc], M1S[:, 0:128], XT[:, 0 * 4096 + h * 64:0 * 4096 + (h + 1) * 64],
                                 start=True, stop=True)
                        t.matmul(ps[1][:, hc], M1S[:, 128:256], XT[:, 0 * 4096 + h * 64:0 * 4096 + (h + 1) * 64],
                                 start=True, stop=False)
                        t.matmul(ps[1][:, hc], M1S[:, 256:384], XT[:, 1 * 4096 + h * 64:1 * 4096 + (h + 1) * 64],
                                 start=False, stop=True)
                        t.matmul(ps[2][:, hc], XBC[:, 33 * CH:33 * CH + 128], RB[:, h * 64:(h + 1) * 64],
                                 start=True, stop=True)
                        s_pe.inc(t.matmul(ps[3][:, hc], XBC[:, 33 * CH + 128:34 * CH], RB[:, h * 64:(h + 1) * 64],
                                          start=True, stop=True))
                    k.t(t_y)
                    ymm_cnt[h] = s_pe.cnt
                pe_hg_end = s_pe.cnt

                def v_evac(v):
                    v.wait_ge(s_pe.h, pe_hg_end)
                    for lh in range(2):
                        ydo = ps[0] if lh == 0 else ps[1]
                        yof = ps[2] if lh == 0 else ps[3]
                        exa = EXAT[:, lh * 64 + hg * 8:lh * 64 + (hg + 1) * 8]
                        exap = exa.unsqueeze(2).broadcast_to([128, 8, 64])
                        dbp = DBC[:, hg * 8:(hg + 1) * 8].unsqueeze(2).broadcast_to([128, 8, 64])
                        xsl = XT[:, lh * 4096 + hg * 512:lh * 4096 + (hg + 1) * 512]
                        ysl = YT[:, lh * 4096 + hg * 512:lh * 4096 + (hg + 1) * 512]
                        r3 = lambda a: a.rearrange("p (a b) -> p a b", a=8)
                        v.wait_ge(s_dve.h, s_dve.cnt)
                        s_dve.inc(v.tensor_tensor(r3(TMPY[:]), r3(yof[:, 0:512]), exap, ALU.mult))
                        s_dve.inc(v.tensor_tensor(r3(TMP2[:]), r3(xsl), dbp, ALU.mult))
                        v.wait_ge(s_dve.h, s_dve.cnt)
                        s_dve.inc(v.tensor_tensor(TMPY[:], TMPY[:], ydo[:, 0:512], ALU.add))
                        v.wait_ge(s_dve.h, s_dve.cnt)
                        s_dve.inc(v.tensor_tensor(ysl, TMPY[:], TMP2[:], ALU.add))
                k.v(v_evac)

        def v_xdtd(v):
            bar3(v)
            for lt in range(2):
                dtp = DTDT[:, lt * 64:(lt + 1) * 64].unsqueeze(2).broadcast_to([128, 64, 64])
                s_dve.inc(v.tensor_tensor(
                    XT[:, lt * 4096:(lt + 1) * 4096].rearrange("p (a b) -> p a b", a=64),
                    XT[:, lt * 4096:(lt + 1) * 4096].rearrange("p (a b) -> p a b", a=64),
                    dtp, ALU.mult))
        k.v(v_xdtd)
        xdtd_done = s_dve.cnt

        for hg in range(8):
            def t_sc(t):
                t.wait_ge(s_dve.h, s_dve.cnt)
                for hh in range(8):
                    h = hg * 8 + hh
                    t.matmul(ps[4][:, hh * 64:(hh + 1) * 64], BTOK[:, 0:128],
                             XT[:, 0 * 4096 + h * 64:0 * 4096 + (h + 1) * 64], start=True, stop=False)
                    s_pe.inc(t.matmul(ps[4][:, hh * 64:(hh + 1) * 64], BTOK[:, 128:256],
                                      XT[:, 1 * 4096 + h * 64:1 * 4096 + (h + 1) * 64],
                                      start=False, stop=True))
            k.t(t_sc)
            pe_sc = s_pe.cnt

            def v_r(v):
                v.wait_ge(s_pe.h, pe_sc)
                v.wait_ge(s_act.h, s_act.cnt)
                ecp = ECEX[:, hg * 8:(hg + 1) * 8].unsqueeze(2).broadcast_to([128, 8, 64])
                rsl = R[:, hg * 512:(hg + 1) * 512]
                s_dve.inc(v.tensor_tensor(rsl.rearrange("p (a b) -> p a b", a=8),
                                          rsl.rearrange("p (a b) -> p a b", a=8), ecp, ALU.mult))
                v.wait_ge(s_dve.h, s_dve.cnt)
                s_dve.inc(v.tensor_tensor(rsl, rsl, ps[4][:, 0:512], ALU.add))
            k.v(v_r)

        def v_rb(v):
            v.wait_ge(s_dve.h, s_dve.cnt)
            s_dve.inc(v.tensor_copy(RB[:], R[:]))
        k.v(v_rb)

        if own:
            zld = {}

            def g_z(g, zb, buf):
                s_w[buf].inc(g.dma_start(
                    out=WSTR[:, buf * 4096:buf * 4096 + 4096].rearrange("p (c m) -> p c m", c=NCH),
                    in_=win_d[:, zb * 256:(zb + 1) * 256].rearrange("(c p) m -> p c m", p=128)))

            def g_z0(g):
                g.wait_ge(s_pe.h, s_pe.cnt)
                g_z(g, 0, 0)
            k.g(g_z0)
            zld[0] = (s_w[0], s_w[0].cnt)
            peZ = s_pe.cnt
            actZ = s_act.cnt
            for zb in range(16):
                buf = zb % 2
                if zb + 1 < 16:
                    def g_zn(g):
                        if zb >= 1:
                            g.wait_ge(s_pe.h, peZ + 2 * zb)
                        g_z(g, zb + 1, 1 - buf)
                    k.g(g_zn)
                    zld[zb + 1] = (s_w[1 - buf], s_w[1 - buf].cnt)

                def t_z(t):
                    sm, cv = zld[zb]
                    t.wait_ge(sm.h, cv)
                    if zb == 0:
                        t.wait_ge(s_dve.h, s_dve.cnt)
                    if zb >= 1:
                        t.wait_ge(s_act.h, actZ + 2 * zb)
                    for tt in range(2):
                        acc_mm(t, ps[3 + tt][:, 0:256],
                               [(XN[:, c * lnw + tt * 128:c * lnw + (tt + 1) * 128],
                                 WSTR[:, buf * 4096 + c * 256:buf * 4096 + (c + 1) * 256])
                                for c in range(NCH)], s_pe)
                k.t(t_z)
                pe_z = s_pe.cnt

                def s_z(s):
                    for tt in range(2):
                        s.wait_ge(s_pe.h, pe_z - 1 + tt)
                        s_act.inc(s.activation(ZS[:, tt * 4096 + zb * 256:tt * 4096 + (zb + 1) * 256],
                                               ps[3 + tt][:, 0:256], AF.Silu))
                k.s(s_z)

            gt0 = s_dve.cnt

            def v_gate(v):
                bar3(v)
                for lt in range(2):
                    ysl = YT[:, lt * 4096:(lt + 1) * 4096]
                    s_dve.inc(v.tensor_tensor(ysl, ysl, ZS[:, lt * 4096:(lt + 1) * 4096], ALU.mult))
            k.v(v_gate)
            g1 = s_dve.cnt

            def s_var(s):
                s.wait_ge(s_dve.h, g1)
                for lt in range(2):
                    s_act.inc(s.activation(ZS[:, lt * 4096:(lt + 1) * 4096],
                                           YT[:, lt * 4096:(lt + 1) * 4096], AF.Square,
                                           accum_out=VARC[:, lt:lt + 1]))
                s.wait_ge(s_act.h, s_act.cnt)
                s_act.inc(s.activation(VARC[:, 2:4], VARC[:, 0:2], AF.Sqrt,
                                       bias=CONSTS[:, 0:1], scale=1.0 / 4096.0))
            k.s(s_var)
            v1c = s_act.cnt

            def v_norm(v):
                v.wait_ge(s_act.h, v1c)
                s_dve.inc(v.reciprocal(VARC[:, 0:2], VARC[:, 2:4]))
                for lt in range(2):
                    ysl = YT[:, lt * 4096:(lt + 1) * 4096]
                    v.wait_ge(s_dve.h, s_dve.cnt)
                    s_dve.inc(v.tensor_scalar(ysl, ysl, VARC[:, lt:lt + 1], None, ALU.mult))
                    v.wait_ge(s_dve.h, s_dve.cnt)
                    s_dve.inc(v.tensor_tensor(ysl, ysl, NWB[:], ALU.mult))
            k.v(v_norm)

            pe_yt = s_pe.cnt
            dve_yt = s_dve.cnt

            def t_ytr(t):
                bar3(t)
                n = 0
                for fc in range(32):
                    for lt in range(2):
                        bank = psta if n % 2 == 0 else pstb
                        if n >= 2:
                            t.wait_ge(s_dve.h, dve_yt + n - 1)
                        s_pe.inc(t.transpose(bank[:], YT[:, lt * 4096 + fc * 128:lt * 4096 + (fc + 1) * 128], IDB[:]))
                        n += 1
            k.t(t_ytr)

            def v_ytr(v):
                n = 0
                for fc in range(32):
                    for lt in range(2):
                        bank = psta if n % 2 == 0 else pstb
                        v.wait_ge(s_pe.h, pe_yt + n + 1)
                        s_dve.inc(v.tensor_copy(YNF[:, fc * CH + lt * 128:fc * CH + (lt + 1) * 128], bank[:]))
                        n += 1
            k.v(v_ytr)
            ynf_done = s_dve.cnt

            wol = {}

            def g_wo(g, oc, buf):
                src = wout_d[:, oc * 128:(oc + 1) * 128]
                for kg in range(2):
                    s_w[buf].inc(g.dma_start(
                        out=WSTR[:, buf * 4096 + kg * 2048:buf * 4096 + (kg + 1) * 2048]
                            .rearrange("p (c m) -> p c m", c=16),
                        in_=src[kg * 2048:(kg + 1) * 2048, :].rearrange("(c p) m -> p c m", p=128)))

            def g_wo0(g):
                g.wait_ge(s_pe.h, s_pe.cnt)
                g_wo(g, 0, 0)
            k.g(g_wo0)
            wol[0] = (s_w[0], s_w[0].cnt)
            peW = s_pe.cnt
            dveW = s_dve.cnt
            stW = s_st.cnt
            for oc in range(16):
                buf = oc % 2
                if oc + 1 < 16:
                    def g_won(g):
                        if oc >= 1:
                            g.wait_ge(s_pe.h, peW + oc)
                        g_wo(g, oc + 1, 1 - buf)
                    k.g(g_won)
                    wol[oc + 1] = (s_w[1 - buf], s_w[1 - buf].cnt)

                def y_xb(y):
                    if oc == 0:
                        y.wait_ge(s_x.h, s_x.cnt)
                        y.wait_ge(s_dve.h, ynf_done)
                    if oc >= 2:
                        y.wait_ge(s_st.h, stW + 16 * oc)
                    s_xb.inc(y.dma_start(out=XB[:, buf * 256:(buf + 1) * 256],
                                        in_=xfull[oc * 128:(oc + 1) * 128, c2 * CH:(c2 + 1) * CH]))
                k.y(y_xb)
                xb_in = s_xb.cnt

                def t_wo(t):
                    sm, cv = wol[oc]
                    t.wait_ge(sm.h, cv)
                    if oc == 0:
                        t.wait_ge(s_dve.h, ynf_done)
                    if oc >= 2:
                        t.wait_ge(s_dve.h, dveW + oc - 1)
                    acc_mm(t, ps[buf][:, 0:256],
                           [(WSTR[:, buf * 4096 + c * 128:buf * 4096 + (c + 1) * 128],
                             YNF[:, c * CH:(c + 1) * CH]) for c in range(32)], s_pe)
                k.t(t_wo)
                pe_oc = s_pe.cnt

                def v_wo(v):
                    v.wait_ge(s_pe.h, pe_oc)
                    v.wait_ge(s_xb.h, xb_in)
                    s_dve.inc(v.tensor_tensor(XB[:, buf * 256:(buf + 1) * 256],
                                              XB[:, buf * 256:(buf + 1) * 256],
                                              ps[buf][:, 0:256], ALU.add))
                k.v(v_wo)
                dve_oc = s_dve.cnt

                def y_sto(y):
                    y.wait_ge(s_dve.h, dve_oc)
                    s_st.inc(y.dma_start(out=xout[oc * 128:(oc + 1) * 128, oc2 * CH:(oc2 + 1) * CH],
                                         in_=XB[:, buf * 256:(buf + 1) * 256]))
                k.y(y_sto)

    def y_fin(y):
        y.wait_ge(s_st.h, s_st.cnt)
    k.y(y_fin)

    nc._dbg = dict(XN=XN, XBC=XBC, YT=YT, YNF=YNF, R=R, ZS=ZS, XT=XT, DTJ=DTJ, ACSF=ACSF)
    k.st.close()
    return nc


def l1attn_inmaps(xs, inputs):
    cw = inputs["L1_attn_conv_w"]
    cwk = np.ascontiguousarray(cw.reshape(4, 34, 128).transpose(2, 0, 1).reshape(128, 4 * 34))
    cb = np.ascontiguousarray(inputs["L1_attn_conv_b"].reshape(34, 128).T)
    auxc = np.ascontiguousarray(np.stack([inputs["L1_attn_dt_bias"], inputs["L1_attn_A_log"]], axis=1))
    drow = np.ascontiguousarray(inputs["L1_attn_D"].reshape(1, 64))
    nwr = np.ascontiguousarray(inputs["L1_attn_norm_weight"].reshape(1, 4096))
    maps = []
    for c in range(8):
        b = c // 2
        xfull = np.ascontiguousarray(
            np.concatenate([np.asarray(xs[2 * b]), np.asarray(xs[2 * b + 1])], axis=1))
        maps.append({"xfull": xfull, "win": inputs["L1_attn_W_in"], "wout": inputs["L1_attn_W_out"],
                     "cwk": cwk, "cb": cb, "auxc": auxc, "drow": drow, "nwr": nwr})
    return maps



def build_fused(upto=99):
    k = K()
    nc = k.nc
    T, NT, CH = 2048, 1024, 256

    xres_d = nc.dram_tensor("xres", [D, NT], F32, kind="ExternalInput").ap()
    meta_d = nc.dram_tensor("meta", [1, 64], I32, kind="ExternalInput").ap()
    wg_d = nc.dram_tensor("wg", [128, 131072], I8, kind="ExternalInput").ap()
    wu_d = nc.dram_tensor("wu", [128, 131072], I8, kind="ExternalInput").ap()
    wd_d = nc.dram_tensor("wd", [128, 131072], I8, kind="ExternalInput").ap()
    sc_d = nc.dram_tensor("sc", [128, 144], F32, kind="ExternalInput").ap()
    wqkv_d = nc.dram_tensor("wqkv", [128, 49152], F32, kind="ExternalInput").ap()
    wo_d = nc.dram_tensor("wo", [128, 32768], F32, kind="ExternalInput").ap()
    attb_d = nc.dram_tensor("attb", [128, 32], F32, kind="ExternalInput").ap()
    bvr_d = nc.dram_tensor("bvr", [1, 1024], F32, kind="ExternalInput").ap()
    aggw_d = nc.dram_tensor("aggw", [D, D], F32, kind="ExternalInput").ap()
    d1h_d = nc.dram_tensor("d1h", [D, 4096], F32, kind="ExternalInput").ap()
    d2h_d = nc.dram_tensor("d2h", [4096, D], F32, kind="ExternalInput").ap()
    d1br_d = nc.dram_tensor("d1br", [1, 4096], F32, kind="ExternalInput").ap()
    bcw_d = nc.dram_tensor("bcw", [D, D], F32, kind="ExternalInput").ap()
    gw_d = nc.dram_tensor("gw", [128, 32768], F32, kind="ExternalInput").ap()
    f1c_d = nc.dram_tensor("f1c", [128, 128], F32, kind="ExternalInput").ap()
    winx_d = nc.dram_tensor("winx", [128, 37376], F32, kind="ExternalInput").ap()
    winz_d = nc.dram_tensor("winz", [128, 32768], F32, kind="ExternalInput").ap()
    wout_d = nc.dram_tensor("wout", [128, 65536], F32, kind="ExternalInput").ap()
    cwk_d = nc.dram_tensor("cwk", [128, 72], F32, kind="ExternalInput").ap()
    cb_d = nc.dram_tensor("cb", [128, 18], F32, kind="ExternalInput").ap()
    auxc_d = nc.dram_tensor("auxc", [32, 2], F32, kind="ExternalInput").ap()
    drow_d = nc.dram_tensor("drow", [1, 32], F32, kind="ExternalInput").ap()
    nwc_d = nc.dram_tensor("nwc", [128, 32], F32, kind="ExternalInput").ap()
    xout = nc.dram_tensor("xout", [D, NT], F32, kind="ExternalOutput").ap()

    scr_wg = nc.dram_tensor("scr_wg", [128, 131072], BF16, kind="Internal").ap()
    scr_wu = nc.dram_tensor("scr_wu", [128, 131072], BF16, kind="Internal").ap()
    scr_wd = nc.dram_tensor("scr_wd", [128, 131072], BF16, kind="Internal").ap()
    scr_wqkv = nc.dram_tensor("scr_wqkv", [128, 49152], BF16, kind="Internal").ap()
    scr_wo = nc.dram_tensor("scr_wo", [128, 32768], BF16, kind="Internal").ap()
    scr_wout = nc.dram_tensor("scr_wout", [128, 65536], BF16, kind="Internal").ap()
    scr_aggw = nc.dram_tensor("scr_aggw", [D, D], BF16, kind="Internal").ap()
    scr_d1h = nc.dram_tensor("scr_d1h", [D, 4096], BF16, kind="Internal").ap()
    scr_d2h = nc.dram_tensor("scr_d2h", [4096, D], BF16, kind="Internal").ap()
    scr_bcw = nc.dram_tensor("scr_bcw", [D, D], BF16, kind="Internal").ap()
    scr_gw = nc.dram_tensor("scr_gw", [128, 32768], BF16, kind="Internal").ap()
    scr_winx = nc.dram_tensor("scr_winx", [128, 37376], BF16, kind="Internal").ap()
    scr_winz = nc.dram_tensor("scr_winz", [128, 32768], BF16, kind="Internal").ap()
    SH_XN = nc.dram_tensor("sh_xn", [2, D, NT], BF16, kind="Internal", addr_space="Shared").ap()
    SH_Y = nc.dram_tensor("sh_y", [2, 2048, T], BF16, kind="Internal", addr_space="Shared").ap()
    SH_C = nc.dram_tensor("sh_c", [2, 128, 256], F32, kind="Internal", addr_space="Shared").ap()
    SH_F = nc.dram_tensor("sh_f", [2, 16], I32, kind="Internal", addr_space="Shared").ap()

    XR = k.sb("XR", 128, NCH * NT, F32)
    A1 = k.sb("A1", 128, NCH * T, BF16)
    A2 = k.sb("A2", 128, 8192, BF16)
    A3 = k.sb("A3", 128, 4096, F32)
    W1 = k.sb("W1", 128, 8192, BF16)
    W2 = k.sb("W2", 128, 4096, BF16)
    SQ = k.sb("SQ", 128, 1024, BF16)
    ROW = k.sb("ROW", 1, 2112, F32)
    ZROWB = k.sb("ZROWB", 1, 512, BF16)
    CONSTS = k.sb("CONSTS", 128, 4, F32)
    ONES = k.sb("ONES", 128, 1, F32)
    BONES = k.sb("BONES", 128, 1, BF16)
    ONESROW = k.sb("ONESROW", 1, 128, F32)
    BONESROW = k.sb("BONESROW", 1, 128, BF16)
    IDF = k.sb("IDF", 32, 32, F32)
    IDB = k.sb("IDB", 128, 128, BF16)
    MASKSEG = k.sb("MASKSEG", 128, 128, BF16)
    MASKC = k.sb("MASKC", 128, 896, BF16)
    SC = k.sb("SC", 128, 160, F32)
    ATTB = k.sb("ATTB", 128, 32, F32)
    F1C = k.sb("F1C", 128, 128, F32)
    G2B2 = k.sb("G2B2", 128, 32, F32)
    AUXC = k.sb("AUXC", 32, 2, F32)
    ACOL = k.sb("ACOL", 32, 2, F32)
    DROW = k.sb("DROW", 1, 32, F32)
    ECEX = k.sb("ECEX", 128, 32, F32)
    DBC = k.sb("DBC", 128, 32, F32)
    NWC = k.sb("NWC", 128, 32, F32)
    CWK = k.sb("CWK", 128, 72, F32)
    CBc = k.sb("CBc", 128, 18, F32)
    DTJ = k.sb("DTJ", 32, 4 * 256, F32)
    NACS = DTJ[0:32, 256:512]
    DECF = DTJ[0:32, 512:768]
    DTD = DECF
    DTT = k.sb("DTT", 128, 64, F32)
    DTDT = k.sb("DTDT", 128, 64, F32)
    EXAT = k.sb("EXAT", 128, 64, F32)
    BTOK = k.sb("BTOK", 128, 256, BF16)
    VARP = k.sb("VARP", 128, 20, F32)
    TH = k.sb("TH", 128, 16, F32)
    THB = k.sb("THB", 128, 16, BF16)
    TC = k.sb("TC", 128, 16, F32)
    TN = k.sb("TN", 128, 16, BF16)
    HC = k.sb("HC", 128, 32, BF16)
    BCT = k.sb("BCT", 128, 16, F32)
    SQT = k.sb("SQT", 128, 16, BF16)
    META = k.sb("META", 1, 64, I32)
    FL = k.sb("FL", 1, 16, I32)
    ACSF = ROW[0:1, 0:1024]
    NACSF = ROW[0:1, 1024:2048]
    ECER = ROW[0:1, 2048:2080]
    BVR = SQ[0:1, 0:1024]
    XBC = A1[:, 24576:24576 + 4608]
    RB = A1[:, 29184:31232]
    XNFS = A1[:, 0:8192]
    KS_A = A1[:, 8192:10240]
    VS_A = A1[:, 10240:12288]
    PT_A = A1[:, 12288:20480]
    YM_A = A1[:, 20480:22528]
    QS_A = A1[:, 22528:24576]
    YFULL = A1[:, 0:16384]
    XT_S = A1[:, 8192:12288]
    YT_S = A1[:, 12288:16384]
    ZS_S = A1[:, 16384:20480]
    YNFC = A1[:, 20480:24576]
    XNL = A1[:, 0:16384]
    D1BR = A2[0:1, 0:4096]
    GTL = A1[:, 16384:32768]
    HROW = A3[0:1, 0:4096]
    R = A3[:, 0:2048]
    CONVT = A3[:, 2048:2560]
    TMPY = A3[:, 2560:3072]
    TMPZ = A3[:, 3072:3584]
    XB0 = A3[:, 0:512]
    XB1 = A3[:, 512:1024]
    CBTS = A2[:, 0:384]
    LSEG = A2[:, 384:768]
    M1S = A2[:, 768:1152]
    TMP2 = A2[:, 1152:1664]
    PREB = A2[:, 1664:1664 + 18 * 259]

    ps = [k.ps(f"ps{i}", 128, 512) for i in range(6)]
    psta = k.ps("psta", 128, 128, BF16)
    pstb = k.ps("pstb", 128, 128, BF16)

    s_x = k.sem("s_x", 16)
    s_st = k.sem("s_st", 16)
    s_wsc = k.sem("s_wsc", 16)
    s_wa = [k.sem("s_wa0", 16), k.sem("s_wa1", 16)]
    s_wb = [k.sem("s_wb0", 16), k.sem("s_wb1", 16)]
    s_sh = k.sem("s_sh", 16)
    s_fl = k.sem("s_fl", 16)
    s_pl = k.sem("s_pl", 16)
    s_xs = k.sem("s_xs", 16)
    s_cb = k.sem("s_cb", 16)
    s_y = [k.sem("s_y0", 16), k.sem("s_y1", 16)]
    s_g = k.sem("s_g", 1)
    s_pe = k.sem("s_pe", 1)
    s_act = k.sem("s_act", 1)
    s_dve = k.sem("s_dve", 1)

    def bar3(eng):
        eng.wait_ge(s_pe.h, s_pe.cnt)
        eng.wait_ge(s_act.h, s_act.cnt)
        eng.wait_ge(s_dve.h, s_dve.cnt)

    k.begin()

    gstate = {}

    def g_comm_init(g):
        s_wsc.inc(g.dma_start(out=META[:], in_=meta_d))
        g.wait_ge(s_wsc.h, s_wsc.cnt)
        me = g.alloc_register()
        pr = g.alloc_register()
        g.reg_load(me, META[0:1, 0:1])
        g.reg_load(pr, META[0:1, 1:2])
        nonce = g.alloc_register()
        g.reg_load(nonce, META[0:1, 2:3])
        gstate["me"], gstate["pr"], gstate["nonce"] = me, pr, nonce
        gstate["sig"] = 0
        plreg = g.alloc_register()
        g.reg_mov(plreg, 0)
        gstate["plreg"] = plreg
        r_tgt = g.alloc_register()
        r_notdone = g.alloc_register()
        r_fv = g.alloc_register()
        gstate["tgt"], gstate["notdone"], gstate["fv"] = r_tgt, r_notdone, r_fv
    k.g(g_comm_init)

    def emit_poll(g, sig):
        tgt, notdone, fv = gstate["tgt"], gstate["notdone"], gstate["fv"]
        g.reg_load(tgt, META[0:1, 8 + sig:9 + sig])
        g.reg_mov(notdone, 1)
        with g.While(notdone):
            g.reg_add(gstate["plreg"], gstate["plreg"], 16)
            for my in (0, 1):
                with g.If_eq(gstate["pr"], my):
                    g.dma_start(out=FL[:], in_=SH_F[my:my + 1, :]).then_inc(s_pl.h, 16)
            g.wait_ge(s_pl.h, gstate["plreg"])
            g.reg_load(fv, FL[0:1, sig % 16:sig % 16 + 1])
            g.reg_alu(notdone, fv, tgt, ALU.is_lt)

    ALL_SEMS = [s_x, s_st, s_wsc, s_wa[0], s_wa[1], s_wb[0], s_wb[1], s_sh, s_fl, s_pl,
                s_g, s_pe, s_act, s_dve]

    def branch_pair(g, reg, fn):
        saves = {id(sm): sm.cnt for sm in ALL_SEMS}
        with g.If_eq(reg, 0):
            fn(0, g)
        after0 = {id(sm): sm.cnt for sm in ALL_SEMS}
        for sm in ALL_SEMS:
            sm.cnt = saves[id(sm)]
        with g.If_eq(reg, 1):
            fn(1, g)
        for sm in ALL_SEMS:
            assert sm.cnt == after0[id(sm)], "branch sem-count mismatch"

    def emit_send(pairs, producer_sems):
        gstate["sig"] += 1
        sig = gstate["sig"]

        def g_send(g):
            for sem, val in producer_sems:
                g.wait_ge(sem.h, val)
            if pairs:
                def pay(slot, g):
                    for slot_fn, src in pairs:
                        s_sh.inc(g.dma_start(out=slot_fn(slot), in_=src))
                branch_pair(g, gstate["me"], pay)
            g.wait_ge(s_sh.h, s_sh.cnt)

            def flg(slot, g):
                s_sh.inc(g.dma_start(out=SH_F[slot:slot + 1, sig % 16:sig % 16 + 1],
                                     in_=META[0:1, 8 + sig:9 + sig]))
            branch_pair(g, gstate["me"], flg)
            g.wait_ge(s_sh.h, s_sh.cnt)
        k.g(g_send)
        return sig

    def emit_recv(sig, reads, static_slot=None):
        def g_recv(g):
            emit_poll(g, sig)
            if static_slot is not None:
                for slot_fn, dst in reads:
                    s_sh.inc(g.dma_start(out=dst, in_=slot_fn(static_slot)))
            elif reads:
                def rd(slot, g):
                    for slot_fn, dst in reads:
                        s_sh.inc(g.dma_start(out=dst, in_=slot_fn(slot)))
                branch_pair(g, gstate["pr"], rd)
            else:
                s_sh.inc(g.dma_start(out=FL[:], in_=SH_F[0:1, :]))
        k.g(g_recv)
        return s_sh.cnt

    class _PL:
        pass

    def g_init(g):
        s_wsc.inc(g.dma_start(out=SC[:, 0:144], in_=sc_d))
        s_wsc.inc(g.dma_start(out=ATTB[:], in_=attb_d))
        s_wsc.inc(g.dma_start(out=BVR[:], in_=bvr_d))
        s_wsc.inc(g.dma_start(out=F1C[:], in_=f1c_d))
        s_wsc.inc(g.dma_start(out=CWK[:], in_=cwk_d))
        s_wsc.inc(g.dma_start(out=CBc[:], in_=cb_d))
        s_wsc.inc(g.dma_start(out=AUXC[:], in_=auxc_d))
        s_wsc.inc(g.dma_start(out=DROW[:], in_=drow_d))
        s_wsc.inc(g.dma_start(out=NWC[:], in_=nwc_d))
        g.memset(ONES[:], 1.0)
        g.memset(BONES[:], 1.0)
        g.memset(ONESROW[:], 1.0)
        g.memset(BONESROW[:], 1.0)
        g.memset(CONSTS[:, 0:1], 1e-5)
        g.memset(CONSTS[:, 1:2], 1e-6)
        g.memset(CONSTS[:, 2:3], -ATT_C)
        g.memset(CONSTS[:, 3:4], 1.0)
        g.memset(IDF[:], 1.0)
        g.memset(IDB[:], 1.0)
        g.memset(MASKSEG[:], 0.0)
        g.memset(ZROWB[:], 0.0)
        g.memset(RB[:], 0.0)
        s_g.inc(g.memset(MASKC[:], 0.0))
        g.wait_ge(s_g.h, s_g.cnt)
        s_g.inc(g.affine_select(MASKC[:], MASKC[:], pattern=[[1, 896]], compare_op=ALU.is_ge,
                                fill=-1e9, base=-384, channel_multiplier=-1))
        g.affine_select(IDF[:], IDF[:], pattern=[[1, 32]], compare_op=ALU.is_equal,
                        fill=0.0, base=0, channel_multiplier=-1)
        g.affine_select(IDB[:], IDB[:], pattern=[[1, 128]], compare_op=ALU.is_equal,
                        fill=0.0, base=0, channel_multiplier=-1)
        s_g.inc(g.affine_select(MASKSEG[:], MASKSEG[:], pattern=[[1, 128]], compare_op=ALU.is_ge,
                                fill=-1e9, base=0, channel_multiplier=-1))
    k.g(g_init)
    g_done = s_g.cnt
    aux_in = s_wsc.cnt

    def v_init(v):
        v.wait_ge(s_wsc.h, aux_in)
        s_dve.inc(v.tensor_scalar(SC[:, 144:160], SC[:, 128:144], 0.5, None, ALU.mult))
        s_dve.inc(v.tensor_scalar(G2B2[:, 0:16], F1C[:, 32:48], 0.5, None, ALU.mult))
        s_dve.inc(v.tensor_scalar(G2B2[:, 16:32], F1C[:, 48:64], 0.5, None, ALU.mult))
    k.v(v_init)

    def s_ainit(s):
        s.wait_ge(s_wsc.h, aux_in)
        s_act.inc(s.activation(ACOL[:, 1:2], AUXC[:, 1:2], AF.Exp))
    k.s(s_ainit)
    a1c = s_act.cnt

    def v_ainit(v):
        v.wait_ge(s_act.h, a1c)
        s_dve.inc(v.tensor_scalar(ACOL[:, 0:1], ACOL[:, 1:2], -1.0, None, ALU.mult))
    k.v(v_ainit)

    def t_dbc(t):
        t.wait_ge(s_wsc.h, aux_in)
        t.wait_ge(s_g.h, g_done)
        s_pe.inc(t.matmul(ps[0][:, 0:32], ONESROW[0:1, :], DROW[0:1, :], start=True, stop=True))
    k.t(t_dbc)
    pe_dbc = s_pe.cnt

    def s_dbc(s):
        s.wait_ge(s_pe.h, pe_dbc)
        s_act.inc(s.copy(DBC[:], ps[0][:, 0:32]))
    k.s(s_dbc)

    def y_xres(y):
        for c in range(NCH):
            s_x.inc(y.dma_start(out=XR[:, c * NT:(c + 1) * NT],
                                in_=xres_d[c * 128:(c + 1) * 128, :]))
    k.y(y_xres)
    xr_in = s_x.cnt

    def t_xr_gate(t):
        t.wait_ge(s_x.h, xr_in)
    k.t(t_xr_gate)

    SUBLAYER = [0]

    def finish_if(idx):
        if upto == idx:
            def y_out(y):
                y.wait_ge(s_pe.h, s_pe.cnt)
                y.wait_ge(s_dve.h, s_dve.cnt)
                y.wait_ge(s_act.h, s_act.cnt)
                for c in range(NCH):
                    s_st.inc(y.dma_start(out=xout[c * 128:(c + 1) * 128, :],
                                         in_=XR[:, c * NT:(c + 1) * NT]))
                y.wait_ge(s_st.h, s_st.cnt)
            k.y(y_out)
            k.st.close()
            return True
        return False

    def emit_l0ffn(inst=0):
        for st in range(2):
            stw = 512
            Xc = [XR[:, c * NT + st * stw:c * NT + (st + 1) * stw] for c in range(NCH)]
            XNc = [A2[:, c * stw:(c + 1) * stw] for c in range(NCH)]

            def t_g0(t):
                bar3(t)
                t.wait_ge(s_g.h, g_done)
            k.t(t_g0)

            class RowPair:
                def __getitem__(self, idx):
                    part, free = idx
                    lo, hi = free.start, free.stop
                    if hi <= stw:
                        return ps[0][part, lo:hi]
                    return ps[2][part, lo - stw:hi - stw]
            ln_done = emit_ln(k, D, CONSTS[0:1, 0:1], Xc, XNc, stw, RowPair(), ps[1], SQ, ROW,
                              ONES, BONES, ONESROW, s_pe, s_act, s_dve)

            wblk = {}

            def g_wblk(y, fb, buf):
                for mo, mat in enumerate((scr_wg, scr_wu)):
                    s_wa[buf].inc(y.dma_start(
                        out=W1[:, buf * 4096 + mo * 2048:buf * 4096 + (mo + 1) * 2048],
                        in_=mat[:, fb * 2048:(fb + 1) * 2048]))

            def g_w0(y):
                y.wait_ge(s_pe.h, s_pe.cnt)
                y.wait_ge(s_cb.h, CAST_AT["wu0"])
                g_wblk(y, 0, 0)
            k.y(g_w0)
            wblk[0] = s_wa[0].cnt

            peA0 = s_pe.cnt
            actA0 = s_act.cnt
            dveA0 = s_dve.cnt
            for fb in range(64):
                buf = fb % 2
                if fb + 1 < 64:
                    def g_pref(y):
                        y.wait_ge(s_pe.h, peA0 + 2 * fb)
                        if (fb + 1) % 8 == 0:
                            y.wait_ge(s_cb.h, CAST_AT["wu%d" % ((fb + 1) // 8)])
                        g_wblk(y, fb + 1, 1 - buf)
                    k.y(g_pref)
                    wblk[fb + 1] = s_wa[1 - buf].cnt

                def t_A(t):
                    t.wait_ge(s_wa[buf].h, wblk[fb])
                    if fb == 0:
                        t.wait_ge(s_dve.h, ln_done)
                    fc = fb
                    pg = ps[3] if fc % 2 == 0 else ps[5]
                    pu = ps[4] if fc % 2 == 0 else ps[2]
                    if fc >= 2:
                        t.wait_ge(s_act.h, actA0 + fc - 1)
                        t.wait_ge(s_dve.h, dveA0 + fc - 1)
                    base = buf * 4096
                    acc_mm(t, pg[:, 0:stw],
                           [(W1[:, base + kc * 128:base + (kc + 1) * 128],
                             XNc[kc]) for kc in range(NCH)], s_pe)
                    acc_mm(t, pu[:, 0:stw],
                           [(W1[:, base + 2048 + kc * 128:base + 2048 + (kc + 1) * 128],
                             XNc[kc]) for kc in range(NCH)], s_pe)
                k.t(t_A)

                def s_A(sc):
                    fc = fb
                    pg = ps[3] if fc % 2 == 0 else ps[5]
                    sc.wait_ge(s_pe.h, peA0 + 2 * fb + 1)
                    if fc >= 2:
                        sc.wait_ge(s_dve.h, dveA0 + fc - 1)
                    s_act.inc(sc.activation(SQ[:, (fc % 2) * 512:(fc % 2) * 512 + 512],
                                            pg[:, 0:stw], AF.Silu, scale=SC[:, fc:fc + 1]))
                k.s(s_A)

                def v_A(v):
                    fc = fb
                    pu = ps[4] if fc % 2 == 0 else ps[2]
                    v.wait_ge(s_pe.h, peA0 + 2 * fb + 2)
                    v.wait_ge(s_act.h, actA0 + fc + 1)
                    s_dve.inc(v.scalar_tensor_tensor(
                        A1[:, fc * stw:(fc + 1) * stw], pu[:, 0:stw], SC[:, 64 + fc:64 + fc + 1],
                        SQ[:, (fc % 2) * 512:(fc % 2) * 512 + 512], ALU.mult, ALU.mult))
                k.v(v_A)
            fuse_done = s_dve.cnt

            peB0 = s_pe.cnt
            evac_cnt = s_dve.cnt
            for ocg in range(4):
                wg8 = {}

                def g_wd0(y, ocg=ocg):
                    y.wait_ge(s_pe.h, s_pe.cnt)
                    if ocg == 0:
                        y.wait_ge(s_cb.h, CAST_AT["wd"])
                    s_wa[0].inc(y.dma_start(
                        out=W1[:, 0:4096],
                        in_=scr_wd[:, (ocg * 64) * 512:(ocg * 64 + 8) * 512]))
                k.y(g_wd0)
                wg8[0] = s_wa[0].cnt
                peG0 = s_pe.cnt
                for kg8 in range(8):
                    buf2 = kg8 % 2
                    if kg8 + 1 < 8:
                        def g_wd(y, ocg=ocg, kg8=kg8, buf2=buf2):
                            if kg8 >= 1:
                                y.wait_ge(s_pe.h, peG0 + 8 * kg8)
                            s_wa[1 - buf2].inc(y.dma_start(
                                out=W1[:, (1 - buf2) * 4096:(2 - buf2) * 4096],
                                in_=scr_wd[:, (ocg * 64 + (kg8 + 1) * 8) * 512:
                                           (ocg * 64 + (kg8 + 2) * 8) * 512]))
                        k.y(g_wd)
                        wg8[kg8 + 1] = s_wa[1 - buf2].cnt

                    def t_B(t, kg8=kg8, buf2=buf2):
                        t.wait_ge(s_wa[buf2].h, wg8[kg8])
                        if kg8 == 0:
                            t.wait_ge(s_dve.h, fuse_done if ocg == 0 else evac_cnt)
                        for kcl in range(8):
                            kc = kg8 * 8 + kcl
                            ins = None
                            for ocl in range(4):
                                ins = t.matmul(ps[ocl][:, 0:stw],
                                               W1[:, buf2 * 4096 + kcl * 512 + ocl * 128:
                                                  buf2 * 4096 + kcl * 512 + (ocl + 1) * 128],
                                               A1[:, kc * stw:(kc + 1) * stw],
                                               start=(kc == 0), stop=(kc == 63))
                            s_pe.inc(ins)
                    k.t(t_B)
                peG_done = s_pe.cnt

                def v_B(v):
                    v.wait_ge(s_pe.h, peG_done)
                    for ocl in range(4):
                        c = ocg * 4 + ocl
                        s_dve.inc(v.scalar_tensor_tensor(
                            XR[:, c * NT + st * stw:c * NT + (st + 1) * stw],
                            ps[ocl][:, 0:stw], SC[:, 144 + c:144 + c + 1],
                            XR[:, c * NT + st * stw:c * NT + (st + 1) * stw], ALU.mult, ALU.add))
                k.v(v_B)
                evac_cnt = s_dve.cnt

    def emit_l0attn():
        for st in range(2):
            stw = 512
            Xc = [XR[:, c * NT + st * stw:c * NT + (st + 1) * stw] for c in range(NCH)]
            XNc = [A2[:, c * stw:(c + 1) * stw] for c in range(NCH)]

            def t_g0(t):
                bar3(t)
                t.wait_ge(s_sh.h, s_sh.cnt)
            k.t(t_g0)

            class RowPair:
                def __getitem__(self, idx):
                    part, free = idx
                    lo, hi = free.start, free.stop
                    if hi <= stw:
                        return ps[0][part, lo:hi]
                    return ps[2][part, lo - stw:hi - stw]
            ln_done = emit_ln(k, D, CONSTS[0:1, 0:1], Xc, XNc, stw, RowPair(), ps[1], SQ, ROW,
                              ONES, BONES, ONESROW, s_pe, s_act, s_dve)
            emit_send([(lambda slot, st=st, cc=c2: SH_XN[slot, cc * 128:(cc + 1) * 128,
                                                         st * stw:(st + 1) * stw],
                        A2[:, c2 * stw:(c2 + 1) * stw])
                       for c2 in range(NCH)], [(s_dve, ln_done)])
        sig_xn = gstate["sig"]

        def g_bvr(g):
            g.wait_ge(s_act.h, s_act.cnt)
            g.wait_ge(s_pe.h, s_pe.cnt)
            s_wsc.inc(g.dma_start(out=BVR, in_=bvr_d))
        k.g(g_bvr)
        bvr_in = s_wsc.cnt

        xn_ready = emit_recv(sig_xn, [])

        scale = float(128 ** -0.5)
        wq_ld = {}

        def wreg(buf, mi):
            if buf == 0:
                return W1[:, mi * 2048:(mi + 1) * 2048]
            return (W1[:, 6144:8192], W2[:, 0:2048], W2[:, 2048:4096])[mi]

        def g_whead(y, h, buf):
            sems = s_wa[0] if buf == 0 else s_wb[0]
            for mi in range(3):
                sems.inc(y.dma_start(
                    out=wreg(buf, mi),
                    in_=scr_wqkv[:, (h * 3 + mi) * 2048:(h * 3 + mi + 1) * 2048]))

        def g_w0(y):
            y.wait_ge(s_pe.h, s_pe.cnt)
            y.wait_ge(s_cb.h, CAST_AT["wqkv"])
            g_whead(y, 0, 0)
        k.y(g_w0)
        wq_ld[0] = (s_wa[0], s_wa[0].cnt)

        def xnfb(st4):
            return XNFS if st4 % 2 == 0 else PT_A

        def g_xnf(y, st4, gatepe):
            y.wait_ge(s_pe.h, gatepe)
            slot = st4 // 2
            off = (st4 % 2) * 512
            buf = xnfb(st4)
            for c2 in range(NCH):
                s_xs.inc(y.dma_start(out=buf[:, c2 * 512:(c2 + 1) * 512],
                                     in_=SH_XN[slot, c2 * 128:(c2 + 1) * 128, off:off + 512]))

        prev_qkv2 = 0
        for h in range(8):
            buf = h % 2
            head_pe0 = s_pe.cnt
            if h + 1 < 8:
                def g_pref(y):
                    y.wait_ge(s_pe.h, s_pe.cnt)
                    g_whead(y, h + 1, 1 - buf)
                k.y(g_pref)

                def g_gap(g):
                    cast_gap(g, 2)
                k.g(g_gap)
                wq_ld[h + 1] = ((s_wa[0] if 1 - buf == 0 else s_wb[0]),
                                (s_wa[0] if 1 - buf == 0 else s_wb[0]).cnt)

            ld_at = {}

            def emit_ld(st4, gate):
                def y_ld(y, st4=st4, gate=gate):
                    if h == 0:
                        y.wait_ge(s_sh.h, xn_ready)
                    g_xnf(y, st4, gate)
                k.y(y_ld)
                ld_at[st4] = s_xs.cnt

            emit_ld(0, prev_qkv2)
            emit_ld(1, head_pe0)
            qkv_pe = {}
            for st4 in range(4):
                bufX = xnfb(st4)

                def t_qkv(t, st4=st4, bufX=bufX):
                    sm, cv = wq_ld[h]
                    t.wait_ge(sm.h, cv)
                    t.wait_ge(s_xs.h, ld_at[st4])
                    t.wait_ge(s_act.h, s_act.cnt)
                    t.wait_ge(s_dve.h, s_dve.cnt)
                    if h == 0 and st4 == 0:
                        t.wait_ge(s_sh.h, xn_ready)
                        t.wait_ge(s_wsc.h, bvr_in)
                    acc_mm(t, ps[0][:, 0:512],
                           [(wreg(buf, 1)[:, c * 128:(c + 1) * 128], bufX[:, c * 512:(c + 1) * 512])
                            for c in range(NCH)], s_pe)
                    acc_mm(t, ps[1][:, 0:512],
                           [(wreg(buf, 0)[:, c * 128:(c + 1) * 128], bufX[:, c * 512:(c + 1) * 512])
                            for c in range(NCH)], s_pe)
                    for tt in range(4):
                        for c in range(NCH):
                            t.matmul(ps[2][:, tt * 128:(tt + 1) * 128],
                                     bufX[:, c * 512 + tt * 128:c * 512 + (tt + 1) * 128],
                                     wreg(buf, 2)[:, c * 128:(c + 1) * 128],
                                     start=(c == 0), stop=False)
                        s_pe.inc(t.matmul(ps[2][:, tt * 128:(tt + 1) * 128], BONESROW[:],
                                          BVR[0:1, h * 128:(h + 1) * 128], start=False, stop=True))
                k.t(t_qkv)
                pe_q = s_pe.cnt
                qkv_pe[st4] = pe_q
                if st4 <= 1:
                    emit_ld(st4 + 2, pe_q)

                def s_kq(sc, st4=st4, pe_q=pe_q):
                    sc.wait_ge(s_pe.h, pe_q - 5)
                    s_act.inc(sc.activation(KS_A[:, st4 * 512:(st4 + 1) * 512], ps[0][:, 0:512],
                                            AF.Identity, bias=ATTB[:, 8 + h:8 + h + 1]))
                    sc.wait_ge(s_pe.h, pe_q - 4)
                    s_act.inc(sc.activation(QS_A[:, st4 * 512:(st4 + 1) * 512],
                                            ps[1][:, 0:512], AF.Identity, bias=ATTB[:, h:h + 1]))
                k.s(s_kq)

                def v_v(v, st4=st4, pe_q=pe_q):
                    v.wait_ge(s_pe.h, pe_q)
                    for tt in range(4):
                        s_dve.inc(v.tensor_copy(VS_A[:, st4 * 512 + tt * 128:st4 * 512 + (tt + 1) * 128],
                                                ps[2][:, tt * 128:(tt + 1) * 128]))
                k.v(v_v)
            prev_qkv2 = qkv_pe[2]
            kv_act = s_act.cnt
            kv_dve = s_dve.cnt

            for qt in range(4):
                kcmax = 4 * (qt + 1)
                peS0 = s_pe.cnt
                actS0 = s_act.cnt

                def t_sc(t, qt=qt, kcmax=kcmax, peS0=peS0, actS0=actS0):
                    if qt == 0:
                        t.wait_ge(s_act.h, kv_act)
                        t.wait_ge(s_dve.h, kv_dve)
                    for kc in range(kcmax):
                        bank = ps[3] if kc % 2 == 0 else ps[4]
                        if kc >= 2:
                            t.wait_ge(s_act.h, actS0 + kc - 1)
                        m = kc - qt * 4
                        diag = 0 <= m <= 3
                        ins = t.matmul(bank[:, 0:512], KS_A[:, kc * 128:(kc + 1) * 128],
                                       QS_A[:, qt * 512:(qt + 1) * 512], start=True, stop=not diag)
                        if diag:
                            ins = t.matmul(bank[:, 0:512], IDB[:],
                                           MASKC[:, (3 - m) * 128:(3 - m) * 128 + 512],
                                           start=False, stop=True)
                        s_pe.inc(ins)
                k.t(t_sc)

                def s_exp(sc, qt=qt, kcmax=kcmax, peS0=peS0):
                    for kc in range(kcmax):
                        bank = ps[3] if kc % 2 == 0 else ps[4]
                        sc.wait_ge(s_pe.h, peS0 + kc + 1)
                        s_act.inc(sc.activation(PT_A[:, kc * 512:(kc + 1) * 512], bank[:, 0:512],
                                                AF.Exp, bias=CONSTS[:, 2:3], scale=scale))
                k.s(s_exp)
                exp_done = s_act.cnt

                def t_den(t, kcmax=kcmax, exp_done=exp_done):
                    t.wait_ge(s_act.h, exp_done)
                    acc_mm(t, ps[5][0:1, 0:512],
                           [(BONES[:, 0:1], PT_A[:, kc * 512:(kc + 1) * 512]) for kc in range(kcmax)],
                           s_pe)
                k.t(t_den)
                pe_den = s_pe.cnt

                def v_rc(v, pe_den=pe_den):
                    v.wait_ge(s_pe.h, pe_den)
                    s_dve.inc(v.tensor_scalar(ROW[0:1, 0:512], ps[5][0:1, 0:512], 1e-30, None, ALU.add))
                    v.wait_ge(s_dve.h, s_dve.cnt)
                    s_dve.inc(v.reciprocal(ROW[0:1, 512:1024], ROW[0:1, 0:512]))
                k.v(v_rc)
                dve_rc = s_dve.cnt

                def t_bc(t, dve_rc=dve_rc):
                    t.wait_ge(s_dve.h, dve_rc)
                    s_pe.inc(t.matmul(ps[5][:, 0:512], ONESROW[0:1, :], ROW[0:1, 512:1024],
                                      start=True, stop=True))
                k.t(t_bc)
                pe_bc = s_pe.cnt

                def s_rb(sc, pe_bc=pe_bc):
                    sc.wait_ge(s_pe.h, pe_bc)
                    s_act.inc(sc.copy(TMPY[:], ps[5][:, 0:512]))
                k.s(s_rb)
                act_rb = s_act.cnt

                def t_av(t, qt=qt, kcmax=kcmax):
                    acc_mm(t, ps[2][:, 0:512],
                           [(VS_A[:, kc * 128:(kc + 1) * 128], PT_A[:, kc * 512:(kc + 1) * 512])
                            for kc in range(kcmax)], s_pe)
                k.t(t_av)
                pe_av = s_pe.cnt

                def v_nm(v, qt=qt, pe_av=pe_av, act_rb=act_rb):
                    v.wait_ge(s_pe.h, pe_av)
                    v.wait_ge(s_act.h, act_rb)
                    if qt == 0:
                        v.wait_ge(s_sh.h, s_sh.cnt)
                    s_dve.inc(v.tensor_tensor(YM_A[:, qt * 512:(qt + 1) * 512], ps[2][:, 0:512],
                                              TMPY[:], ALU.mult))
                k.v(v_nm)
            ym_done = s_dve.cnt
            emit_send([(lambda slot, h=h: SH_Y[slot, h * 128:(h + 1) * 128, :], YM_A[:])],
                      [(s_dve, ym_done)])
        sig_y = gstate["sig"]

        def g_yf(g):
            g.wait_ge(s_pe.h, s_pe.cnt)
            g.wait_ge(s_act.h, s_act.cnt)
            g.wait_ge(s_dve.h, s_dve.cnt)
            emit_poll(g, sig_y)
            def rd(my, g):
                for c in range(NCH):
                    s_sh.inc(g.dma_start(
                        out=YFULL[:, c * 1024:(c + 1) * 1024],
                        in_=SH_Y[c // 8, (c % 8) * 128:(c % 8 + 1) * 128,
                                 my * 1024:(my + 1) * 1024]))
            branch_pair(g, gstate["me"], rd)
        k.g(g_yf)
        yf_in = s_sh.cnt

        for ocg in range(4):
            wo_base = s_pe.cnt

            def g_wo(y, ocg=ocg):
                y.wait_ge(s_pe.h, s_pe.cnt)
                if ocg == 0:
                    y.wait_ge(s_cb.h, CAST_AT["wo"])
                for kg in range(2):
                    s_wa[0].inc(y.dma_start(
                        out=W1[:, kg * 4096:(kg + 1) * 4096],
                        in_=scr_wo[:, ocg * 8192 + kg * 4096:ocg * 8192 + (kg + 1) * 4096]))
            k.y(g_wo)
            wo_in = s_wa[0].cnt

            for tt in range(2):
                def t_wo(t, ocg=ocg, tt=tt, wo_in=wo_in):
                    t.wait_ge(s_wa[0].h, wo_in)
                    if ocg == 0 and tt == 0:
                        t.wait_ge(s_sh.h, yf_in)
                    if ocg > 0 or tt > 0:
                        t.wait_ge(s_dve.h, s_dve.cnt)
                    for ocl in range(4):
                        acc_mm(t, ps[ocl][:, 0:512],
                               [(W1[:, (c // 8) * 4096 + (c % 8) * 512 + ocl * 128:
                                     (c // 8) * 4096 + (c % 8) * 512 + (ocl + 1) * 128],
                                 YFULL[:, c * 1024 + tt * 512:c * 1024 + (tt + 1) * 512])
                                for c in range(NCH)], s_pe)
                k.t(t_wo)
                pe_wo = s_pe.cnt

                def v_wo(v, ocg=ocg, tt=tt, pe_wo=pe_wo):
                    v.wait_ge(s_pe.h, pe_wo)
                    for ocl in range(4):
                        c = ocg * 4 + ocl
                        s_dve.inc(v.scalar_tensor_tensor(
                            XR[:, c * NT + tt * 512:c * NT + (tt + 1) * 512],
                            ps[ocl][:, 0:512], ATTB[:, 16 + c:16 + c + 1],
                            XR[:, c * NT + tt * 512:c * NT + (tt + 1) * 512], ALU.add, ALU.add))
                k.v(v_wo)


    def emit_l1ffn():
        for st in range(2):
            stw = 512
            Xc = [XR[:, c * NT + st * stw:c * NT + (st + 1) * stw] for c in range(NCH)]
            XNc = [XNL[:, c * 1024 + st * stw:c * 1024 + (st + 1) * stw] for c in range(NCH)]

            def t_g0(t):
                bar3(t)
                t.wait_ge(s_sh.h, s_sh.cnt)
                t.wait_ge(s_fl.h, s_fl.cnt)
            k.t(t_g0)

            class RowPair:
                def __getitem__(self, idx):
                    part, free = idx
                    lo, hi = free.start, free.stop
                    if hi <= stw:
                        return ps[0][part, lo:hi]
                    return ps[2][part, lo - stw:hi - stw]
            emit_ln(k, D, CONSTS[0:1, 0:1], Xc, XNc, stw, RowPair(), ps[1], SQ, ROW,
                    ONES, BONES, ONESROW, s_pe, s_act, s_dve)
        ln_own = s_dve.cnt

        def v_xl(v):
            v.wait_ge(s_dve.h, ln_own)
            s_dve.inc(v.tensor_copy(BCT[:], XNL[:, 1023:16384:1024]))
        k.v(v_xl)
        xl_done = s_dve.cnt
        emit_send([(lambda slot: SH_C[slot, :, 0:16], BCT[:])], [(s_dve, xl_done)])
        sig_xl = gstate["sig"]
        xl_in = emit_recv(sig_xl, [(lambda slot: SH_C[slot, :, 0:16], BCT[:])], static_slot=1)

        pe_gate_pre = s_pe.cnt

        def v_thb0(v):
            v.wait_ge(s_sh.h, xl_in)
            s_dve.inc(v.tensor_copy(THB[:], BCT[:]))
        k.v(v_thb0)
        thb0 = s_dve.cnt

        def mv_row(wdram, nkc, col_lo, width, rhs_cols_fn, gate_dve, bias_row=None, ready=None,
                   pe_gate=None):
            ld = {}
            g0 = s_pe.cnt if pe_gate is None else pe_gate

            def g_l0(y):
                y.wait_ge(s_pe.h, g0)
                if ready is not None:
                    y.wait_ge(s_cb.h, ready)
                s_y[0].inc(y.dma_start(out=W2[:, 0:width], in_=wdram[0:128, col_lo:col_lo + width]))
            k.y(g_l0)
            ld[0] = (s_y[0], s_y[0].cnt)
            pe0 = s_pe.cnt
            for kc in range(nkc):
                buf = kc % 2
                if kc + 1 < nkc:
                    def g_l(y, kc=kc, buf=buf):
                        if kc >= 1:
                            y.wait_ge(s_pe.h, pe0 + kc)
                        s_y[1 - buf].inc(y.dma_start(
                            out=W2[:, (1 - buf) * 2048:(1 - buf) * 2048 + width],
                            in_=wdram[(kc + 1) * 128:(kc + 2) * 128, col_lo:col_lo + width]))
                    k.y(g_l)
                    ld[kc + 1] = (s_y[1 - buf], s_y[1 - buf].cnt)

                def t_mv(t, kc=kc, buf=buf):
                    sm, cv = ld[kc]
                    t.wait_ge(sm.h, cv)
                    if kc == 0:
                        t.wait_ge(s_dve.h, gate_dve)
                        t.wait_ge(s_act.h, s_act.cnt)
                    ins = None
                    for nb in range((width + 511) // 512):
                        w = min(512, width - nb * 512)
                        ins = t.matmul(ps[nb][0:1, 0:w], rhs_cols_fn(kc),
                                       W2[:, buf * 2048 + nb * 512:buf * 2048 + nb * 512 + w],
                                       start=(kc == 0), stop=False)
                    s_pe.inc(ins)
                k.t(t_mv)

            def t_bias(t):
                ins = None
                for nb in range((width + 511) // 512):
                    w = min(512, width - nb * 512)
                    rhs = bias_row(nb, w) if bias_row is not None else ZROWB[0:1, 0:w]
                    ins = t.matmul(ps[nb][0:1, 0:w], BONES[0:1, 0:1], rhs, start=False, stop=True)
                s_pe.inc(ins)
            k.t(t_bias)
            return s_pe.cnt

        def row_to_cols(dst_cols):
            pe_now = s_pe.cnt

            def v_ev(v):
                v.wait_ge(s_pe.h, pe_now)
                for nb in range(4):
                    s_dve.inc(v.tensor_copy(ROW[0:1, nb * 512:(nb + 1) * 512], ps[nb][0:1, 0:512]))
            k.v(v_ev)
            dv = s_dve.cnt

            def t_r2c(t):
                t.wait_ge(s_dve.h, dv)
                for c in range(16):
                    s_pe.inc(t.matmul(ps[4][:, 32 + c:33 + c], ROW[0:1, c * 128:(c + 1) * 128],
                                      ONESROW[0:1, 0:1], start=True, stop=True))
            k.t(t_r2c)
            pe2 = s_pe.cnt

            def v_cp(v):
                v.wait_ge(s_pe.h, pe2)
                s_dve.inc(v.tensor_copy(dst_cols[:], ps[4][:, 32:48]))
            k.v(v_cp)
            return s_dve.cnt

        def g_d1br(g):
            g.wait_ge(s_pe.h, s_pe.cnt)
            g.wait_ge(s_act.h, s_act.cnt)
            s_wsc.inc(g.dma_start(out=D1BR, in_=d1br_d))
        k.g(g_d1br)
        d1br_in = s_wsc.cnt

        gwl = {}
        peG = {}

        def g_gw(y, oc, gate_pe):
            y.wait_ge(s_pe.h, gate_pe)
            if oc == 0:
                y.wait_ge(s_cb.h, CAST_AT["gw"])
            dstbase = (oc % 2) * 4096
            s_y[oc % 2].inc(y.dma_start(
                out=W1[:, dstbase:dstbase + 2048],
                in_=scr_gw[:, oc * 2048:(oc + 1) * 2048]))

        k.y(lambda y: g_gw(y, 0, s_pe.cnt))
        gwl[0] = (s_y[0], s_y[0].cnt)
        actG0 = s_act.cnt
        for oc in range(16):
            buf = oc % 2
            if oc + 1 < 16:
                gate_pe = peG.get(oc - 1, s_pe.cnt)
                k.y(lambda y, oc=oc, gate_pe=gate_pe: g_gw(y, oc + 1, gate_pe))
                gwl[oc + 1] = (s_y[1 - buf], s_y[1 - buf].cnt)

            def t_gw(t, oc=oc, buf=buf):
                sm, cv = gwl[oc]
                t.wait_ge(sm.h, cv)
                if oc == 0:
                    t.wait_ge(s_dve.h, ln_own)
                if oc >= 1:
                    t.wait_ge(s_act.h, actG0 + 2 * oc)
                for j in range(2):
                    acc_mm(t, ps[2 * (j % 2)][:, 0:512] if False else ps[j][:, 0:512],
                           [(W1[:, buf * 4096 + c * 128:buf * 4096 + (c + 1) * 128],
                             XNL[:, c * 1024 + j * 512:c * 1024 + (j + 1) * 512])
                            for c in range(NCH)], s_pe)
            k.t(t_gw)
            peG[oc] = s_pe.cnt

            def s_sig(sc, oc=oc):
                for j in range(2):
                    sc.wait_ge(s_pe.h, peG[oc] - 1 + j)
                    s_act.inc(sc.activation(GTL[:, oc * 1024 + j * 512:oc * 1024 + (j + 1) * 512],
                                            ps[j][:, 0:512], AF.Sigmoid,
                                            bias=F1C[:, 96 + oc:96 + oc + 1]))
            k.s(s_sig)
        act_gates = s_act.cnt

        pe_after_gate = s_pe.cnt

        agg_done = mv_row(scr_aggw, 16, 0, 2048, lambda kc: THB[:, kc:kc + 1], thb0,
                          ready=CAST_AT["aggw"], pe_gate=pe_gate_pre)
        fl_t0 = row_to_cols(TC)

        def v_t0(v):
            v.wait_ge(s_dve.h, fl_t0)
            s_dve.inc(v.tensor_tensor(TH[:], TC[:], F1C[:, 0:16], ALU.add))
            v.wait_ge(s_dve.h, s_dve.cnt)
            s_dve.inc(v.tensor_copy(THB[:], TH[:]))
        k.v(v_t0)

        for step in range(4):
            def t_ts(t):
                t.wait_ge(s_dve.h, s_dve.cnt)
                s_pe.inc(t.matmul(ps[4][0:1, 0:16], ONES[:, 0:1], TH[:], start=True, stop=True))
            k.t(t_ts)
            pe_ts = s_pe.cnt

            def v_tm(v, pe_ts=pe_ts):
                v.wait_ge(s_pe.h, pe_ts)
                s_dve.inc(v.reduce_sum(ROW[0:1, 0:1], ps[4][0:1, 0:16], axis=mybir.AxisListType.X))
                v.wait_ge(s_dve.h, s_dve.cnt)
                s_dve.inc(v.tensor_scalar(ROW[0:1, 1:2], ROW[0:1, 0:1], -1.0 / D, None, ALU.mult))
            k.v(v_tm)
            dve_tm = s_dve.cnt

            def t_bm(t, dve_tm=dve_tm):
                t.wait_ge(s_dve.h, dve_tm)
                s_pe.inc(t.matmul(ps[5][:, 0:1], ONESROW[0:1, :], ROW[0:1, 1:2], start=True, stop=True))
            k.t(t_bm)
            pe_bm = s_pe.cnt

            def v_ct(v, pe_bm=pe_bm):
                v.wait_ge(s_pe.h, pe_bm)
                s_dve.inc(v.tensor_scalar(TC[:], TH[:], ps[5][:, 0:1], None, ALU.add))
            k.v(v_ct)
            dve_ct = s_dve.cnt

            def s_sq2(sc, dve_ct=dve_ct):
                sc.wait_ge(s_dve.h, dve_ct)
                s_act.inc(sc.activation(SQT[:], TC[:], AF.Square))
            k.s(s_sq2)
            act_sq2 = s_act.cnt

            def t_ss(t, act_sq2=act_sq2):
                t.wait_ge(s_act.h, act_sq2)
                s_pe.inc(t.matmul(ps[4][0:1, 16:32], BONES[:, 0:1], SQT[:], start=True, stop=True))
            k.t(t_ss)
            pe_ss = s_pe.cnt

            def v_vv(v, pe_ss=pe_ss):
                v.wait_ge(s_pe.h, pe_ss)
                s_dve.inc(v.reduce_sum(ROW[0:1, 2:3], ps[4][0:1, 16:32], axis=mybir.AxisListType.X))
            k.v(v_vv)
            dve_vv = s_dve.cnt

            def s_sr(sc, dve_vv=dve_vv):
                sc.wait_ge(s_dve.h, dve_vv)
                s_act.inc(sc.activation(ROW[0:1, 3:4], ROW[0:1, 2:3], AF.Sqrt,
                                        bias=CONSTS[0:1, 1:2], scale=1.0 / D))
            k.s(s_sr)
            act_sr = s_act.cnt

            def v_rc2(v, act_sr=act_sr):
                v.wait_ge(s_act.h, act_sr)
                s_dve.inc(v.reciprocal(ROW[0:1, 4:5], ROW[0:1, 3:4]))
            k.v(v_rc2)
            dve_rc2 = s_dve.cnt

            def t_br(t, dve_rc2=dve_rc2):
                t.wait_ge(s_dve.h, dve_rc2)
                s_pe.inc(t.matmul(ps[5][:, 1:2], ONESROW[0:1, :], ROW[0:1, 4:5], start=True, stop=True))
            k.t(t_br)
            pe_br = s_pe.cnt

            def v_tn2(v, pe_br=pe_br):
                v.wait_ge(s_pe.h, pe_br)
                s_dve.inc(v.tensor_scalar(TC[:], TC[:], ps[5][:, 1:2], None, ALU.mult))
                v.wait_ge(s_dve.h, s_dve.cnt)
                s_dve.inc(v.scalar_tensor_tensor(TN[:], TC[:], 1.0, F1C[:, 64:80], ALU.mult, ALU.mult))
                v.wait_ge(s_dve.h, s_dve.cnt)
                s_dve.inc(v.tensor_tensor(TN[:], TN[:], F1C[:, 80:96], ALU.add))
            k.v(v_tn2)
            tn_ok = s_dve.cnt

            for hhalf in range(2):
                d1_done = mv_row(scr_d1h, 16, hhalf * 2048, 2048,
                                 lambda kc: TN[:, kc:kc + 1], tn_ok, ready=CAST_AT["d1h"],
                                 bias_row=lambda nb, w, hh=hhalf: D1BR[0:1, hh * 2048 + nb * 512:hh * 2048 + nb * 512 + w])
                if hhalf == 0:
                    def s_g1(sc, d1_done=d1_done):
                        sc.wait_ge(s_pe.h, d1_done)
                        sc.wait_ge(s_wsc.h, d1br_in)
                        for nb in range(4):
                            s_act.inc(sc.activation(HROW[0:1, nb * 512:(nb + 1) * 512],
                                                    ps[nb][0:1, 0:512], AF.Gelu))
                    k.s(s_g1)
                else:
                    def s_g2(sc, d1_done=d1_done):
                        sc.wait_ge(s_pe.h, d1_done)
                        for nb in range(4):
                            s_act.inc(sc.activation(HROW[0:1, 2048 + nb * 512:2048 + (nb + 1) * 512],
                                                    ps[nb][0:1, 0:512], AF.Gelu))
                    k.s(s_g2)
            gel_done = s_act.cnt

            def t_hc(t, gel_done=gel_done):
                t.wait_ge(s_act.h, gel_done)
                for c in range(32):
                    s_pe.inc(t.matmul(ps[5][:, 32 + c:33 + c], HROW[0:1, c * 128:(c + 1) * 128],
                                      ONESROW[0:1, 0:1], start=True, stop=True))
            k.t(t_hc)
            pe_hc = s_pe.cnt

            def v_hcg(v, pe_hc=pe_hc):
                v.wait_ge(s_pe.h, pe_hc)
                s_dve.inc(v.tensor_copy(HC[:], ps[5][:, 32:64]))
            k.v(v_hcg)
            hc_ok = s_dve.cnt

            mv_row(scr_d2h, 32, 0, 2048, lambda kc: HC[:, kc:kc + 1], hc_ok,
                   ready=CAST_AT["d2h"])
            fl_pt = row_to_cols(BCT)

            def v_pt(v, fl_pt=fl_pt):
                v.wait_ge(s_dve.h, fl_pt)
                s_dve.inc(v.engine_nop())
            k.v(v_pt)
            pt_ok = s_dve.cnt

            emit_send([(lambda slot, step=step: SH_C[slot, :, 16 * (step + 1):16 * (step + 2)], BCT[:])],
                      [(s_dve, pt_ok)])
            sigp = gstate["sig"]
            pin = emit_recv(sigp, [(lambda slot, step=step: SH_C[slot, :, 16 * (step + 1):16 * (step + 2)], TC[:])])

            def v_upd(v, pin=pin):
                v.wait_ge(s_sh.h, pin)
                s_dve.inc(v.tensor_tensor(TC[:], TC[:], BCT[:], ALU.add))
                v.wait_ge(s_dve.h, s_dve.cnt)
                s_dve.inc(v.tensor_tensor(TC[:], TC[:], F1C[:, 16:32], ALU.add))
                v.wait_ge(s_dve.h, s_dve.cnt)
                s_dve.inc(v.tensor_tensor(TH[:], TH[:], TC[:], ALU.add))
                v.wait_ge(s_dve.h, s_dve.cnt)
                s_dve.inc(v.tensor_copy(THB[:], TH[:]))
            k.v(v_upd)

        mv_row(scr_bcw, 16, 0, 2048, lambda kc: THB[:, kc:kc + 1], s_dve.cnt,
               ready=CAST_AT["bcw"])
        fl_bc = row_to_cols(BCT)

        def v_bcf(v):
            v.wait_ge(s_dve.h, fl_bc)
            s_dve.inc(v.tensor_tensor(BCT[:], BCT[:], F1C[:, 112:128], ALU.add))
        k.v(v_bcf)
        bc_ok = s_dve.cnt


        def v_ot(v):
            v.wait_ge(s_act.h, act_gates)
            v.wait_ge(s_dve.h, bc_ok)
            for oc in range(16):
                v.wait_ge(s_dve.h, s_dve.cnt)
                s_dve.inc(v.scalar_tensor_tensor(
                    XNL[:, oc * 1024:(oc + 1) * 1024], GTL[:, oc * 1024:(oc + 1) * 1024],
                    BCT[:, oc:oc + 1], XNL[:, oc * 1024:(oc + 1) * 1024], ALU.mult, ALU.add))
        k.v(v_ot)
        ot_done = s_dve.cnt

        for st in range(2):
            stw = 512

            def t_g2(t):
                t.wait_ge(s_dve.h, ot_done)
            k.t(t_g2)

            class RowPair2:
                def __getitem__(self, idx):
                    part, free = idx
                    lo, hi = free.start, free.stop
                    if hi <= stw:
                        return ps[0][part, lo:hi]
                    return ps[2][part, lo - stw:hi - stw]
            Xc = [XNL[:, c * 1024 + st * stw:c * 1024 + (st + 1) * stw] for c in range(NCH)]
            XNc2 = [GTL[:, c * 1024 + st * stw:c * 1024 + (st + 1) * stw] for c in range(NCH)]
            emit_ln(k, D, CONSTS[0:1, 1:2], Xc, XNc2, stw, RowPair2(), ps[1], SQ, ROW,
                    BONES, BONES, ONESROW, s_pe, s_act, s_dve)
        lnf = s_dve.cnt

        def v_app(v):
            v.wait_ge(s_dve.h, lnf)
            for c in range(NCH):
                for j in range(2):
                    v.wait_ge(s_dve.h, s_dve.cnt)
                    s_dve.inc(v.tensor_scalar(TMPZ[:], GTL[:, c * 1024 + j * 512:c * 1024 + (j + 1) * 512],
                                              G2B2[:, c:c + 1], G2B2[:, 16 + c:16 + c + 1],
                                              ALU.mult, ALU.add))
                    v.wait_ge(s_dve.h, s_dve.cnt)
                    s_dve.inc(v.tensor_tensor(XR[:, c * NT + j * 512:c * NT + (j + 1) * 512],
                                              XR[:, c * NT + j * 512:c * NT + (j + 1) * 512],
                                              TMPZ[:], ALU.add))
        k.v(v_app)

    def emit_ssd():
        for st in range(2):
            stw = 512
            Xc = [XR[:, c * NT + st * stw:c * NT + (st + 1) * stw] for c in range(NCH)]
            XNc = [A2[:, c * stw:(c + 1) * stw] for c in range(NCH)]

            def t_g0(t):
                bar3(t)
                t.wait_ge(s_sh.h, s_sh.cnt)
                t.wait_ge(s_fl.h, s_fl.cnt)
            k.t(t_g0)

            class RowPair:
                def __getitem__(self, idx):
                    part, free = idx
                    lo, hi = free.start, free.stop
                    if hi <= stw:
                        return ps[0][part, lo:hi]
                    return ps[2][part, lo - stw:hi - stw]
            ln_done = emit_ln(k, D, CONSTS[0:1, 0:1], Xc, XNc, stw, RowPair(), ps[1], SQ, ROW,
                              ONES, BONES, ONESROW, s_pe, s_act, s_dve)
            emit_send([(lambda slot, st=st, cc=c2: SH_XN[slot, cc * 128:(cc + 1) * 128,
                                                         st * stw:(st + 1) * stw],
                        A2[:, c2 * stw:(c2 + 1) * stw])
                       for c2 in range(NCH)], [(s_dve, ln_done)])
        sig_xn2 = gstate["sig"]
        xn2_ready = emit_recv(sig_xn2, [])

        def g_rz(g):
            g.wait_ge(s_dve.h, s_dve.cnt)
            g.wait_ge(s_pe.h, s_pe.cnt)
            g.memset(R[:], 0.0)
            s_g.inc(g.memset(RB, 0.0))
        k.g(g_rz)
        rz_ok = s_g.cnt

        oV, oA, oE, oL = [j * 256 for j in range(4)]
        oMX, oDA = oA, oE
        oDT, oACS = oV, oL
        CH2 = 256

        yw_cnt = 0
        xnf_free = [s_pe.cnt, s_pe.cnt]
        w1_free = [s_pe.cnt]
        w2_free = [s_pe.cnt]

        def g_wx(y, ch, buf):
            wd_ = 128 if ch < 18 else 32
            s_y[buf].inc(y.dma_start(
                out=W1[:, buf * 4096:buf * 4096 + 16 * wd_],
                in_=scr_winx[:, ch * 2048:ch * 2048 + 16 * wd_]))

        pre_state = {}

        def emit_p1p2_loads(c2):
            slot_t = c2 // 4
            off_t = (c2 % 4) * 256
            XNF = A1[:, (c2 % 2) * 4096:(c2 % 2) * 4096 + 4096]

            def g_xn(y, c2=c2, slot_t=slot_t, off_t=off_t, XNF=XNF):
                y.wait_ge(s_pe.h, xnf_free[c2 % 2])
                if c2 == 0:
                    y.wait_ge(s_sh.h, xn2_ready)
                    y.wait_ge(s_g.h, rz_ok)
                for c in range(NCH):
                    s_xs.inc(y.dma_start(out=XNF[:, c * 256:(c + 1) * 256],
                                         in_=SH_XN[slot_t, c * 128:(c + 1) * 128, off_t:off_t + 256]))
            k.y(g_xn)
            wload = {}

            def g_wx0(y, c2=c2):
                y.wait_ge(s_pe.h, w1_free[0])
                if c2 == 0:
                    y.wait_ge(s_cb.h, CAST_AT["winx"])
                g_wx(y, 0, 0)
            k.y(g_wx0)
            wload[0] = (s_y[0], s_y[0].cnt)
            k.y(lambda y: g_wx(y, 1, 1))
            wload[1] = (s_y[1], s_y[1].cnt)
            pre_state[c2] = dict(xn_in=s_xs.cnt, wload=wload, XNF=XNF)

        emit_p1p2_loads(0)
        for c2 in range(8):
            st_pre = pre_state.pop(c2)
            XNF = st_pre["XNF"]
            xn_in = st_pre["xn_in"]
            wload = st_pre["wload"]

            peP2 = s_pe.cnt
            dveP2 = s_dve.cnt
            for ch in range(19):
                buf = ch % 2
                wd_ = 128 if ch < 18 else 32
                if 1 <= ch < 18:
                    def g_wn(y, ch=ch, buf=buf):
                        y.wait_ge(s_pe.h, peP2 + ch)
                        g_wx(y, ch + 1, 1 - buf)
                    k.y(g_wn)
                    wload[ch + 1] = (s_y[1 - buf], s_y[1 - buf].cnt)

                def t_px(t, ch=ch, buf=buf, wd_=wd_):
                    sm, cv = wload[ch]
                    t.wait_ge(sm.h, cv)
                    if ch == 0:
                        t.wait_ge(s_xs.h, xn_in)
                        t.wait_ge(s_act.h, s_act.cnt)
                        t.wait_ge(s_dve.h, s_dve.cnt)
                    if ch >= 2:
                        t.wait_ge(s_dve.h, dveP2 + ch - 1)
                    acc_mm(t, ps[3 + buf][0:wd_, 0:CH2],
                           [(W1[:, buf * 4096 + c * wd_:buf * 4096 + (c + 1) * wd_],
                             XNF[:, c * 256:(c + 1) * 256]) for c in range(NCH)], s_pe)
                k.t(t_px)
                pe_ch = s_pe.cnt

                def v_px(v, ch=ch, buf=buf, pe_ch=pe_ch):
                    v.wait_ge(s_pe.h, pe_ch)
                    if ch < 18:
                        s_dve.inc(v.tensor_copy(PREB[:, ch * 259 + 3:(ch + 1) * 259], ps[3 + buf][:, 0:CH2]))
                    else:
                        s_dve.inc(v.tensor_copy(DTJ[:, oV:oV + 256], ps[3 + buf][0:32, 0:CH2]))
                k.v(v_px)

            w1_free[0] = s_pe.cnt
            if c2 + 1 < 8:
                emit_p1p2_loads(c2 + 1)

            def v_halo(v, c2=c2):
                v.wait_ge(s_dve.h, s_dve.cnt)
                if c2 == 0:
                    s_dve.inc(v.memset(PREB.rearrange("p (c w) -> p c w", c=18)[:, :, 0:3], 0.0))
                else:
                    s_dve.inc(v.engine_nop())
            k.v(v_halo)

            dve_c3 = s_dve.cnt
            act_c3 = s_act.cnt

            def v_conv(v):
                v.wait_ge(s_dve.h, dve_c3)
                for ch in range(18):
                    base = ch * 259
                    cb2 = (ch % 2) * 256
                    if ch >= 2:
                        v.wait_ge(s_act.h, act_c3 + ch - 1)
                    s_dve.inc(v.tensor_scalar(CONVT[:, cb2:cb2 + 256], PREB[:, base:base + 256],
                                              CWK[:, ch:ch + 1], None, ALU.mult))
                    for kk in range(1, 4):
                        v.wait_ge(s_dve.h, s_dve.cnt)
                        s_dve.inc(v.scalar_tensor_tensor(CONVT[:, cb2:cb2 + 256],
                                                         PREB[:, base + kk:base + kk + 256],
                                                         CWK[:, kk * 18 + ch:kk * 18 + ch + 1],
                                                         CONVT[:, cb2:cb2 + 256], ALU.mult, ALU.add))
            k.v(v_conv)

            def s_conv(s):
                for ch in range(18):
                    cb2 = (ch % 2) * 256
                    s.wait_ge(s_dve.h, dve_c3 + 4 * (ch + 1))
                    s_act.inc(s.activation(XBC[:, ch * CH2:(ch + 1) * CH2], CONVT[:, cb2:cb2 + 256],
                                           AF.Silu, bias=CBc[:, ch:ch + 1]))
            k.s(s_conv)
            conv_done = s_act.cnt

            def v_halo2(v):
                v.wait_ge(s_act.h, conv_done)
                s_dve.inc(v.tensor_copy(
                    PREB.rearrange("p (c w) -> p c w", c=18)[:, :, 0:3],
                    PREB.rearrange("p (c w) -> p c w", c=18)[:, :, 256:259]))
            k.v(v_halo2)

            def s_dt1(s):
                bar3(s)
                s_act.inc(s.activation(DTJ[0:32, oV:oV + 256], DTJ[0:32, oV:oV + 256],
                                       AF.Identity, bias=AUXC[:, 0:1]))
                s.wait_ge(s_act.h, s_act.cnt)
                s_act.inc(s.activation(DTJ[0:32, oA:oA + 256], DTJ[0:32, oV:oV + 256], AF.Abs))
                s.wait_ge(s_act.h, s_act.cnt)
                s_act.inc(s.activation(DTJ[0:32, oE:oE + 256], DTJ[0:32, oA:oA + 256],
                                       AF.Exp, scale=-1.0))
                s.wait_ge(s_act.h, s_act.cnt)
                s_act.inc(s.activation(DTJ[0:32, oL:oL + 256], DTJ[0:32, oE:oE + 256],
                                       AF.Ln, bias=CONSTS[0:32, 3:4]))
            k.s(s_dt1)
            dt1 = s_act.cnt

            def v_dt2(v):
                v.wait_ge(s_act.h, dt1)
                s_dve.inc(v.tensor_scalar(DTJ[0:32, oMX:oMX + 256], DTJ[0:32, oV:oV + 256],
                                          0.0, None, ALU.max))
                v.wait_ge(s_dve.h, s_dve.cnt)
                s_dve.inc(v.tensor_tensor(DTJ[0:32, oDT:oDT + 256], DTJ[0:32, oMX:oMX + 256],
                                          DTJ[0:32, oL:oL + 256], ALU.add))
                v.wait_ge(s_dve.h, s_dve.cnt)
                s_dve.inc(v.tensor_scalar(DTJ[0:32, oDT:oDT + 256], DTJ[0:32, oDT:oDT + 256],
                                          100.0, None, ALU.min))
                v.wait_ge(s_dve.h, s_dve.cnt)
                s_dve.inc(v.tensor_scalar(DTJ[0:32, oDA:oDA + 256], DTJ[0:32, oDT:oDT + 256],
                                          ACOL[:, 0:1], None, ALU.mult))
                v.wait_ge(s_dve.h, s_dve.cnt)
                ones_ap = ONES[0:32, 0:1].unsqueeze(2).broadcast_to([32, 1, 256]).squeeze(1)
                s_dve.inc(v.tensor_tensor_scan(DTJ[0:32, oACS:oACS + 256], ones_ap,
                                               DTJ[0:32, oDA:oDA + 256], 0.0, ALU.mult, ALU.add))
                v.wait_ge(s_dve.h, s_dve.cnt)
                s_dve.inc(v.tensor_scalar(NACS[:], DTJ[0:32, oACS:oACS + 256], -1.0, None, ALU.mult))
                v.wait_ge(s_dve.h, s_dve.cnt)
                s_dve.inc(v.tensor_scalar(DECF[:], DTJ[0:32, oACS:oACS + 256],
                                          DTJ[0:32, oACS + 255:oACS + 256], -1.0,
                                          ALU.subtract, ALU.mult))
            k.v(v_dt2)
            dt2 = s_dve.cnt

            def s_dt3(s):
                s.wait_ge(s_dve.h, dt2)
                s_act.inc(s.activation(DECF[:], DECF[:], AF.Exp))
            k.s(s_dt3)
            dt3 = s_act.cnt

            def v_dt4(v):
                v.wait_ge(s_act.h, dt3)
                s_dve.inc(v.tensor_tensor(DTD[:], DTJ[0:32, oDT:oDT + 256], DECF[:], ALU.mult))
            k.v(v_dt4)

            zld = {}

            def g_z(y, zb, buf):
                s_y[buf].inc(y.dma_start(
                    out=W2[:, buf * 2048:buf * 2048 + 2048],
                    in_=scr_winz[:, zb * 2048:(zb + 1) * 2048]))

            def g_z0(y):
                y.wait_ge(s_pe.h, w2_free[0])
                y.wait_ge(s_cb.h, CAST_AT["winz"])
                g_z(y, 0, 0)
            k.y(g_z0)
            zld[0] = (s_y[0], s_y[0].cnt)
            peZ = s_pe.cnt
            actZ = s_act.cnt
            for zb in range(16):
                buf = zb % 2
                if zb + 1 < 16:
                    def g_zn(y, zb=zb, buf=buf):
                        if zb >= 1:
                            y.wait_ge(s_pe.h, peZ + 2 * zb)
                        g_z(y, zb + 1, 1 - buf)
                    k.y(g_zn)
                    zld[zb + 1] = (s_y[1 - buf], s_y[1 - buf].cnt)

                def t_z(t, zb=zb, buf=buf):
                    sm, cv = zld[zb]
                    t.wait_ge(sm.h, cv)
                    if zb == 0:
                        t.wait_ge(s_dve.h, dveP2 + 19)
                    if zb >= 1:
                        t.wait_ge(s_act.h, actZ + 2 * zb)
                    for tt in range(2):
                        acc_mm(t, ps[2 * buf + tt][:, 0:128],
                               [(XNF[:, c * 256 + tt * 128:c * 256 + (tt + 1) * 128],
                                 W2[:, buf * 2048 + c * 128:buf * 2048 + (c + 1) * 128])
                                for c in range(NCH)], s_pe)
                k.t(t_z)
                pe_z = s_pe.cnt

                def s_z(s, zb=zb, buf=buf, pe_z=pe_z):
                    for tt in range(2):
                        s.wait_ge(s_pe.h, pe_z - 1 + tt)
                        s_act.inc(s.activation(ZS_S[:, tt * 2048 + zb * 128:tt * 2048 + (zb + 1) * 128],
                                               ps[2 * buf + tt][:, 0:128], AF.Silu))
                k.s(s_z)

            w2_free[0] = s_pe.cnt
            xnf_free[c2 % 2] = s_pe.cnt

            def t_tr(t):
                bar3(t)
                for lt in range(2):
                    t.transpose(ps[5][:, lt * 32:(lt + 1) * 32],
                                DTJ[0:32, oDT + lt * 128:oDT + (lt + 1) * 128], IDF[0:32, 0:32])
                    t.transpose(ps[5][:, 64 + lt * 32:64 + (lt + 1) * 32],
                                DTD[:, lt * 128:(lt + 1) * 128], IDF[0:32, 0:32])
                    s_pe.inc(t.transpose(ps[4][:, lt * 32:(lt + 1) * 32],
                                         DTJ[0:32, oACS + lt * 128:oACS + (lt + 1) * 128], IDF[0:32, 0:32]))
                s_pe.inc(t.transpose(ps[4][0:1, 64:96], DTJ[0:32, oACS + 255:oACS + 256], IDF[0:32, 0:32]))
            k.t(t_tr)
            pe_tr1 = s_pe.cnt

            def v_tr1(v):
                v.wait_ge(s_pe.h, pe_tr1)
                s_dve.inc(v.tensor_copy(DTT[:, 0:64], ps[5][:, 0:64]))
                s_dve.inc(v.tensor_copy(DTDT[:, 0:64], ps[5][:, 64:128]))
            k.v(v_tr1)

            def s_tr1(s):
                s.wait_ge(s_pe.h, pe_tr1)
                s_act.inc(s.activation(EXAT[:, 0:64], ps[4][:, 0:64], AF.Exp))
                s_act.inc(s.activation(ECER[:], ps[4][0:1, 64:96], AF.Exp))
            k.s(s_tr1)
            ece1 = s_act.cnt

            def t_ecex(t):
                t.wait_ge(s_act.h, ece1)
                s_pe.inc(t.matmul(ps[4][:, 96:128], ONESROW[0:1, :], ECER[:], start=True, stop=True))
            k.t(t_ecex)
            pe_ecex = s_pe.cnt

            def s_ecex(s):
                s.wait_ge(s_pe.h, pe_ecex)
                s_act.inc(s.copy(ECEX[:], ps[4][:, 96:128]))
            k.s(s_ecex)

            dve_xt = s_dve.cnt
            pe_xt = s_pe.cnt

            def t_xt(t):
                t.wait_ge(s_act.h, s_act.cnt)
                t.wait_ge(s_dve.h, s_dve.cnt)
                n = 0
                for ch in range(17):
                    for lt in range(2):
                        bank = psta if n % 2 == 0 else pstb
                        if n >= 2:
                            t.wait_ge(s_dve.h, dve_xt + n - 1)
                        s_pe.inc(t.transpose(bank[:], XBC[:, ch * CH2 + lt * 128:ch * CH2 + (lt + 1) * 128], IDB[:]))
                        n += 1
            k.t(t_xt)

            def v_xt(v):
                n = 0
                for ch in range(17):
                    for lt in range(2):
                        bank = psta if n % 2 == 0 else pstb
                        v.wait_ge(s_pe.h, pe_xt + n + 1)
                        if ch < 16:
                            dst = XT_S[:, lt * 2048 + ch * 128:lt * 2048 + (ch + 1) * 128]
                        else:
                            dst = BTOK[:, lt * 128:(lt + 1) * 128]
                        s_dve.inc(v.tensor_copy(dst, bank[:]))
                        n += 1
            k.v(v_xt)

            def t_cbt(t):
                t.wait_ge(s_dve.h, s_dve.cnt)
                B_ = XBC[:, 16 * CH2:17 * CH2]
                C_ = XBC[:, 17 * CH2:18 * CH2]
                t.matmul(ps[5][:, 0:128], B_[:, 0:128], C_[:, 0:128], start=True, stop=True)
                t.matmul(ps[5][:, 128:256], B_[:, 0:128], C_[:, 128:256], start=True, stop=True)
                s_pe.inc(t.matmul(ps[5][:, 256:384], B_[:, 128:256], C_[:, 128:256], start=True, stop=True))
            k.t(t_cbt)
            pe_cbt = s_pe.cnt

            def v_cbt(v):
                v.wait_ge(s_pe.h, pe_cbt)
                s_dve.inc(v.tensor_copy(CBTS, ps[5][:, 0:384]))
            k.v(v_cbt)

            def seg_cols(j):
                return slice(j * 128, (j + 1) * 128)

            def t_p7gate(t):
                t.wait_ge(s_dve.h, s_dve.cnt)
            k.t(t_p7gate)
            p7_dve = s_dve.cnt
            exp_cnt = {}
            m1_cnt = {}
            ymm_cnt = {}
            flat_of = {}
            for fg in range(8):
                last_seg_pe = s_pe.cnt

                def g_flat(g, fg=fg, last_seg_pe=last_seg_pe):
                    g.wait_ge(s_dve.h, p7_dve)
                    if fg > 0:
                        g.wait_ge(s_pe.h, last_seg_pe)
                    s_fl.inc(g.dma_start(out=ACSF[:].rearrange("o (h l) -> o h l", h=4),
                                         in_=DTJ[fg * 4:(fg + 1) * 4, oACS:oACS + 256].unsqueeze(1)))
                    s_fl.inc(g.dma_start(out=NACSF[:].rearrange("o (h l) -> o h l", h=4),
                                         in_=NACS[fg * 4:(fg + 1) * 4, :].unsqueeze(1)))
                k.g(g_flat)
                flat_of[fg] = s_fl.cnt

                for hh4 in range(4):
                    h = fg * 4 + hh4
                    hg8 = h // 8
                    hh8 = h % 8
                    hb = hh4 * 256
                    segbank = ps[4] if h % 2 == 0 else ps[5]

                    def t_seg(t, h=h, hh4=hh4, hb=hb, segbank=segbank, fg=fg):
                        if hh4 == 0:
                            t.wait_ge(s_fl.h, flat_of[fg])
                        if h >= 2:
                            t.wait_ge(s_act.h, exp_cnt[h - 2])
                        for j, (sh, lh) in enumerate(((0, 0), (0, 1), (1, 1))):
                            t.matmul(segbank[:, seg_cols(j)],
                                     NACSF[0:1, hb + sh * 128:hb + (sh + 1) * 128],
                                     ONESROW[0:1, :], start=True, stop=False)
                            ins = t.matmul(segbank[:, seg_cols(j)], ONESROW[0:1, :],
                                           ACSF[0:1, hb + lh * 128:hb + (lh + 1) * 128],
                                           start=False, stop=(sh != lh))
                            if sh == lh:
                                ins = t.matmul(segbank[:, seg_cols(j)], IDB[:], MASKSEG[:],
                                               start=False, stop=True)
                        s_pe.inc(ins)
                    k.t(t_seg)
                    pe_seg = s_pe.cnt

                    def s_exp(s, h=h, segbank=segbank, pe_seg=pe_seg):
                        s.wait_ge(s_pe.h, pe_seg)
                        if h >= 1:
                            s.wait_ge(s_dve.h, m1_cnt[h - 1])
                        s_act.inc(s.activation(LSEG, segbank[:, 0:384], AF.Exp))
                    k.s(s_exp)
                    exp_cnt[h] = s_act.cnt

                    def v_m1(v, h=h):
                        v.wait_ge(s_act.h, exp_cnt[h])
                        if h >= 1:
                            v.wait_ge(s_pe.h, ymm_cnt[h - 1])
                        s_dve.inc(v.scalar_tensor_tensor(M1S[:, 0:256], LSEG[:, 0:256],
                                                         DTT[:, h:h + 1], CBTS[:, 0:256],
                                                         ALU.mult, ALU.mult))
                        v.wait_ge(s_dve.h, s_dve.cnt)
                        s_dve.inc(v.scalar_tensor_tensor(M1S[:, 256:384], LSEG[:, 256:384],
                                                         DTT[:, 32 + h:33 + h], CBTS[:, 256:384],
                                                         ALU.mult, ALU.mult))
                    k.v(v_m1)
                    m1_cnt[h] = s_dve.cnt

                    def t_y(t, h=h, hg8=hg8, hh8=hh8):
                        t.wait_ge(s_dve.h, m1_cnt[h])
                        if hh8 == 0 and hg8 > 0:
                            t.wait_ge(s_dve.h, s_dve.cnt)
                        hc = slice(hh8 * 64, (hh8 + 1) * 64)
                        t.matmul(ps[0][:, hc], M1S[:, 0:128], XT_S[:, 0 * 2048 + h * 64:0 * 2048 + (h + 1) * 64],
                                 start=True, stop=True)
                        t.matmul(ps[1][:, hc], M1S[:, 128:256], XT_S[:, 0 * 2048 + h * 64:0 * 2048 + (h + 1) * 64],
                                 start=True, stop=False)
                        t.matmul(ps[1][:, hc], M1S[:, 256:384], XT_S[:, 1 * 2048 + h * 64:1 * 2048 + (h + 1) * 64],
                                 start=False, stop=True)
                        t.matmul(ps[2][:, hc], XBC[:, 17 * CH2:17 * CH2 + 128], RB[:, h * 64:(h + 1) * 64],
                                 start=True, stop=True)
                        s_pe.inc(t.matmul(ps[3][:, hc], XBC[:, 17 * CH2 + 128:18 * CH2], RB[:, h * 64:(h + 1) * 64],
                                          start=True, stop=True))
                    k.t(t_y)
                    ymm_cnt[h] = s_pe.cnt

                    if hh8 == 7:
                        pe_hg_end = s_pe.cnt

                        def v_evac(v, hg8=hg8, pe_hg_end=pe_hg_end):
                            v.wait_ge(s_pe.h, pe_hg_end)
                            for lh in range(2):
                                ydo = ps[0] if lh == 0 else ps[1]
                                yof = ps[2] if lh == 0 else ps[3]
                                exa = EXAT[:, lh * 32 + hg8 * 8:lh * 32 + (hg8 + 1) * 8]
                                exap = exa.unsqueeze(2).broadcast_to([128, 8, 64])
                                dbp = DBC[:, hg8 * 8:(hg8 + 1) * 8].unsqueeze(2).broadcast_to([128, 8, 64])
                                xsl = XT_S[:, lh * 2048 + hg8 * 512:lh * 2048 + (hg8 + 1) * 512]
                                ysl = YT_S[:, lh * 2048 + hg8 * 512:lh * 2048 + (hg8 + 1) * 512]
                                r3 = lambda a: a.rearrange("p (a b) -> p a b", a=8)
                                v.wait_ge(s_dve.h, s_dve.cnt)
                                s_dve.inc(v.tensor_tensor(r3(TMPY[:]), r3(yof[:, 0:512]), exap, ALU.mult))
                                s_dve.inc(v.tensor_tensor(r3(TMP2), r3(xsl), dbp, ALU.mult))
                                v.wait_ge(s_dve.h, s_dve.cnt)
                                s_dve.inc(v.tensor_tensor(TMPY[:], TMPY[:], ydo[:, 0:512], ALU.add))
                                v.wait_ge(s_dve.h, s_dve.cnt)
                                s_dve.inc(v.tensor_tensor(ysl, TMPY[:], TMP2, ALU.add))
                        k.v(v_evac)

            def v_xdtd(v):
                v.wait_ge(s_pe.h, s_pe.cnt)
                v.wait_ge(s_dve.h, s_dve.cnt)
                for lt in range(2):
                    dtp = DTDT[:, lt * 32:(lt + 1) * 32].unsqueeze(2).broadcast_to([128, 32, 64])
                    v.wait_ge(s_dve.h, s_dve.cnt)
                    s_dve.inc(v.tensor_tensor(
                        XT_S[:, lt * 2048:(lt + 1) * 2048].rearrange("p (a b) -> p a b", a=32),
                        XT_S[:, lt * 2048:(lt + 1) * 2048].rearrange("p (a b) -> p a b", a=32),
                        dtp, ALU.mult))
            k.v(v_xdtd)

            for hg in range(4):
                def t_sc(t, hg=hg):
                    t.wait_ge(s_dve.h, s_dve.cnt)
                    for hh in range(8):
                        h = hg * 8 + hh
                        t.matmul(ps[4][:, hh * 64:(hh + 1) * 64], BTOK[:, 0:128],
                                 XT_S[:, 0 * 2048 + h * 64:0 * 2048 + (h + 1) * 64], start=True, stop=False)
                        s_pe.inc(t.matmul(ps[4][:, hh * 64:(hh + 1) * 64], BTOK[:, 128:256],
                                          XT_S[:, 1 * 2048 + h * 64:1 * 2048 + (h + 1) * 64],
                                          start=False, stop=True))
                k.t(t_sc)
                pe_sc = s_pe.cnt

                def v_r(v, hg=hg, pe_sc=pe_sc):
                    v.wait_ge(s_pe.h, pe_sc)
                    v.wait_ge(s_act.h, s_act.cnt)
                    ecp = ECEX[:, hg * 8:(hg + 1) * 8].unsqueeze(2).broadcast_to([128, 8, 64])
                    rsl = R[:, hg * 512:(hg + 1) * 512]
                    s_dve.inc(v.tensor_tensor(rsl.rearrange("p (a b) -> p a b", a=8),
                                              rsl.rearrange("p (a b) -> p a b", a=8), ecp, ALU.mult))
                    v.wait_ge(s_dve.h, s_dve.cnt)
                    s_dve.inc(v.tensor_tensor(rsl, rsl, ps[4][:, 0:512], ALU.add))
                k.v(v_r)

            def v_rb(v):
                v.wait_ge(s_dve.h, s_dve.cnt)
                s_dve.inc(v.tensor_copy(RB, R[:]))
            k.v(v_rb)

            g1base = s_dve.cnt

            def v_gate(v):
                v.wait_ge(s_pe.h, s_pe.cnt)
                v.wait_ge(s_act.h, s_act.cnt)
                for lt in range(2):
                    ysl = YT_S[:, lt * 2048:(lt + 1) * 2048]
                    v.wait_ge(s_dve.h, s_dve.cnt)
                    s_dve.inc(v.tensor_tensor(ysl, ysl, ZS_S[:, lt * 2048:(lt + 1) * 2048], ALU.mult))
            k.v(v_gate)
            g1 = s_dve.cnt

            def s_var(s, c2=c2):
                s.wait_ge(s_dve.h, g1)
                for lt in range(2):
                    s_act.inc(s.activation(ZS_S[:, lt * 2048:(lt + 1) * 2048],
                                           YT_S[:, lt * 2048:(lt + 1) * 2048], AF.Square,
                                           accum_out=VARP[:, c2 * 2 + lt:c2 * 2 + lt + 1]))
            k.s(s_var)
            var_ok = s_act.cnt

            pe_yt = s_pe.cnt
            dve_yt = s_dve.cnt

            def t_ytr(t, var_ok=var_ok):
                t.wait_ge(s_act.h, var_ok)
                n = 0
                for fc in range(16):
                    for lt in range(2):
                        bank = psta if n % 2 == 0 else pstb
                        if n >= 2:
                            t.wait_ge(s_dve.h, dve_yt + n - 1)
                        s_pe.inc(t.transpose(bank[:], YT_S[:, lt * 2048 + fc * 128:lt * 2048 + (fc + 1) * 128], IDB[:]))
                        n += 1
            k.t(t_ytr)

            def v_ytr(v):
                n = 0
                for fc in range(16):
                    for lt in range(2):
                        bank = psta if n % 2 == 0 else pstb
                        v.wait_ge(s_pe.h, pe_yt + n + 1)
                        s_dve.inc(v.tensor_scalar(YNFC[:, fc * 256 + lt * 128:fc * 256 + (lt + 1) * 128],
                                                  bank[:], NWC[:, fc:fc + 1], None, ALU.mult))
                        n += 1
            k.v(v_ytr)
            ynfc_ok = s_dve.cnt

            def g_yw(g, c2=c2, ynfc_ok=ynfc_ok):
                g.wait_ge(s_dve.h, ynfc_ok)

                def wr(my, g):
                    for fc in range(16):
                        s_sh.inc(g.dma_start(
                            out=SH_Y[my, fc * 128:(fc + 1) * 128, c2 * 256:(c2 + 1) * 256],
                            in_=YNFC[:, fc * 256:(fc + 1) * 256]))
                branch_pair(g, gstate["me"], wr)
            k.g(g_yw)

        def v_vp(v):
            v.wait_ge(s_act.h, s_act.cnt)
            s_dve.inc(v.engine_nop())
        k.v(v_vp)
        vp_ok = s_dve.cnt
        emit_send([(lambda slot: SH_C[slot, :, 144:160], VARP[:, 0:16])], [(s_dve, vp_ok)])
        sig_yf = gstate["sig"]

        def g_yr(g):
            emit_poll(g, sig_yf)
            def rd1(my, g):
                s_sh.inc(g.dma_start(out=BCT[:], in_=SH_C[my, :, 144:160]))
            branch_pair(g, gstate["pr"], rd1)

            def rd2(my, g):
                for c in range(32):
                    s_sh.inc(g.dma_start(
                        out=A1[:, c * 1024:(c + 1) * 1024],
                        in_=SH_Y[c // 16, (c % 16) * 128:(c % 16 + 1) * 128,
                                 my * 1024:(my + 1) * 1024]))
            branch_pair(g, gstate["me"], rd2)
        k.g(g_yr)
        yr_in = s_sh.cnt

        def v_var2(v):
            v.wait_ge(s_sh.h, yr_in)
            s_dve.inc(v.tensor_tensor(VARP[:, 0:16], VARP[:, 0:16], BCT[:], ALU.add))
        k.v(v_var2)
        var2 = s_dve.cnt

        def g_vflat(g):
            g.wait_ge(s_dve.h, var2)
            def vf(my, g):
                s_fl.inc(g.dma_start(
                    out=ROW[0:1, 0:1024].rearrange("o (p c) -> o p c", p=128),
                    in_=VARP[:, my * 8:(my + 1) * 8].unsqueeze(1)))
            branch_pair(g, gstate["me"], vf)
        k.g(g_vflat)
        vf_in = s_fl.cnt

        def s_rstd(s):
            s.wait_ge(s_fl.h, vf_in)
            s_act.inc(s.activation(ROW[0:1, 1024:2048], ROW[0:1, 0:1024], AF.Sqrt,
                                   bias=CONSTS[0:1, 0:1], scale=1.0 / 4096.0))
        k.s(s_rstd)
        rstd1 = s_act.cnt

        def v_rstd(v):
            v.wait_ge(s_act.h, rstd1)
            s_dve.inc(v.reciprocal(ROW[0:1, 0:1024], ROW[0:1, 1024:2048]))
        k.v(v_rstd)
        rstd2 = s_dve.cnt

        def t_rbc(t):
            t.wait_ge(s_dve.h, rstd2)
            ins = None
            for c in range(8):
                ins = t.matmul(ps[4 + c // 4][:, (c % 4) * 128:(c % 4 + 1) * 128],
                               ONESROW[0:1, :], ROW[0:1, c:c + 1017:8], start=True, stop=True)
            s_pe.inc(ins)
        k.t(t_rbc)
        pe_rbc = s_pe.cnt

        def v_nrm(v):
            v.wait_ge(s_pe.h, pe_rbc)
            for c in range(32):
                for j in range(2):
                    v.wait_ge(s_dve.h, s_dve.cnt)
                    s_dve.inc(v.tensor_tensor(A1[:, c * 1024 + j * 512:c * 1024 + (j + 1) * 512],
                                              A1[:, c * 1024 + j * 512:c * 1024 + (j + 1) * 512],
                                              ps[4 + j][:, 0:512], ALU.mult))
        k.v(v_nrm)
        nrm_ok = s_dve.cnt

        for ocg in range(8):
            def g_wo2(y, ocg=ocg):
                y.wait_ge(s_pe.h, s_pe.cnt)
                if ocg == 0:
                    y.wait_ge(s_cb.h, CAST_AT["wout"])
                for kg in range(4):
                    s_wa[0].inc(y.dma_start(
                        out=W1[:, kg * 2048:(kg + 1) * 2048],
                        in_=scr_wout[:, ocg * 8192 + kg * 2048:ocg * 8192 + (kg + 1) * 2048]))
            k.y(g_wo2)
            wo2_in = s_wa[0].cnt

            for tt in range(2):
                def t_wo2(t, ocg=ocg, tt=tt, wo2_in=wo2_in):
                    t.wait_ge(s_wa[0].h, wo2_in)
                    if ocg == 0 and tt == 0:
                        t.wait_ge(s_dve.h, nrm_ok)
                    else:
                        t.wait_ge(s_dve.h, s_dve.cnt)
                    for ocl in range(2):
                        acc_mm(t, ps[ocl][:, 0:512],
                               [(W1[:, (c // 8) * 2048 + (c % 8) * 256 + ocl * 128:
                                     (c // 8) * 2048 + (c % 8) * 256 + (ocl + 1) * 128],
                                 A1[:, c * 1024 + tt * 512:c * 1024 + (tt + 1) * 512])
                                for c in range(32)], s_pe)
                k.t(t_wo2)
                pe_wo2 = s_pe.cnt

                def v_wo2(v, ocg=ocg, tt=tt, pe_wo2=pe_wo2):
                    v.wait_ge(s_pe.h, pe_wo2)
                    for ocl in range(2):
                        c = ocg * 2 + ocl
                        s_dve.inc(v.tensor_tensor(
                            XR[:, c * NT + tt * 512:c * NT + (tt + 1) * 512],
                            XR[:, c * NT + tt * 512:c * NT + (tt + 1) * 512],
                            ps[ocl][:, 0:512], ALU.add))
                k.v(v_wo2)

    CAST_AT = {}

    def emit_cast_front():
        def g_cast(g):
            wgu_pieces = []
            for j in range(8):
                lo, hi = j * 16384, (j + 1) * 16384
                wgu_pieces.append(("wg%d" % j, wg_d[:, lo:hi], scr_wg[:, lo:hi]))
                wgu_pieces.append(("wu%d" % j, wu_d[:, lo:hi], scr_wu[:, lo:hi]))
            for name, src, dst in tuple(wgu_pieces) + (
                                   ("wd", wd_d, scr_wd), ("wqkv", wqkv_d, scr_wqkv),
                                   ("wo", wo_d, scr_wo), ("aggw", aggw_d, scr_aggw),
                                   ("d1h", d1h_d, scr_d1h), ("bcw", bcw_d, scr_bcw)):
                g.wait_ge(s_cb.h, s_cb.cnt)
                s_cb.inc(g.dma_start(out=dst, in_=src))
                CAST_AT[name] = s_cb.cnt
        k.g(g_cast)

    _gap_pieces = []
    for name, src, dst, rows in (("d2h", d2h_d, scr_d2h, 4096), ("gw", gw_d, scr_gw, 128),
                                 ("winx", winx_d, scr_winx, 128), ("winz", winz_d, scr_winz, 128),
                                 ("wout", wout_d, scr_wout, 128)):
        if rows == 128:
            w = src.shape[1]
            for j in range(0, w, 16384):
                ww = min(16384, w - j)
                _gap_pieces.append((name, src[:, j:j + ww], dst[:, j:j + ww]))
        else:
            for r in range(0, rows, 1024):
                _gap_pieces.append((name, src[r:r + 1024, :], dst[r:r + 1024, :]))
    _gap_state = {"i": 0}

    def cast_gap(g, n=2):
        for _ in range(n):
            i = _gap_state["i"]
            if i >= len(_gap_pieces):
                return
            name, srcap, dstap = _gap_pieces[i]
            g.wait_ge(s_cb.h, s_cb.cnt)
            s_cb.inc(g.dma_start(out=dstap, in_=srcap))
            CAST_AT[name] = s_cb.cnt
            _gap_state["i"] = i + 1

    def cast_gap_drain():
        def g_dr(g):
            cast_gap(g, len(_gap_pieces))
        k.g(g_dr)

    nc._dbg = dict(XR=XR, A1=A1, A2=A2, A3=A3, TH=TH, THB=THB, BCT=BCT, TN=TN, HC=HC,
                   TC=TC, ROW=ROW, XNL=XNL, GTL=GTL, VARP=VARP, DTJ=DTJ)

    SUBLAYER[0] = 0
    emit_cast_front()
    emit_l0ffn(0)
    SUBLAYER[0] += 1
    if finish_if(SUBLAYER[0]):
        return nc
    emit_l0attn()
    cast_gap_drain()
    SUBLAYER[0] += 1
    if finish_if(SUBLAYER[0]):
        return nc
    emit_l0ffn(1)
    SUBLAYER[0] += 1
    if finish_if(SUBLAYER[0]):
        return nc
    emit_l1ffn()
    SUBLAYER[0] += 1
    if finish_if(SUBLAYER[0]):
        return nc
    emit_ssd()
    SUBLAYER[0] += 1
    if finish_if(SUBLAYER[0]):
        return nc
    emit_l1ffn()
    SUBLAYER[0] += 1
    if finish_if(SUBLAYER[0]):
        return nc

    def y_fin(y):
        y.wait_ge(s_pe.h, s_pe.cnt)
        y.wait_ge(s_dve.h, s_dve.cnt)
        y.wait_ge(s_act.h, s_act.cnt)
        for c in range(NCH):
            s_st.inc(y.dma_start(out=xout[c * 128:(c + 1) * 128, :], in_=XR[:, c * NT:(c + 1) * NT]))
        y.wait_ge(s_st.h, s_st.cnt)
    k.y(y_fin)

    k.st.close()
    return nc


_NONCE_BASE = [0]


def fused_inmaps(inputs, nonce=None):
    import time
    if nonce is None:
        nonce = int(time.time_ns() // 1_000_000) % (1 << 29)
    x = inputs["L0_attn_x"].astype(np.float32)
    xs = shard_fm(x)
    cat = np.concatenate
    asc = np.ascontiguousarray

    def colify(v, n=16):
        return asc(v.reshape(n, 128).T)

    sc = asc(cat([inputs["L0_ffn_s_gate"].reshape(64, 128).T, inputs["L0_ffn_s_up"].reshape(64, 128).T,
                  inputs["L0_ffn_s_down"].reshape(16, 128).T], axis=1))
    f1c = asc(cat([colify(inputs["L1_ffn_agg_b"]), colify(inputs["L1_ffn_d2_b"]),
                   colify(inputs["L1_ffn_out_gamma"]), colify(inputs["L1_ffn_out_beta"]),
                   colify(inputs["L1_ffn_pn_gamma"]), colify(inputs["L1_ffn_pn_beta"]),
                   colify(inputs["L1_ffn_gate_b"]), colify(inputs["L1_ffn_bc_b"])], axis=1))
    cw = inputs["L1_attn_conv_w"]
    cb = inputs["L1_attn_conv_b"]
    nw = inputs["L1_attn_norm_weight"]

    wgt = asc(inputs["L0_ffn_w_gate"].reshape(16, 128, 64, 128)
              .transpose(1, 2, 0, 3).reshape(128, 131072))
    wut = asc(inputs["L0_ffn_w_up"].reshape(16, 128, 64, 128)
              .transpose(1, 2, 0, 3).reshape(128, 131072))
    wdt = asc(inputs["L0_ffn_w_down"].reshape(64, 128, 4, 512)
              .transpose(1, 2, 0, 3).reshape(128, 131072))
    wot = asc(inputs["L0_attn_Wo"].reshape(2, 8, 128, 4, 512)
              .transpose(2, 3, 0, 1, 4).reshape(128, 32768))
    gwt = asc(inputs["L1_ffn_gate_w"].reshape(16, 128, 16, 128)
              .transpose(1, 2, 0, 3).reshape(128, 32768))
    woutt = asc(inputs["L1_attn_W_out"].reshape(4, 8, 128, 8, 256)
                .transpose(2, 3, 0, 1, 4).reshape(128, 65536))
    maps = []
    for c in range(8):
        b, me = c // 2, c % 2
        meta = np.zeros((1, 64), np.int32)
        meta[0, 0] = me
        meta[0, 1] = 1 - me
        meta[0, 2] = nonce
        for j in range(56):
            meta[0, 8 + j] = nonce + j
        hsl = slice(me * 8, (me + 1) * 8)
        wq = inputs["L0_attn_Wq"][hsl].transpose(1, 0, 2).reshape(D, 1024)
        wk = inputs["L0_attn_Wk"][hsl].transpose(1, 0, 2).reshape(D, 1024)
        wv = inputs["L0_attn_Wv"][hsl].transpose(1, 0, 2).reshape(D, 1024)
        wqkv = asc(np.stack([wq, wk, wv]).reshape(3, 16, 128, 8, 128)
                   .transpose(2, 3, 0, 1, 4).reshape(128, 49152))
        attb = np.zeros((128, 32), np.float32)
        attb[:, 0:8] = inputs["L0_attn_bq"][hsl].T
        attb[:, 8:16] = inputs["L0_attn_bk"][hsl].T
        attb[:, 16:32] = inputs["L0_attn_bo"].reshape(16, 128).T
        bvr = asc(inputs["L0_attn_bv"][hsl].reshape(1, 1024))
        d1h = asc(inputs["L1_ffn_d1_w"][:, me * 4096:(me + 1) * 4096])
        d2h = asc(inputs["L1_ffn_d2_w"][me * 4096:(me + 1) * 4096, :])
        d1br = asc(inputs["L1_ffn_d1_b"][me * 4096:(me + 1) * 4096].reshape(1, 4096))
        win_f = inputs["L1_attn_W_in"]
        xbc = cat([win_f[:, 4096 + me * 2048:4096 + (me + 1) * 2048],
                   win_f[:, 8192:8448]], axis=1)
        winx = asc(cat([xbc.reshape(16, 128, 18, 128).transpose(1, 2, 0, 3).reshape(128, 36864),
                        win_f[:, 8448 + me * 32:8448 + (me + 1) * 32]
                        .reshape(16, 128, 32).transpose(1, 0, 2).reshape(128, 512)], axis=1))
        winz = asc(win_f[:, me * 2048:(me + 1) * 2048]
                   .reshape(16, 128, 16, 128).transpose(1, 2, 0, 3).reshape(128, 32768))
        cwk = np.zeros((128, 72), np.float32)
        cbt = np.zeros((128, 18), np.float32)
        for kk in range(4):
            cwk[:, kk * 18:kk * 18 + 16] = cw[kk, me * 2048:(me + 1) * 2048].reshape(16, 128).T
            cwk[:, kk * 18 + 16:kk * 18 + 18] = cw[kk, 4096:4352].reshape(2, 128).T
        cbt[:, 0:16] = cb[me * 2048:(me + 1) * 2048].reshape(16, 128).T
        cbt[:, 16:18] = cb[4096:4352].reshape(2, 128).T
        auxc = asc(np.stack([inputs["L1_attn_dt_bias"][me * 32:(me + 1) * 32],
                             inputs["L1_attn_A_log"][me * 32:(me + 1) * 32]], axis=1))
        drow = asc(inputs["L1_attn_D"][me * 32:(me + 1) * 32].reshape(1, 32))
        nwc = asc(nw[me * 2048:(me + 1) * 2048].reshape(16, 128).T)
        nwc2 = np.zeros((128, 32), np.float32)
        nwc2[:, 0:16] = nwc
        maps.append({
            "xres": np.asarray(xs[c]), "meta": meta,
            "wg": wgt, "wu": wut, "wd": wdt,
            "sc": sc, "wqkv": wqkv, "wo": wot, "attb": attb, "bvr": bvr,
            "aggw": inputs["L1_ffn_agg_w"], "d1h": d1h, "d2h": d2h, "d1br": d1br,
            "bcw": inputs["L1_ffn_bc_w"], "gw": gwt, "f1c": f1c,
            "winx": winx, "winz": winz, "wout": woutt, "cwk": cwk, "cb": cbt,
            "auxc": auxc, "drow": drow, "nwc": nwc2,
        })
    return maps
